# version 20
# baseline (speedup 1.0000x reference)
import math

import numpy as np

import concourse.bass as bass
import concourse.mybir as mybir
from concourse.bass_utils import run_bass_kernel_spmd

F32 = mybir.dt.float32
BF16 = mybir.dt.bfloat16
AF = mybir.ActivationFunctionType
ALU = mybir.AluOpType
AXX = mybir.AxisListType.X

H = 32
NOPE = 64
VD = 128
QHD = 128
KVR = 256
D = 4096
S = 2048
B = 4
INTER = 11008
NIC = INTER // 128
ICB_SPLIT = [29, 29, 28]
EPS_LN = 1e-5
EPS_RMS = 1e-6
_m = 0.1 * 1.0 * math.log(128.0) + 1.0
SOFTMAX_SCALE = (QHD ** -0.5) * _m * _m
MARGIN = 55.0
NEG = -1.0e9

NQ = 1024
NSLOT = 8
LJ = [2 * (j + 1) for j in range(NSLOT)]
OFFJ = [j * (j + 1) * 128 for j in range(NSLOT)]
PTOT = OFFJ[-1] + LJ[-1] * 128


def _yarn_cos_sin(seq_len=S, dim=64, base=10000.0, factor=128.0, beta_fast=32.0,
                  beta_slow=1.0, orig_max=8192, mscale=1.0, mscale_all_dim=1.0):
    def corr_dim(r):
        return dim * math.log(orig_max / (r * 2 * math.pi)) / (2 * math.log(base))
    low = max(math.floor(corr_dim(beta_fast)), 0)
    high = min(math.ceil(corr_dim(beta_slow)), dim - 1)
    hi = high + 0.001 if low == high else float(high)
    ramp = np.clip((np.arange(dim // 2, dtype=np.float32) - low) / (hi - low), 0.0, 1.0)
    inv_freq_mask = 1.0 - ramp
    freq_extra = 1.0 / base ** (np.arange(0, dim, 2, dtype=np.float32) / dim)
    freq_inter = freq_extra / factor
    inv_freq = freq_inter * (1.0 - inv_freq_mask) + freq_extra * inv_freq_mask
    t = np.arange(seq_len, dtype=np.float32)
    freqs = np.outer(t, inv_freq)
    emb = np.concatenate([freqs, freqs], axis=-1)

    def gm(s_, m_):
        return 1.0 if s_ <= 1 else 0.1 * m_ * math.log(s_) + 1.0
    ms = gm(factor, mscale) / gm(factor, mscale_all_dim)
    return (np.cos(emb) * ms).astype(np.float32), (np.sin(emb) * ms).astype(np.float32)


def _perm_tiles(p):
    order = []
    for j in range(8):
        order.append(2 * j + p)
        order.append(2 * j + 1 - p)
    return order


def _pack_shared(inp):
    pk = {}
    A = np.ascontiguousarray(np.asarray(inp["L0_attn_wq_a"], np.float32).T).reshape(32, 128, 8, 128)
    pk["wqaP"] = np.ascontiguousarray(A.transpose(1, 2, 0, 3).reshape(128, 8, 4096))
    perm = np.empty(4096, dtype=np.int64)
    for h in range(H):
        b0 = h * 128
        perm[b0:b0 + 32] = b0 + 64 + 2 * np.arange(32)
        perm[b0 + 32:b0 + 64] = b0 + 64 + 2 * np.arange(32) + 1
        perm[b0 + 64:b0 + 128] = b0 + np.arange(64)
    W = np.asarray(inp["L0_attn_wq_b"], np.float32)[perm].reshape(32, 128, 8, 128)
    pk["wqbP"] = np.ascontiguousarray(W.transpose(3, 0, 2, 1).reshape(128, 32, 1024))
    permk = np.concatenate([np.arange(256), 256 + 2 * np.arange(32),
                            256 + 2 * np.arange(32) + 1])
    A = np.asarray(inp["L0_attn_wkv_a"], np.float32)[permk].reshape(320, 32, 128)
    pk["wkvaP"] = np.ascontiguousarray(A.transpose(2, 1, 0).reshape(128, 32 * 320))
    Wb = np.asarray(inp["L0_attn_wkv_b"], np.float32).reshape(H, NOPE + VD, KVR)
    K = Wb[:, 0:NOPE, :].reshape(H, NOPE, 2, 128)
    pk["wkvbkP"] = np.ascontiguousarray(K.transpose(3, 0, 2, 1).reshape(128, 32, 128))
    V = Wb[:, NOPE:, :].reshape(8, 4, VD, 2, 128)
    pk["wkvbvP"] = np.ascontiguousarray(V.transpose(4, 0, 3, 1, 2).reshape(128, 8, 1024))
    W = np.ascontiguousarray(np.asarray(inp["L0_attn_wo"], np.float32).T).reshape(32, 128, 32, 128)
    pk["woP"] = np.ascontiguousarray(W.transpose(1, 2, 0, 3).reshape(128, 32, 4096))
    G = np.ascontiguousarray(np.asarray(inp["L0_ffn_gate_w"], np.float32).T).reshape(32, 128, NIC, 128)
    pk["gateP"] = np.ascontiguousarray(G.transpose(1, 2, 0, 3).reshape(128, NIC, 4096))
    U = np.ascontiguousarray(np.asarray(inp["L0_ffn_up_w"], np.float32).T).reshape(32, 128, NIC, 128)
    pk["upP"] = np.ascontiguousarray(U.transpose(1, 2, 0, 3).reshape(128, NIC, 4096))
    Dw = np.ascontiguousarray(np.asarray(inp["L0_ffn_down_w"], np.float32).T).reshape(NIC, 128, 32, 128)
    pk["downP"] = np.ascontiguousarray(Dw.transpose(1, 2, 0, 3).reshape(128, 32, INTER))
    pk["qgain"] = np.ascontiguousarray(np.asarray(inp["L0_attn_q_a_ln_w"], np.float32).reshape(8, 128).T)
    pk["kvgain"] = np.ascontiguousarray(np.asarray(inp["L0_attn_kv_a_ln_w"], np.float32).reshape(2, 128).T)
    sc = np.array([float(inp["L0_ffn_gate_scale"]), float(inp["L0_ffn_up_scale"]),
                   float(inp["L0_ffn_down_scale"])], np.float32).reshape(1, 3)
    pk["scal"] = np.ascontiguousarray(np.broadcast_to(sc, (128, 3)).astype(np.float32))
    pk["ident"] = np.eye(128, dtype=np.float32)
    return pk


def _core_pack(inp, cos, sin, b, p):
    d = {}
    order = _perm_tiles(p)
    x = np.asarray(inp["L0_attn_x"][b], np.float32)
    xt = x.reshape(16, 128, 4096)[order].reshape(2048, 4096)
    xt = np.ascontiguousarray(xt.T).reshape(32, 128, S)
    d["xP"] = np.ascontiguousarray(xt.transpose(1, 0, 2))
    posk = np.concatenate([np.arange(t * 128, (t + 1) * 128) for t in order])
    ck = np.ascontiguousarray(cos[posk].T)
    sk = np.ascontiguousarray(sin[posk].T).copy()
    sk[0:32] = -sk[0:32]
    d["cosk"], d["sink"] = ck, sk
    posq = np.concatenate([(2 * j + p) * 128 + np.arange(128) for j in range(NSLOT)])
    cq = np.ascontiguousarray(cos[posq].T)
    sq = np.ascontiguousarray(sin[posq].T).copy()
    sq[0:32] = -sq[0:32]
    d["cosq"], d["sinq"] = cq, sq
    i_ = np.arange(128)[:, None]
    c_ = np.arange(128)[None, :]
    diag = np.where(c_ <= i_, np.float32(0.0), np.float32(NEG))
    oth = np.full((128, 128), 0.0 if p == 1 else NEG, np.float32)
    d["masks"] = np.ascontiguousarray(np.concatenate([diag, oth], axis=1))
    return d


ENGS = ("sp", "pe", "act", "dve", "pool")


class _Op:
    __slots__ = ("eng", "fn", "idx", "need_inc", "deps", "dma", "dsem", "duse")

    def __init__(self, eng, fn, dma, idx):
        self.eng, self.fn, self.dma, self.idx = eng, fn, dma, idx
        self.need_inc = False
        self.deps = []
        self.dsem = -1
        self.duse = 0


NDMASEM = 24


class Bld:
    def __init__(self):
        self.eng_ops = {e: [] for e in ENGS}
        self.last_w = {}
        self.readers = {}
        self.key_last = {}
        self.dma_n = 0
        self.dma_uses = [0] * NDMASEM
        self.dma_last_op = [None] * NDMASEM

    def op(self, eng, fn, reads=(), writes=(), dma=False, after=()):
        o = _Op(eng, fn, eng if dma else None, len(self.eng_ops[eng]))
        if dma:
            o.need_inc = True
            o.dsem = self.dma_n % NDMASEM
            self.dma_n += 1
            self.dma_uses[o.dsem] += 1
            o.duse = self.dma_uses[o.dsem]
        deps, seen = [], set()

        def add(pr):
            if pr is None or pr is o or id(pr) in seen:
                return
            if pr.eng == "pe" and eng == "pe" and pr.dma is None and not dma:
                return
            seen.add(id(pr))
            deps.append(pr)
            pr.need_inc = True

        for pr_ in after:
            add(pr_)
        for t in reads:
            add(self.last_w.get(t))
        for t in writes:
            add(self.last_w.get(t))
            for r in self.readers.get(t, ()):
                add(r)
        for t in writes:
            self.last_w[t] = o
            self.readers[t] = []
        for t in reads:
            self.readers.setdefault(t, []).append(o)
        if dma and o.duse > 1:
            prev = self.dma_last_op[o.dsem]
            if prev is not None and id(prev) not in seen:
                seen.add(id(prev))
                deps.append(prev)
        if dma:
            self.dma_last_op[o.dsem] = o
        o.deps = deps
        self.eng_ops[eng].append(o)
        self.key_last[(eng, bool(dma))] = o
        return o

    def barrier(self):
        lasts = [o for o in self.key_last.values() if o.dma is None]
        lasts += [o for o in self.dma_last_op if o is not None]
        for pr in lasts:
            pr.need_inc = True
        for e in ENGS:
            o = _Op(e, None, None, len(self.eng_ops[e]))
            o.deps = lasts[:]
            self.eng_ops[e].append(o)

    def materialize(self, sems, dsems):
        cnt, val = {}, {}
        for ops in self.eng_ops.values():
            for o in ops:
                if o.dma is not None:
                    val[id(o)] = ("D", o.dsem, o.duse * 16)
                else:
                    key = o.eng
                    if o.need_inc:
                        cnt[key] = cnt.get(key, 0) + 1
                    val[id(o)] = ("E", key, cnt.get(key, 0))
        self.final_counts = {("E", k): v for k, v in cnt.items()}
        for i in range(NDMASEM):
            if self.dma_uses[i]:
                self.final_counts[("D", i)] = self.dma_uses[i] * 16

        def emit(engine, ops):
            water = {}
            for o in ops:
                for pr in o.deps:
                    kind, key, v = val[id(pr)]
                    sem = dsems[key] if kind == "D" else sems[key]
                    wk = (kind, key)
                    if water.get(wk, -1) >= v:
                        continue
                    water[wk] = v
                    engine.wait_ge(sem, v)
                if o.fn is None:
                    continue
                ins = o.fn(engine)
                if o.dma is not None:
                    ins.then_inc(dsems[o.dsem], 16)
                elif o.need_inc:
                    ins.then_inc(sems[o.eng], 1)
        return emit


_CACHE = {}
DEBUG = 0
PHASES = 99


def _build(debug=0, phases=99):
    ck_ = (debug, phases)
    if ck_ in _CACHE:
        return _CACHE[ck_]
    nc = bass.Bass("TRN2", target_bir_lowering=False, debug=False)

    ein = lambda n, s_, dt=F32: nc.dram_tensor(n, s_, dt, kind="ExternalInput").ap()
    xP = ein("xP", [128, 32, S])
    masks = ein("masks", [128, 256])
    cosq, sinq = ein("cosq", [64, NQ]), ein("sinq", [64, NQ])
    cosk, sink = ein("cosk", [64, S]), ein("sink", [64, S])
    qgain, kvgain = ein("qgain", [128, 8]), ein("kvgain", [128, 2])
    scal = ein("scal", [128, 3])
    identf = ein("ident", [128, 128])
    wqaP = ein("wqaP", [128, 8, 4096])
    wqbP = ein("wqbP", [128, 32, 1024])
    wkvaP = ein("wkvaP", [128, 32 * 320])
    wkvbkP = ein("wkvbkP", [128, 32, 128])
    wkvbvP = ein("wkvbvP", [128, 8, 1024])
    woP = ein("woP", [128, 32, 4096])
    gateP = ein("gateP", [128, NIC, 4096])
    upP = ein("upP", [128, NIC, 4096])
    downP = ein("downP", [128, 32, INTER])

    outP = nc.dram_tensor("outP", [128, 32, NQ], F32, kind="ExternalOutput").ap()
    attn_d = nc.dram_tensor("attn_d", [128, 32, NQ], BF16, kind="Internal").ap()
    o_d = nc.dram_tensor("o_d", [128, 32, NQ], F32, kind="Internal").ap()

    dbg = {}
    if debug:
        dd = lambda n, s_, dt: nc.dram_tensor(n, s_, dt, kind="ExternalOutput").ap()
        dbg["xn"] = dd("D_xn", [128, 32 * 1024], BF16)
        dbg["ckv"] = dd("D_ckv", [128, 2 * 2048], BF16)
        dbg["kpe"] = dd("D_kpe", [64, 2048], BF16)
        dbg["cqn"] = dd("D_cqn", [128, 8 * 1024], BF16)
        dbg["qh"] = dd("D_qh", [128, 1024], BF16)
        dbg["kf"] = dd("D_kf", [128, 2048], BF16)
        dbg["pb"] = dd("D_pb", [128, PTOT], BF16)
        dbg["at"] = dd("D_at", [128, 32, NQ], BF16)
        dbg["od"] = dd("D_od", [128, 32, NQ], F32)
        dbg["v4"] = dd("D_v4", [128, 8192], BF16)

    XN = nc.alloc_sbuf_tensor("XN", [128, 32 * 1024], BF16).ap()
    CQN = nc.alloc_sbuf_tensor("CQN", [128, 8 * 1024], BF16).ap()
    CKV = nc.alloc_sbuf_tensor("CKV", [128, 2 * 2048], BF16).ap()
    KPE = nc.alloc_sbuf_tensor("KPE", [64, 2048], BF16).ap()
    GQ = nc.alloc_sbuf_tensor("GQ", [128, 8], F32).ap()
    GKV = nc.alloc_sbuf_tensor("GKV", [128, 2], F32).ap()
    SCAL = nc.alloc_sbuf_tensor("SCAL", [128, 3], F32).ap()
    DSEFF = nc.alloc_sbuf_tensor("DSEFF", [128, 1], F32).ap()
    IDENT = nc.alloc_sbuf_tensor("IDENT", [128, 128], BF16).ap()
    ONESB = nc.alloc_sbuf_tensor("ONESB", [128, 1], BF16).ap()
    ONES128 = nc.alloc_sbuf_tensor("ONES128", [128, 128], BF16).ap()
    ONESF = nc.alloc_sbuf_tensor("ONESF", [1, 128], F32).ap()
    SROW = nc.alloc_sbuf_tensor("SROW", [1, 1536], F32).ap()
    BIAS8 = nc.alloc_sbuf_tensor("BIAS8", [128, 16], F32).ap()
    T8 = nc.alloc_sbuf_tensor("T8", [128, 16], F32).ap()
    LROW = nc.alloc_sbuf_tensor("LROW", [128, 48], F32).ap()
    LINV = nc.alloc_sbuf_tensor("LINV", [128, 16], F32).ap()
    K2MAX = nc.alloc_sbuf_tensor("K2MAX", [128, 2], F32).ap()
    BND = nc.alloc_sbuf_tensor("BND", [128, 2], F32).ap()
    F2G = nc.alloc_sbuf_tensor("F2G", [128, 8], F32).ap()
    KB2 = nc.alloc_sbuf_tensor("KB2", [128, 2], F32).ap()
    CT = nc.alloc_sbuf_tensor("CT", [128, 4], F32).ap()

    ARENA = nc.alloc_sbuf_tensor("ARENA", [128, (106 * 1024) // 4], F32).ap()

    def carve(off_b, part, nfree, dtype):
        assert off_b % 4 == 0
        if dtype == F32:
            v = ARENA[:, off_b // 4: off_b // 4 + nfree]
        else:
            assert nfree % 2 == 0
            v = ARENA[:, off_b // 4: off_b // 4 + nfree // 2].bitcast(BF16)
        if part != 128:
            v = v[0:part, :]
        return v

    PSA = nc.alloc_psum_tensor("PSA", [128, 2048], F32).ap()
    PSB = nc.alloc_psum_tensor("PSB", [128, 2048], F32).ap()
    bA = lambda i: f"A{i}"
    bB = lambda i: f"B{i}"

    bld = Bld()

    def mm(dst, banks, lhsT, rhs, reads, start, stop, after=()):
        return bld.op("pe", lambda e, d=dst, l=lhsT, r=rhs, s1=start, s2=stop:
                      e.matmul(d, l, r, start=s1, stop=s2),
                      reads=tuple(reads), writes=tuple(banks), after=after)

    ident_stage = carve(0, 128, 128, F32)
    bld.op("sp", lambda e: e.dma_start(out=GQ, in_=qgain), writes=("gq",), dma=True)
    bld.op("sp", lambda e: e.dma_start(out=GKV, in_=kvgain), writes=("gkv",), dma=True)
    bld.op("sp", lambda e: e.dma_start(out=SCAL, in_=scal), writes=("scal",), dma=True)
    bld.op("sp", lambda e: e.dma_start(out=ident_stage, in_=identf), writes=("idst",), dma=True)
    bld.op("dve", lambda e: e.tensor_copy(IDENT, ident_stage), reads=("idst",), writes=("ident",))
    bld.op("dve", lambda e: e.tensor_mul(DSEFF, SCAL[:, 1:2], SCAL[:, 2:3]),
           reads=("scal",), writes=("dseff",))
    bld.op("pool", lambda e: e.memset(ONESB, 1.0), writes=("onesb",))
    bld.op("pool", lambda e: e.memset(ONES128, 1.0), writes=("ones128",))
    bld.op("pool", lambda e: e.memset(ONESF, 1.0), writes=("onesf",))
    bld.op("pool", lambda e: e.memset(CT[:, 0:1], EPS_LN), writes=("constv",))
    bld.op("pool", lambda e: e.memset(CT[:, 1:2], EPS_RMS), writes=("constv",))
    bld.op("pool", lambda e: e.memset(CT[:, 2:3], 1e-20), writes=("constv",))
    bld.barrier()

    xstage = carve(0, 128, 8192, F32)
    wkva_bf = carve(32 * 1024, 128, 32 * 320, BF16)
    wkva_st = carve(52 * 1024, 128, 2560, F32)
    xbf = [carve(62 * 1024, 128, 1024, BF16), carve(62 * 1024 + 2048, 128, 1024, BF16)]
    sqr = [carve(66 * 1024, 128, 1024, BF16), carve(66 * 1024 + 2048, 128, 1024, BF16)]
    ckt = [[carve(70 * 1024 + (2 * i + k) * 1024, 64, 256, F32) for k in range(2)]
           for i in range(2)]
    rt1 = carve(74 * 1024, 64, 256, F32)
    rt2 = carve(74 * 1024 + 1024, 64, 256, F32)
    tsc = carve(76 * 1024, 128, 1024, F32)
    ckf = carve(80 * 1024, 128, 512, F32)
    rbs = carve(82 * 1024, 128, 256, F32)
    mrbs = carve(82 * 1024 + 1024, 128, 256, F32)
    sq2 = carve(84 * 1024, 128, 512, BF16)
    sxrow = carve(86 * 1024, 1, 2048, F32)

    for pc in range(4):
        bld.op("sp", lambda e, pc=pc: e.dma_start(
            out=wkva_st, in_=wkvaP[:, pc * 2560:(pc + 1) * 2560]),
            reads=("wkvast",), writes=("wkvast",), dma=True)
        bld.op("dve", lambda e, pc=pc: e.tensor_copy(
            wkva_bf[:, pc * 2560:(pc + 1) * 2560], wkva_st),
            reads=("wkvast",), writes=("wkva",))

    xs3 = xstage.rearrange("p (d t) -> p d t", t=256)
    XN3 = XN.rearrange("p (d t) -> p d t", t=1024)
    r_a = SROW[:, 0:256]
    r_b = SROW[:, 256:512]
    r_m = SROW[:, 512:768]
    r_v = SROW[:, 768:1024]
    r_r = SROW[:, 1024:1280]
    r_mr = SROW[:, 1280:1536]

    for c in range(8):
        cb = c % 2
        for half in range(2):
            htag = "xsA" if half == 0 else "xsB"
            bld.op("sp", lambda e, c=c, half=half: e.dma_start(
                out=xs3[:, half * 16:(half + 1) * 16, :],
                in_=xP[:, half * 16:(half + 1) * 16, c * 256:(c + 1) * 256]),
                reads=(htag,), writes=(htag,), dma=True)
        bld.op("sp", lambda e, c=c, cb=cb: e.dma_start(
            out=ckt[cb][0], in_=cosk[:, c * 256:(c + 1) * 256]),
            reads=(f"ckt{cb}",), writes=(f"ckt{cb}",), dma=True)
        bld.op("sp", lambda e, c=c, cb=cb: e.dma_start(
            out=ckt[cb][1], in_=sink[:, c * 256:(c + 1) * 256]),
            reads=(f"ckt{cb}",), writes=(f"ckt{cb}",), dma=True)
        for g in range(8):
            gb = g % 2
            xg = xstage[:, g * 1024:(g + 1) * 1024]
            htag = "xsA" if g < 4 else "xsB"
            bld.op("act", lambda e, xg=xg, gb=gb: e.copy(xbf[gb], xg),
                   reads=(htag,), writes=(f"xbf{gb}",))
            bld.op("act", lambda e, gb=gb: e.activation(sqr[gb], xbf[gb], AF.Square),
                   reads=(f"xbf{gb}",), writes=(f"sq{gb}",))
            for hf in range(2):
                mm(PSB[0:1, hf * 512:(hf + 1) * 512], (bB(hf),),
                   ONESB, xbf[gb][:, hf * 512:(hf + 1) * 512],
                   ("onesb", f"xbf{gb}"), g == 0, g == 7)
                mm(PSB[0:1, 1024 + hf * 512:1024 + (hf + 1) * 512], (bB(2 + hf),),
                   ONESB, sqr[gb][:, hf * 512:(hf + 1) * 512],
                   ("onesb", f"sq{gb}"), g == 0, g == 7)
        bld.op("act", lambda e: e.copy(sxrow[:, 0:1024], PSB[0:1, 0:1024]),
               reads=(bB(0), bB(1)), writes=("sxrow",))
        bld.op("act", lambda e: e.copy(sxrow[:, 1024:2048], PSB[0:1, 1024:2048]),
               reads=(bB(2), bB(3)), writes=("sxrow",))
        sx = sxrow[:, 0:1024]
        sx2 = sxrow[:, 1024:2048]
        bld.op("dve", lambda e: e.tensor_add(r_a, sx[:, 0:256], sx[:, 256:512]),
               reads=("sxrow",), writes=("ra",))
        bld.op("dve", lambda e: e.tensor_add(r_b, sx[:, 512:768], sx[:, 768:1024]),
               reads=("sxrow",), writes=("rb_",))
        bld.op("dve", lambda e: e.tensor_add(r_m, r_a, r_b), reads=("ra", "rb_"), writes=("rm",))
        bld.op("dve", lambda e: e.tensor_scalar_mul(r_m, r_m, 1.0 / D), reads=("rm",), writes=("rm",))
        bld.op("dve", lambda e: e.tensor_add(r_a, sx2[:, 0:256], sx2[:, 256:512]),
               reads=("sxrow",), writes=("ra",))
        bld.op("dve", lambda e: e.tensor_add(r_b, sx2[:, 512:768], sx2[:, 768:1024]),
               reads=("sxrow",), writes=("rb_",))
        bld.op("dve", lambda e: e.tensor_add(r_v, r_a, r_b), reads=("ra", "rb_"), writes=("rv",))
        bld.op("dve", lambda e: e.tensor_scalar_mul(r_v, r_v, 1.0 / D), reads=("rv",), writes=("rv",))
        bld.op("dve", lambda e: e.tensor_mul(r_a, r_m, r_m), reads=("rm",), writes=("ra",))
        bld.op("dve", lambda e: e.tensor_sub(r_v, r_v, r_a), reads=("rv", "ra"), writes=("rv",))
        bld.op("act", lambda e: e.activation(r_v, r_v, AF.Ln, bias=CT[0:1, 0:1]),
               reads=("rv",), writes=("rv",))
        bld.op("act", lambda e: e.activation(r_r, r_v, AF.Exp, scale=-0.5),
               reads=("rv",), writes=("rr",))
        bld.op("dve", lambda e: e.tensor_mul(r_mr, r_m, r_r), reads=("rm", "rr"), writes=("rmr",))
        mm(PSA[:, 0:256], (bA(0),), ONESF, r_r, ("onesf", "rr"), True, True)
        mm(PSA[:, 512:768], (bA(1),), ONESF, r_mr, ("onesf", "rmr"), True, True)
        bld.op("act", lambda e: e.copy(rbs, PSA[:, 0:256]), reads=(bA(0),), writes=("rbs",))
        bld.op("act", lambda e: e.copy(mrbs, PSA[:, 512:768]), reads=(bA(1),), writes=("mrbs",))
        for g in range(8):
            gb = g % 2
            xg3 = xs3[:, 4 * g:4 * (g + 1), :]
            xnr3 = xbf[gb].rearrange("p (d t) -> p d t", t=256)
            t3 = tsc.rearrange("p (d t) -> p d t", t=256)
            rb_b = rbs.unsqueeze(1).broadcast_to((128, 4, 256))
            mrb_b = mrbs.unsqueeze(1).broadcast_to((128, 4, 256))
            htag = "xsA" if g < 4 else "xsB"
            bld.op("pool", lambda e, xg3=xg3, rb_b=rb_b, t3=t3:
                   e.tensor_tensor(t3, xg3, rb_b, ALU.mult),
                   reads=(htag, "rbs"), writes=("tsc",))
            bld.op("dve", lambda e, xnr3=xnr3, mrb_b=mrb_b, t3=t3:
                   e.tensor_sub(xnr3, t3, mrb_b),
                   reads=("tsc", "mrbs"), writes=(f"xbf{gb}",))
            bld.op("dve", lambda e, g=g, c=c, xnr3=xnr3:
                   e.tensor_copy(XN3[:, 4 * g:4 * (g + 1), c * 128:(c + 1) * 128],
                                 xnr3[:, :, 0:128]),
                   reads=(f"xbf{gb}",), writes=("xn",))
            for dd_ in range(4):
                dc = 4 * g + dd_
                rhs = xnr3[:, dd_, :]
                mm(PSA[:, 1024:1280], (bA(2),),
                   wkva_bf[:, dc * 320:dc * 320 + 128], rhs,
                   ("wkva", f"xbf{gb}"), dc == 0, dc == 31)
                mm(PSA[:, 1536:1792], (bA(3),),
                   wkva_bf[:, dc * 320 + 128:dc * 320 + 256], rhs,
                   ("wkva", f"xbf{gb}"), dc == 0, dc == 31)
                mm(PSB[0:64, 512:768], (bB(1),),
                   wkva_bf[:, dc * 320 + 256:dc * 320 + 320], rhs,
                   ("wkva", f"xbf{gb}"), dc == 0, dc == 31)
        bld.op("act", lambda e: e.activation(sq2[:, 0:256], PSA[:, 1024:1280], AF.Square),
               reads=(bA(2),), writes=("sq2",))
        bld.op("act", lambda e: e.activation(sq2[:, 256:512], PSA[:, 1536:1792], AF.Square),
               reads=(bA(3),), writes=("sq2",))
        mm(PSB[0:1, 1024:1280], (bB(2),), ONESB, sq2[:, 0:256], ("onesb", "sq2"), True, False)
        mm(PSB[0:1, 1024:1280], (bB(2),), ONESB, sq2[:, 256:512], ("onesb", "sq2"), False, True)
        r_k = SROW[:, 0:256]
        bld.op("act", lambda e: e.activation(r_k, PSB[0:1, 1024:1280], AF.Ln,
                                             bias=CT[0:1, 1:2], scale=1.0 / KVR),
               reads=(bB(2), "ra"), writes=("ra",))
        bld.op("act", lambda e: e.activation(r_k, r_k, AF.Exp, scale=-0.5),
               reads=("ra",), writes=("ra",))
        mm(PSB[:, 1536:1792], (bB(3),), ONESF, r_k, ("onesf", "ra"), True, True)
        for l in range(2):
            src = PSA[:, 1024:1280] if l == 0 else PSA[:, 1536:1792]
            btag = bA(2) if l == 0 else bA(3)
            bld.op("act", lambda e, src=src: e.copy(ckf[:, 0:256], src),
                   reads=(btag,), writes=("ckf",))
            bld.op("dve", lambda e: e.tensor_tensor(ckf[:, 256:512], ckf[:, 0:256],
                                                    PSB[:, 1536:1792], ALU.mult),
                   reads=("ckf", bB(3)), writes=("ckf2",))
            bld.op("dve", lambda e, l=l, c=c: e.tensor_scalar(
                CKV[:, l * 2048 + c * 256:l * 2048 + (c + 1) * 256],
                ckf[:, 256:512], GKV[:, l:l + 1], None, ALU.mult),
                reads=("ckf2", "gkv"), writes=("ckv",))
        kp = PSB[0:64, 512:768]
        bld.op("dve", lambda e, cb=cb: e.tensor_tensor(rt1, kp, ckt[cb][0], ALU.mult),
               reads=(bB(1), f"ckt{cb}"), writes=("rt1",))
        bld.op("dve", lambda e, cb=cb: e.tensor_tensor(rt2[0:32, :], kp[32:64, :],
                                                       ckt[cb][1][0:32, :], ALU.mult),
               reads=(bB(1), f"ckt{cb}"), writes=("rt2",))
        bld.op("dve", lambda e, cb=cb: e.tensor_tensor(rt2[32:64, :], kp[0:32, :],
                                                       ckt[cb][1][32:64, :], ALU.mult),
               reads=(bB(1), f"ckt{cb}"), writes=("rt2",))
        bld.op("dve", lambda e, c=c: e.tensor_add(KPE[:, c * 256:(c + 1) * 256], rt1, rt2),
               reads=("rt1", "rt2"), writes=("kpe",))

    SQKP = carve(94 * 1024, 128, 2048, BF16)
    SQCK = carve(98 * 1024, 128, 4096, BF16)
    bld.op("pool", lambda e: e.tensor_tensor(SQKP[0:64, :], KPE, KPE, ALU.mult),
           reads=("kpe",), writes=("sqkp",))
    bld.op("pool", lambda e: e.tensor_tensor(SQCK, CKV, CKV, ALU.mult),
           reads=("ckv",), writes=("sqck",))
    for n in range(4):
        mm(PSB[0:1, n * 512:(n + 1) * 512], (bB(n),), ONESB[0:64, :],
           SQKP[0:64, n * 512:(n + 1) * 512], ("onesb", "sqkp"), True, True)
        for l in range(2):
            mm(PSA[0:1, n * 512:(n + 1) * 512], (bA(n),), ONESB,
               SQCK[:, l * 2048 + n * 512:l * 2048 + (n + 1) * 512],
               ("onesb", "sqck"), l == 0, l == 1)
    bld.op("dve", lambda e: e.reduce_max(SROW[:, 0:1], PSB[0:1, 0:2048], AXX),
           reads=(bB(0), bB(1), bB(2), bB(3)), writes=("brow",))
    bld.op("dve", lambda e: e.reduce_max(SROW[:, 1:2], PSA[0:1, 0:2048], AXX),
           reads=(bA(0), bA(1), bA(2), bA(3)), writes=("brow",))
    mm(PSB[:, 0:2], (bB(0),), ONESF, SROW[:, 0:2], ("onesf", "brow"), True, True)
    bld.op("act", lambda e: e.copy(BND, PSB[:, 0:2]), reads=(bB(0),), writes=("bnd",))

    if debug:
        bld.op("act", lambda e: e.dma_start(out=dbg["xn"], in_=XN),
               reads=("xn",), dma=True)
        bld.op("act", lambda e: e.dma_start(out=dbg["ckv"], in_=CKV),
               reads=("ckv",), dma=True)
        bld.op("act", lambda e: e.dma_start(out=dbg["kpe"], in_=KPE),
               reads=("kpe",), dma=True)
    bld.barrier()

    if phases >= 2:
        _p2(nc, bld, locals())
    if phases >= 3:
        _p3(nc, bld, locals())
    if phases >= 4:
        _p4(nc, bld, locals())
    if phases >= 5:
        _p5(nc, bld, locals())

    from contextlib import ExitStack
    es = ExitStack()
    sems = {}
    for e in ENGS:
        sems[e] = es.enter_context(nc.semaphore(f"m_{e}"))
    dsems = [es.enter_context(nc.semaphore(f"d_{i}")) for i in range(NDMASEM)]
    with nc.Block() as blk:
        emit = bld.materialize(sems, dsems)
        blk.sync(lambda eng: emit(eng, bld.eng_ops["sp"]))
        blk.tensor(lambda eng: emit(eng, bld.eng_ops["pe"]))
        blk.scalar(lambda eng: emit(eng, bld.eng_ops["act"]))
        blk.vector(lambda eng: emit(eng, bld.eng_ops["dve"]))
        blk.gpsimd(lambda eng: emit(eng, bld.eng_ops["pool"]))
    fc = bld.final_counts
    with nc.Block() as blk2:
        def fin(eng):
            for key, v in fc.items():
                if key[0] == "D":
                    eng.wait_ge(dsems[key[1]], v)
                else:
                    eng.wait_ge(sems[key[1]], v)
        blk2.sync(fin)
    es.close()
    _CACHE[ck_] = nc
    return nc


def _p2(nc, bld, env):
    carve, mm, PSA, PSB, bA, bB = (env["carve"], env["mm"], env["PSA"], env["PSB"],
                                   env["bA"], env["bB"])
    wqaP, XN, CQN, GQ, ONESB, ONESF, SROW = (env["wqaP"], env["XN"], env["CQN"],
                                             env["GQ"], env["ONESB"], env["ONESF"],
                                             env["SROW"])
    CT = env["CT"]
    debug, dbg = env["debug"], env["dbg"]
    wqa_st = carve(0, 128, 4096, F32)
    wqa_bf = [carve(16 * 1024, 128, 4096, BF16), carve(24 * 1024, 128, 4096, BF16)]
    cqf = carve(32 * 1024, 128, 8192, F32)
    sqc = carve(64 * 1024, 128, 1024, BF16)

    def _wqa_fetch(oc):
        ob = oc % 2
        bld.op("sp", lambda e, oc=oc: e.dma_start(out=wqa_st, in_=wqaP[:, oc, :]),
               reads=("wqast",), writes=("wqast",), dma=True)
        bld.op("act", lambda e, ob=ob: e.copy(wqa_bf[ob], wqa_st),
               reads=("wqast",), writes=(f"wqab{ob}",))
    _wqa_fetch(0)
    for oc in range(8):
        ob = oc % 2
        if oc + 1 < 8:
            _wqa_fetch(oc + 1)
        ps = PSA[:, 0:1024] if ob == 0 else PSA[:, 1024:2048]
        pbt = (bA(0), bA(1)) if ob == 0 else (bA(2), bA(3))
        for dc in range(32):
            for hf in range(2):
                mm(ps[:, hf * 512:(hf + 1) * 512], (pbt[hf],),
                   wqa_bf[ob][:, dc * 128:(dc + 1) * 128],
                   XN[:, dc * 1024 + hf * 512:dc * 1024 + (hf + 1) * 512],
                   (f"wqab{ob}", "xn"), dc == 0, dc == 31)
        bld.op("act", lambda e, ps=ps: e.activation(sqc, ps, AF.Square),
               reads=pbt, writes=("sqc",))
        for hf in range(2):
            mm(PSB[0:1, hf * 512:(hf + 1) * 512], (bB(hf),),
               ONESB, sqc[:, hf * 512:(hf + 1) * 512],
               ("onesb", "sqc"), oc == 0, oc == 7)
        bld.op("act", lambda e, ps=ps, oc=oc: e.copy(cqf[:, oc * 1024:(oc + 1) * 1024], ps),
               reads=pbt, writes=("cqf",))
    for hf in range(2):
        bld.op("act", lambda e, hf=hf: e.activation(
            SROW[:, hf * 512:(hf + 1) * 512], PSB[0:1, hf * 512:(hf + 1) * 512],
            AF.Ln, bias=CT[0:1, 1:2], scale=1.0 / 1024.0),
            reads=(bB(hf),), writes=("qrow",))
        bld.op("act", lambda e, hf=hf: e.activation(
            SROW[:, hf * 512:(hf + 1) * 512], SROW[:, hf * 512:(hf + 1) * 512],
            AF.Exp, scale=-0.5), reads=("qrow",), writes=("qrow",))
        mm(PSB[:, 1024 + hf * 512:1024 + (hf + 1) * 512], (bB(2 + hf),),
           ONESF, SROW[:, hf * 512:(hf + 1) * 512], ("onesf", "qrow"), True, True)
    rqb = carve(66 * 1024, 128, 1024, F32)
    bld.op("act", lambda e: e.copy(rqb, PSB[:, 1024:2048]),
           reads=(bB(2), bB(3)), writes=("rqb",))
    for oc in range(8):
        bld.op("pool", lambda e, oc=oc: e.tensor_tensor(
            cqf[:, oc * 1024:(oc + 1) * 1024], cqf[:, oc * 1024:(oc + 1) * 1024],
            rqb, ALU.mult),
            reads=("cqf", "rqb"), writes=(f"cqfn{oc}",))
        bld.op("dve", lambda e, oc=oc: e.tensor_scalar(
            CQN[:, oc * 1024:(oc + 1) * 1024], cqf[:, oc * 1024:(oc + 1) * 1024],
            GQ[:, oc:oc + 1], None, ALU.mult),
            reads=(f"cqfn{oc}", "gq"), writes=("cqn",))
    if debug:
        bld.op("act", lambda e: e.dma_start(out=dbg["cqn"], in_=CQN),
               reads=("cqn",), dma=True)


def _p3(nc, bld, env):
    carve, mm, PSA, PSB, bA, bB = (env["carve"], env["mm"], env["PSA"], env["PSB"],
                                   env["bA"], env["bB"])
    (masks, cosq, sinq, wqbP, wkvbkP, wkvbvP, attn_d) = (
        env["masks"], env["cosq"], env["sinq"], env["wqbP"],
        env["wkvbkP"], env["wkvbvP"], env["attn_d"])
    (XN, CQN, CKV, KPE, IDENT, ONESB, ONES128, BIAS8, T8, LROW, LINV) = (
        env["XN"], env["CQN"], env["CKV"], env["KPE"], env["IDENT"], env["ONESB"],
        env["ONES128"], env["BIAS8"], env["T8"], env["LROW"], env["LINV"])
    BND, F2G, KB2, CT = env["BND"], env["F2G"], env["KB2"], env["CT"]
    debug, dbg = env["debug"], env["dbg"]

    MASKS = carve(0, 128, 256, F32)
    COSQ = carve(2 * 1024, 64, 1024, F32)
    SINQ = carve(6 * 1024, 64, 1024, F32)
    QH = [carve(10 * 1024, 128, 1024, BF16), carve(12 * 1024, 128, 1024, BF16)]
    KF = [carve(14 * 1024, 128, 2048, BF16), carve(18 * 1024, 128, 2048, BF16)]
    V4 = carve(22 * 1024, 128, 8192, BF16)
    V43 = V4.rearrange("p (k v) -> p k v", v=512)
    PBUF = carve(38 * 1024, 128, PTOT, BF16)
    PT = carve(56 * 1024, 128, 8192, BF16)
    PT4 = PT.rearrange("p (k j c) -> p k j c", j=8, c=128)
    RTA = carve(72 * 1024, 64, 1024, F32)
    RTB = carve(76 * 1024, 64, 1024, F32)
    SQKV = carve(76 * 1024, 128, 512, BF16)
    Q2K = carve(72 * 1024, 128, 2048, BF16)
    WQB_ST = carve(80 * 1024, 128, 1024, F32)
    WQB_BF = [carve(84 * 1024, 128, 1024, BF16), carve(86 * 1024, 128, 1024, BF16)]
    KVK_ST = carve(88 * 1024, 128, 512, F32)
    KVV_ST = carve(90 * 1024, 128, 1024, F32)
    KVK_BF = [carve(94 * 1024, 128, 512, BF16), carve(95 * 1024, 128, 512, BF16)]
    KVV_BF = [carve(96 * 1024, 128, 1024, BF16), carve(98 * 1024, 128, 1024, BF16)]
    ATTS = carve(100 * 1024, 128, 1024, BF16)
    QSTG = carve(102 * 1024, 128, 1024, F32)

    units = []
    for j in range(NSLOT):
        n = LJ[j] * 128
        for k0 in range(0, n, 512):
            units.append((j, k0, min(k0 + 512, n)))
    ufirst = {}
    for ui, (j, _, _k) in enumerate(units):
        ufirst.setdefault(j, ui)

    def _wqb_fetch(h):
        qb = h % 2
        bld.op("sp", lambda e, h=h: e.dma_start(out=WQB_ST, in_=wqbP[:, h, :]),
               reads=("wqbst",), writes=("wqbst",), dma=True)
        bld.op("act", lambda e, qb=qb: e.copy(WQB_BF[qb], WQB_ST),
               reads=("wqbst",), writes=(f"wqbb{qb}",))

    def _kv_fetch(g):
        gb = g % 2
        bld.op("sp", lambda e, g=g: e.dma_start(out=KVK_ST, in_=wkvbkP[:, 4 * g:4 * g + 4, :]),
               reads=("kvkst",), writes=("kvkst",), dma=True)
        bld.op("sp", lambda e, g=g: e.dma_start(out=KVV_ST, in_=wkvbvP[:, g, :]),
               reads=("kvvst",), writes=("kvvst",), dma=True)
        bld.op("pool", lambda e, gb=gb: e.tensor_copy(KVK_BF[gb], KVK_ST),
               reads=("kvkst",), writes=(f"kvkb{gb}",))
        bld.op("pool", lambda e, gb=gb: e.tensor_copy(KVV_BF[gb], KVV_ST),
               reads=("kvvst",), writes=(f"kvvb{gb}",))

    def _vgen(g):
        gb = g % 2
        if g + 1 < 8:
            _kv_fetch(g + 1)
        bld.op("pool", lambda e, gb=gb: e.tensor_tensor(SQKV, KVK_BF[gb], KVK_BF[gb],
                                                        ALU.mult),
               reads=(f"kvkb{gb}",), writes=("rtb",))
        mm(PSB[:, 1024:1536], (bB(2),), ONES128, SQKV, ("ones128", "rtb"), True, True)
        bld.op("dve", lambda e, gb=gb: e.reduce_sum(
            F2G[:, 4 * gb:4 * gb + 4],
            PSB[:, 1024:1536].rearrange("p (h c) -> p h c", c=128), AXX),
            reads=(bB(2),), writes=(f"f2g{gb}",))
        for kt in range(16):
            vb = kt % 2
            for l in range(2):
                mm(PSB[:, vb * 512:(vb + 1) * 512], (bB(vb),),
                   CKV[:, l * 2048 + kt * 128:l * 2048 + (kt + 1) * 128],
                   KVV_BF[gb][:, l * 512:(l + 1) * 512],
                   ("ckv", f"kvvb{gb}"), l == 0, l == 1)
            bld.op("act", lambda e, kt=kt, vb=vb: e.copy(
                V43[:, kt, :], PSB[:, vb * 512:(vb + 1) * 512]),
                reads=(bB(vb),), writes=("v4",))
        if debug and g == 0:
            bld.op("act", lambda e: e.dma_start(out=dbg["v4"], in_=V4),
                   reads=("v4",), dma=True)

    def _gen_a(h):
        g, hh = h // 4, h % 4
        gb = g % 2
        qb = kb = h % 2
        if h + 1 < 32:
            _wqb_fetch(h + 1)
        for dc in range(8):
            for hf in range(2):
                mm(PSA[:, hf * 512:(hf + 1) * 512], (bA(hf),),
                   WQB_BF[qb][:, dc * 128:(dc + 1) * 128],
                   CQN[:, dc * 1024 + hf * 512:dc * 1024 + (hf + 1) * 512],
                   (f"wqbb{qb}", "cqn"), dc == 0, dc == 7)
        bld.op("act", lambda e: e.copy(QSTG, PSA[:, 0:1024]),
               reads=(bA(0), bA(1)), writes=("qstg",))
        for n in range(4):
            pb_ = 2 + (n % 2)
            for l in range(2):
                mm(PSB[0:64, pb_ * 512:(pb_ + 1) * 512], (bB(pb_),),
                   KVK_BF[gb][:, hh * 128 + l * 64:hh * 128 + l * 64 + 64],
                   CKV[:, l * 2048 + n * 512:l * 2048 + (n + 1) * 512],
                   (f"kvkb{gb}", "ckv"), l == 0, l == 1)
            bld.op("dve", lambda e, n=n, kb=kb, pb_=pb_: e.tensor_copy(
                KF[kb][64:128, n * 512:(n + 1) * 512],
                PSB[0:64, pb_ * 512:(pb_ + 1) * 512]),
                reads=(bB(pb_),), writes=(f"kf{kb}",))
    def _gen_b1(h):
        g, hh = h // 4, h % 4
        gb = g % 2
        qb = hb = h % 2
        bld.op("act", lambda e: e.copy(RTA[0:32, :], QSTG[32:64, :]),
               reads=("qstg",), writes=("rta", "q2k"))
        bld.op("act", lambda e: e.copy(RTA[32:64, :], QSTG[0:32, :]),
               reads=("qstg",), writes=("rta", "q2k"))
        bld.op("dve", lambda e: e.tensor_tensor(RTB, RTA, SINQ, ALU.mult),
               reads=("rta", "sinq"), writes=("rtb",))
        bld.op("dve", lambda e: e.tensor_tensor(RTA, QSTG[0:64, :], COSQ, ALU.mult),
               reads=("qstg", "cosq", "rtb"), writes=("rta", "q2k"))
        bld.op("dve", lambda e, qb=qb: e.tensor_add(QH[qb][0:64, :], RTA, RTB),
               reads=("rta", "rtb"), writes=(f"qh{qb}",))
        bld.op("act", lambda e, qb=qb: e.copy(QH[qb][64:128, :], QSTG[64:128, :]),
               reads=("qstg",), writes=(f"qh{qb}",))
        bld.op("act", lambda e: e.activation(Q2K[:, 0:1024], QSTG, AF.Square),
               reads=("qstg",), writes=("q2k", "rta"))

    def _gen_b2(h):
        g, hh = h // 4, h % 4
        gb = g % 2
        qb = hb = h % 2
        for j in range(8):
            mm(PSB[:, 1536 + 8 * hb + j:1537 + 8 * hb + j], (bB(3),),
               Q2K[:, j * 128:(j + 1) * 128], ONESB,
               ("q2k", "onesb"), True, True)
        bld.op("dve", lambda e, gb=gb, hh=hh, hb=hb: e.tensor_tensor(
            KB2[:, hb:hb + 1], F2G[:, 4 * gb + hh:4 * gb + hh + 1], BND[:, 1:2],
            ALU.mult),
            reads=(f"f2g{gb}", "bnd"), writes=(f"kb2{hb}",))
        bld.op("dve", lambda e, hb=hb: e.tensor_add(
            KB2[:, hb:hb + 1], KB2[:, hb:hb + 1], BND[:, 0:1]),
            reads=(f"kb2{hb}", "bnd"), writes=(f"kb2{hb}",))
        t8v = T8[:, 8 * hb:8 * hb + 8]
        bld.op("dve", lambda e, hb=hb, t8v=t8v: e.tensor_scalar(
            t8v, PSB[:, 1536 + 8 * hb:1544 + 8 * hb], KB2[:, hb:hb + 1], None,
            ALU.mult),
            reads=(bB(3), f"kb2{hb}"), writes=(f"t8{hb}",))
        bld.op("act", lambda e, t8v=t8v: e.activation(t8v, t8v, AF.Ln, bias=CT[:, 2:3]),
               reads=(f"t8{hb}",), writes=(f"t8{hb}",))
        bld.op("act", lambda e, t8v=t8v: e.activation(t8v, t8v, AF.Exp, scale=0.5),
               reads=(f"t8{hb}",), writes=(f"t8{hb}",))
        bld.op("dve", lambda e, hb=hb, t8v=t8v: e.tensor_scalar(
            BIAS8[:, 8 * hb:8 * hb + 8], t8v, -SOFTMAX_SCALE, MARGIN,
            ALU.mult, ALU.add),
            reads=(f"t8{hb}",), writes=(f"bias8{hb}",))

    def _att_units(h):
        g, hh = h // 4, h % 4
        qb = kb = hb = h % 2
        pending = []
        for ui, (j, klo, khi) in enumerate(units):
            rg = ui % 4
            reg = PSA[:, rg * 512:(rg + 1) * 512]
            rtags = (bA(rg),)
            n = khi - klo
            mm(reg[:, 0:n], rtags,
               QH[qb][:, j * 128:(j + 1) * 128],
               KF[kb][:, klo:khi],
               (f"qh{qb}", f"kf{kb}"), True, True)
            while len(pending) > 2:
                pending.pop(0)()
            last = (khi == LJ[j] * 128)
            if last:
                moff = n - 256
                bld.op("dve", lambda e, reg=reg, moff=moff: e.tensor_tensor(
                    reg[:, moff:moff + 256], reg[:, moff:moff + 256],
                    MASKS, ALU.add),
                    reads=rtags + ("masks",), writes=rtags)
            uoff = OFFJ[j] + klo
            uix = 24 * hb + ui
            bld.op("act", lambda e, reg=reg, n=n, uoff=uoff, j=j, uix=uix, hb=hb:
                   e.activation(PBUF[:, uoff:uoff + n], reg[:, 0:n], AF.Exp,
                                bias=BIAS8[:, 8 * hb + j:8 * hb + j + 1],
                                scale=SOFTMAX_SCALE,
                                accum_out=LROW[:, uix:uix + 1]),
                   reads=rtags + (f"bias8{hb}",), writes=(f"pb{j}", f"lrow{uix}"))
            if last:
                u0 = 24 * hb + ufirst[j]
                for ux in range(u0, uix):
                    bld.op("dve", lambda e, uix=uix, ux=ux: e.tensor_add(
                        LROW[:, uix:uix + 1], LROW[:, uix:uix + 1],
                        LROW[:, ux:ux + 1]),
                        reads=(f"lrow{uix}", f"lrow{ux}"), writes=(f"lrow{uix}",))
                jx = 8 * hb + j
                bld.op("dve", lambda e, jx=jx, uix=uix: e.reciprocal(
                    LINV[:, jx:jx + 1], LROW[:, uix:uix + 1]),
                    reads=(f"lrow{uix}",), writes=(f"linv{jx}",))
                bld.op("dve", lambda e, j=j, jx=jx: e.tensor_single_scalar(
                    PBUF[:, OFFJ[j]:OFFJ[j] + LJ[j] * 128],
                    PBUF[:, OFFJ[j]:OFFJ[j] + LJ[j] * 128],
                    LINV[:, jx:jx + 1], ALU.mult),
                    reads=(f"pb{j}", f"linv{jx}"), writes=(f"pb{j}",))

                def _mk_tr(j=j):
                    def go():
                        c0 = min(LJ[j], 8)
                        ptv = PSB[:, 0:512].bitcast(BF16)
                        for kt in range(c0):
                            bld.op("pe", lambda e, j=j, kt=kt, ptv=ptv: e.transpose(
                                ptv[:, kt * 128:(kt + 1) * 128],
                                PBUF[:, OFFJ[j] + kt * 128:OFFJ[j] + (kt + 1) * 128],
                                IDENT),
                                reads=(f"pb{j}", "ident"), writes=(bB(0),))
                        ev_eng = "dve" if (j % 2 == 0) else "act"

                        def _ev0(e, j=j, c0=c0, ptv=ptv):
                            s_ = ptv.rearrange("p (k c) -> p k c", c=128)[:, 0:c0, :]
                            d_ = PT4[:, 0:c0, j, :]
                            return (e.tensor_copy(d_, s_) if hasattr(e, "tensor_copy")
                                    else e.copy(d_, s_))
                        bld.op(ev_eng, _ev0, reads=(bB(0),), writes=(f"ptg{j}",))
                    return go
                pending.append(_mk_tr())
        for fn_ in pending:
            fn_()

    def _att_av(h):
        g, hh = h // 4, h % 4
        qb = kb = hb = h % 2
        for kt in range(8):
            j0 = kt // 2
            rd = tuple(f"ptg{j}" for j in range(j0, 8)) + ("v4",)
            lhs = V43[:, kt, hh * 128:(hh + 1) * 128]
            base = kt * 1024
            if j0 * 128 < 512:
                mm(PSB[:, 1024 + j0 * 128:1536], (bB(2),),
                   lhs, PT[:, base + j0 * 128:base + 512], rd, kt == 0, kt == 7)
            mm(PSB[:, 1536:2048], (bB(3),),
               lhs, PT[:, base + 512:base + 1024], rd, kt == 0, False)
        for j in range(4, 8):
            c1 = LJ[j] - 8
            ptv = PSB[:, 0:512].bitcast(BF16)
            for kt in range(c1):
                bld.op("pe", lambda e, j=j, kt=kt, ptv=ptv: e.transpose(
                    ptv[:, kt * 128:(kt + 1) * 128],
                    PBUF[:, OFFJ[j] + (8 + kt) * 128:OFFJ[j] + (9 + kt) * 128],
                    IDENT),
                    reads=(f"pb{j}", "ident"), writes=(bB(0),))
            ev_eng = "dve" if (j % 2 == 0) else "act"

            def _ev1(e, j=j, c1=c1, ptv=ptv):
                s_ = ptv.rearrange("p (k c) -> p k c", c=128)[:, 0:c1, :]
                d_ = PT4[:, 0:c1, j, :]
                return (e.tensor_copy(d_, s_) if hasattr(e, "tensor_copy")
                        else e.copy(d_, s_))
            bld.op(ev_eng, _ev1, reads=(bB(0),), writes=(f"ptg{j}",))
        for kt in range(8, 16):
            j0 = kt // 2
            rd = tuple(f"ptg{j}" for j in range(j0, 8)) + ("v4",)
            lhs = V43[:, kt, hh * 128:(hh + 1) * 128]
            base = (kt - 8) * 1024
            mm(PSB[:, 1024 + j0 * 128:2048], (bB(3),),
               lhs, PT[:, base + j0 * 128:base + 1024], rd, False, kt == 15)
        bld.op("dve", lambda e: e.tensor_copy(ATTS, PSB[:, 1024:2048]),
               reads=(bB(2), bB(3)), writes=("atts",))
        bld.op("act", lambda e, h=h: e.dma_start(out=attn_d[:, h, :], in_=ATTS),
               reads=("atts",), writes=(f"attnd{h}",), dma=True)
        if debug and h == 0:
            bld.op("act", lambda e: e.dma_start(out=dbg["qh"], in_=QH[0]),
                   reads=("qh0",), dma=True)
            bld.op("act", lambda e: e.dma_start(out=dbg["kf"], in_=KF[0]),
                   reads=("kf0",), dma=True)
            bld.op("act", lambda e: e.dma_start(out=dbg["pb"], in_=PBUF),
                   reads=tuple(f"pb{j}" for j in range(8)), dma=True)

    _kv_fetch(0)
    _wqb_fetch(0)
    bld.barrier()
    bld.op("sp", lambda e: e.dma_start(out=MASKS, in_=masks), writes=("masks",), dma=True)
    bld.op("sp", lambda e: e.dma_start(out=COSQ, in_=cosq), writes=("cosq",), dma=True)
    bld.op("sp", lambda e: e.dma_start(out=SINQ, in_=sinq), writes=("sinq",), dma=True)
    for kb in range(2):
        bld.op("pool", lambda e, kb=kb: e.tensor_copy(KF[kb][0:64, :], KPE),
               reads=("kpe",), writes=(f"kf{kb}",))
    _vgen(0)
    _gen_a(0)
    _gen_b1(0)
    _gen_b2(0)
    for h in range(32):
        nh = h + 1
        boundary = (nh < 32 and nh % 4 == 0)
        if nh < 32 and not boundary:
            _gen_a(nh)
        _att_units(h)
        if nh < 32 and not boundary:
            _gen_b1(nh)
        _att_av(h)
        if nh < 32:
            if boundary:
                _vgen(nh // 4)
                _gen_a(nh)
                _gen_b1(nh)
            _gen_b2(nh)
    if debug:
        bld.op("sp", lambda e: e.dma_start(out=dbg["at"], in_=attn_d),
               reads=tuple(f"attnd{h}" for h in range(32)), dma=True)
    bld.barrier()


def _p4(nc, bld, env):
    carve, mm, PSA, PSB, bA, bB = (env["carve"], env["mm"], env["PSA"], env["PSB"],
                                   env["bA"], env["bB"])
    woP, attn_d, o_d = env["woP"], env["attn_d"], env["o_d"]
    debug, dbg = env["debug"], env["dbg"]
    ATTN = carve(0, 128, 32 * 1024, BF16)
    at3 = ATTN.rearrange("p (h t) -> p h t", t=1024)
    WO_ST = [carve(64 * 1024, 128, 2048, F32), carve(72 * 1024, 128, 2048, F32)]
    WO_BF = [carve(80 * 1024, 128, 4096, BF16), carve(88 * 1024, 128, 4096, BF16)]
    OSTG = carve(96 * 1024, 128, 1024, F32)

    for q in range(4):
        bld.op("sp", lambda e, q=q: e.dma_start(
            out=at3[:, q * 8:(q + 1) * 8, :], in_=attn_d[:, q * 8:(q + 1) * 8, :]),
            reads=tuple(f"attnd{h2}" for h2 in range(q * 8, q * 8 + 8)),
            writes=("attn",), dma=True)
    def _wo_fetch(oc):
        ob = oc % 2
        for hf in range(2):
            dq = "sp" if hf == 0 else "act"
            bld.op(dq, lambda e, oc=oc, hf=hf: e.dma_start(
                out=WO_ST[hf], in_=woP[:, oc, hf * 2048:(hf + 1) * 2048]),
                reads=(f"wost{hf}",), writes=(f"wost{hf}",), dma=True)
            eng = "act" if hf == 0 else "dve"
            def _woc(e, hf=hf, ob=ob):
                d_ = WO_BF[ob][:, hf * 2048:(hf + 1) * 2048]
                return e.copy(d_, WO_ST[hf]) if hf == 0 else e.tensor_copy(d_, WO_ST[hf])
            bld.op(eng, _woc, reads=(f"wost{hf}",), writes=(f"wob{ob}",))
    _wo_fetch(0)
    for oc in range(32):
        ob = oc % 2
        if oc + 1 < 32:
            _wo_fetch(oc + 1)
        ps = PSA[:, 0:1024] if ob == 0 else PSA[:, 1024:2048]
        pbt = (bA(0), bA(1)) if ob == 0 else (bA(2), bA(3))
        for hv in range(32):
            for hf in range(2):
                mm(ps[:, hf * 512:(hf + 1) * 512], (pbt[hf],),
                   WO_BF[ob][:, hv * 128:(hv + 1) * 128],
                   at3[:, hv, hf * 512:(hf + 1) * 512],
                   (f"wob{ob}", "attn"), hv == 0, hv == 31)
        bld.op("act", lambda e, ps=ps: e.copy(OSTG, ps), reads=pbt, writes=("ostg",))
        bld.op("act", lambda e, oc=oc: e.dma_start(out=o_d[:, oc, :], in_=OSTG),
               reads=("ostg",), writes=(f"od{oc}",), dma=True)
    if debug:
        bld.op("sp", lambda e: e.dma_start(out=dbg["od"], in_=o_d),
               reads=tuple(f"od{oc}" for oc in range(32)), dma=True)
    bld.barrier()


def _p5(nc, bld, env):
    carve, mm, PSA, PSB, bA, bB = (env["carve"], env["mm"], env["PSA"], env["PSB"],
                                   env["bA"], env["bB"])
    (xP, gateP, upP, downP, o_d, outP, XN, SCAL, DSEFF) = (
        env["xP"], env["gateP"], env["upP"], env["downP"], env["o_d"], env["outP"],
        env["XN"], env["SCAL"], env["DSEFF"])
    ICMAX = max(ICB_SPLIT)
    INTERB = carve(0, 128, ICMAX * 512, BF16)
    G_ST = carve(30 * 1024, 128, 4096, F32)
    U_ST = carve(46 * 1024, 128, 4096, F32)
    G_BF = [carve(62 * 1024, 128, 4096, BF16), carve(70 * 1024, 128, 4096, BF16)]
    U_BF = [carve(78 * 1024, 128, 4096, BF16), carve(86 * 1024, 128, 4096, BF16)]
    SILU = carve(94 * 1024, 128, 512, F32)
    D_ST = [carve(30 * 1024, 128, ICMAX * 128, F32),
            carve(45 * 1024, 128, ICMAX * 128, F32)]
    D_BF = [carve(60 * 1024, 128, ICMAX * 128, BF16),
            carve(68 * 1024, 128, ICMAX * 128, BF16)]
    XOC = [carve(76 * 1024, 128, 512, F32), carve(78 * 1024, 128, 512, F32)]
    OOC = [carve(80 * 1024, 128, 512, F32), carve(82 * 1024, 128, 512, F32)]
    RIN = [carve(84 * 1024, 128, 512, F32), carve(86 * 1024, 128, 512, F32)]
    OUTS = [carve(88 * 1024, 128, 512, F32), carve(90 * 1024, 128, 512, F32)]

    def _gu_fetch(ic, ib, after_dma=(), after_cast=()):
        o1 = bld.op("sp", lambda e, ic=ic: e.dma_start(out=G_ST, in_=gateP[:, ic, :]),
                    reads=("gst",), writes=("gst",), dma=True, after=after_dma)
        o2 = bld.op("act", lambda e, ib=ib: e.copy(G_BF[ib], G_ST),
                    reads=("gst",), writes=(f"gbf{ib}",), after=after_cast)
        o3 = bld.op("act", lambda e, ic=ic: e.dma_start(out=U_ST, in_=upP[:, ic, :]),
                    reads=("ust",), writes=("ust",), dma=True, after=after_dma)
        o4 = bld.op("dve", lambda e, ib=ib: e.tensor_copy(U_BF[ib], U_ST),
                    reads=("ust",), writes=(f"ubf{ib}",), after=after_cast)
        return o2, o4

    trans = {"mm": None, "dacts": (), "stt": None, "silu": None}
    for th in range(2):
        for icb in range(3):
            ic0 = sum(ICB_SPLIT[:icb])
            nic = ICB_SPLIT[icb]
            ad = tuple(x for x in trans["dacts"] if x is not None)
            am = (trans["mm"],) if trans["mm"] is not None else ()
            lastg, lastu = _gu_fetch(ic0, 0, after_dma=ad + am, after_cast=am + ad)
            lastmm = None
            lastsilu = None
            laststt = None
            for i in range(nic):
                ic = ic0 + i
                ib = i % 2
                if i + 1 < nic:
                    xa = (am + ad) if i == 0 else ()
                    lastg, lastu = _gu_fetch(ic + 1, (i + 1) % 2,
                                             after_dma=xa, after_cast=xa)
                pg = PSA[:, 0:512] if ib == 0 else PSA[:, 1024:1536]
                pu = PSA[:, 512:1024] if ib == 0 else PSA[:, 1536:2048]
                tg = bA(0) if ib == 0 else bA(2)
                tu = bA(1) if ib == 0 else bA(3)
                af = am if i == 0 else ()
                for dc in range(32):
                    mm(pg, (tg,), G_BF[ib][:, dc * 128:(dc + 1) * 128],
                       XN[:, dc * 1024 + th * 512:dc * 1024 + (th + 1) * 512],
                       (f"gbf{ib}", "xn"), dc == 0, dc == 31)
                    lastmm = mm(pu, (tu,), U_BF[ib][:, dc * 128:(dc + 1) * 128],
                                XN[:, dc * 1024 + th * 512:dc * 1024 + (th + 1) * 512],
                                (f"ubf{ib}", "xn"), dc == 0, dc == 31)
                lastsilu = bld.op("act", lambda e, pg=pg, tg=tg: e.activation(
                    SILU, pg, AF.Silu, scale=SCAL[:, 0:1]),
                    reads=(tg, "scal"), writes=("silu",))
                laststt = bld.op("dve", lambda e, pu=pu, i=i: e.scalar_tensor_tensor(
                    INTERB[:, i * 512:(i + 1) * 512], SILU, DSEFF, pu,
                    ALU.mult, ALU.mult),
                    reads=("silu", "dseff", tu), writes=("inter",), after=af)
            def _dn_dma(oc, ic0=ic0, nic=nic, after=()):
                ob = oc % 2
                bld.op("sp", lambda e, oc=oc, ob=ob: e.dma_start(
                    out=D_ST[ob][:, 0:nic * 128],
                    in_=downP[:, oc, ic0 * 128:(ic0 + nic) * 128]),
                    reads=(f"dst{ob}",), writes=(f"dst{ob}",), dma=True, after=after)

            dn_acts = []

            def _dn_fetch(oc, th=th, icb=icb, ic0=ic0, nic=nic, after=()):
                ob = oc % 2
                nh = (nic * 128) // 2
                dn_acts.append(bld.op("act", lambda e, ob=ob, nh=nh: e.copy(
                    D_BF[ob][:, 0:nh], D_ST[ob][:, 0:nh]),
                    reads=(f"dst{ob}",), writes=(f"dbf{ob}",), after=after))
                dn_acts.append(bld.op("dve", lambda e, ob=ob, nh=nh: e.tensor_copy(
                    D_BF[ob][:, nh:nic * 128], D_ST[ob][:, nh:nic * 128]),
                    reads=(f"dst{ob}",), writes=(f"dbf{ob}",), after=after))
                if icb == 0:
                    xoc3 = XOC[ob].rearrange("p (s t) -> p s t", t=128)
                    bld.op("act", lambda e, oc=oc, xoc3=xoc3: e.dma_start(
                        out=xoc3,
                        in_=xP[:, oc, th * 1024:(th + 1) * 1024].rearrange(
                            "p (s t) -> p s t", t=256)[:, :, 0:128]),
                        reads=(f"xoc{ob}",), writes=(f"xoc{ob}",), dma=True, after=after)
                    bld.op("act", lambda e, oc=oc, ob=ob: e.dma_start(
                        out=OOC[ob], in_=o_d[:, oc, th * 512:(th + 1) * 512]),
                        reads=(f"ooc{ob}", f"od{oc}"), writes=(f"ooc{ob}",), dma=True,
                        after=after)
                else:
                    bld.op("act", lambda e, oc=oc, ob=ob: e.dma_start(
                        out=RIN[ob], in_=outP[:, oc, th * 512:(th + 1) * 512]),
                        reads=(f"rin{ob}", f"out_{oc}_{th}"), writes=(f"rin{ob}",),
                        dma=True, after=after)

            _dn_dma(0, after=(lastg, lastu))
            _dn_dma(1, after=(lastg, lastu))
            _dn_fetch(0, after=(lastmm, lastsilu, laststt))
            out_slots = {}
            for oc in range(32):
                ob = oc % 2
                if oc + 2 < 32:
                    _dn_dma(oc + 2)
                if oc + 1 < 32:
                    _dn_fetch(oc + 1, after=(lastmm,) if oc == 0 else ())
                ps = PSB[:, 0:512] if ob == 0 else PSB[:, 512:1024]
                pt = bB(0) if ob == 0 else bB(1)
                for i in range(nic):
                    lastmm = mm(ps, (pt,), D_BF[ob][:, i * 128:(i + 1) * 128],
                                INTERB[:, i * 512:(i + 1) * 512],
                                (f"dbf{ob}", "inter"), i == 0, i == nic - 1)
                if icb == 0:
                    bld.op("dve", lambda e, ps=ps, ob=ob: e.tensor_tensor(
                        OUTS[ob], ps, OOC[ob], ALU.add),
                        reads=(pt, f"ooc{ob}", f"outs{ob}"), writes=(f"outs{ob}",)),
                    bld.op("dve", lambda e, ob=ob: e.tensor_tensor(
                        OUTS[ob], OUTS[ob], XOC[ob], ALU.add),
                        reads=(f"outs{ob}", f"xoc{ob}"), writes=(f"outs{ob}",))
                else:
                    bld.op("dve", lambda e, ps=ps, ob=ob: e.tensor_tensor(
                        OUTS[ob], ps, RIN[ob], ALU.add),
                        reads=(pt, f"rin{ob}", f"outs{ob}"), writes=(f"outs{ob}",))
                oo = bld.op("act", lambda e, oc=oc, th=th, ob=ob: e.dma_start(
                    out=outP[:, oc, th * 512:(th + 1) * 512], in_=OUTS[ob]),
                    reads=(f"outs{ob}",), writes=(f"out_{oc}_{th}",), dma=True)
                out_slots[oo.dsem] = oo
            trans = {"mm": lastmm,
                     "dacts": tuple(dn_acts[-4:]) + tuple(out_slots.values()),
                     "stt": laststt, "silu": lastsilu}
            dn_acts.clear()


def kernel(**inputs):
    pk = _pack_shared(inputs)
    cos, sin = _yarn_cos_sin()
    nc = _build(DEBUG, PHASES)
    in_maps = []
    for c in range(8):
        b, p = c // 2, c % 2
        m = dict(pk)
        m.update(_core_pack(inputs, cos, sin, b, p))
        in_maps.append(m)
    res = run_bass_kernel_spmd(nc, in_maps, core_ids=list(range(8)))
    out = np.empty((B, S, D), np.float32)
    for c in range(8):
        b, p = c // 2, c % 2
        O = np.asarray(res.results[c]["outP"])
        t = O.reshape(128, 32, 8, 128).transpose(2, 3, 1, 0).reshape(8, 128, 4096)
        ov = out[b].reshape(16, 128, 4096)
        for j in range(8):
            ov[2 * j + p] = t[j]
    return out


# version 21
# speedup vs baseline: 1.1686x; 1.1686x over previous
import math

import numpy as np

import concourse.bass as bass
import concourse.mybir as mybir
from concourse.bass_utils import run_bass_kernel_spmd

F32 = mybir.dt.float32
BF16 = mybir.dt.bfloat16
AF = mybir.ActivationFunctionType
ALU = mybir.AluOpType
AXX = mybir.AxisListType.X

H = 32
NOPE = 64
VD = 128
QHD = 128
KVR = 256
D = 4096
S = 2048
B = 4
INTER = 11008
NIC = INTER // 128
ICB_SPLIT = [29, 29, 28]
EPS_LN = 1e-5
EPS_RMS = 1e-6
_m = 0.1 * 1.0 * math.log(128.0) + 1.0
SOFTMAX_SCALE = (QHD ** -0.5) * _m * _m
MARGIN = 55.0
NEG = -1.0e9

NQ = 1024
NSLOT = 8
LJ = [2 * (j + 1) for j in range(NSLOT)]
OFFJ = [j * (j + 1) * 128 for j in range(NSLOT)]
PTOT = OFFJ[-1] + LJ[-1] * 128


def _yarn_cos_sin(seq_len=S, dim=64, base=10000.0, factor=128.0, beta_fast=32.0,
                  beta_slow=1.0, orig_max=8192, mscale=1.0, mscale_all_dim=1.0):
    def corr_dim(r):
        return dim * math.log(orig_max / (r * 2 * math.pi)) / (2 * math.log(base))
    low = max(math.floor(corr_dim(beta_fast)), 0)
    high = min(math.ceil(corr_dim(beta_slow)), dim - 1)
    hi = high + 0.001 if low == high else float(high)
    ramp = np.clip((np.arange(dim // 2, dtype=np.float32) - low) / (hi - low), 0.0, 1.0)
    inv_freq_mask = 1.0 - ramp
    freq_extra = 1.0 / base ** (np.arange(0, dim, 2, dtype=np.float32) / dim)
    freq_inter = freq_extra / factor
    inv_freq = freq_inter * (1.0 - inv_freq_mask) + freq_extra * inv_freq_mask
    t = np.arange(seq_len, dtype=np.float32)
    freqs = np.outer(t, inv_freq)
    emb = np.concatenate([freqs, freqs], axis=-1)

    def gm(s_, m_):
        return 1.0 if s_ <= 1 else 0.1 * m_ * math.log(s_) + 1.0
    ms = gm(factor, mscale) / gm(factor, mscale_all_dim)
    return (np.cos(emb) * ms).astype(np.float32), (np.sin(emb) * ms).astype(np.float32)


def _perm_tiles(p):
    order = []
    for j in range(8):
        order.append(2 * j + p)
        order.append(2 * j + 1 - p)
    return order


def _pack_shared(inp):
    pk = {}
    A = np.ascontiguousarray(np.asarray(inp["L0_attn_wq_a"], np.float32).T).reshape(32, 128, 8, 128)
    pk["wqaP"] = np.ascontiguousarray(A.transpose(1, 2, 0, 3).reshape(128, 8, 4096))
    perm = np.empty(4096, dtype=np.int64)
    for h in range(H):
        b0 = h * 128
        perm[b0:b0 + 32] = b0 + 64 + 2 * np.arange(32)
        perm[b0 + 32:b0 + 64] = b0 + 64 + 2 * np.arange(32) + 1
        perm[b0 + 64:b0 + 128] = b0 + np.arange(64)
    W = np.asarray(inp["L0_attn_wq_b"], np.float32)[perm].reshape(32, 128, 8, 128)
    pk["wqbP"] = np.ascontiguousarray(W.transpose(3, 0, 2, 1).reshape(128, 32, 1024))
    permk = np.concatenate([np.arange(256), 256 + 2 * np.arange(32),
                            256 + 2 * np.arange(32) + 1])
    A = np.asarray(inp["L0_attn_wkv_a"], np.float32)[permk].reshape(320, 32, 128)
    pk["wkvaP"] = np.ascontiguousarray(A.transpose(2, 1, 0).reshape(128, 32 * 320))
    Wb = np.asarray(inp["L0_attn_wkv_b"], np.float32).reshape(H, NOPE + VD, KVR)
    K = Wb[:, 0:NOPE, :].reshape(H, NOPE, 2, 128)
    pk["wkvbkP"] = np.ascontiguousarray(K.transpose(3, 0, 2, 1).reshape(128, 32, 128))
    V = Wb[:, NOPE:, :].reshape(8, 4, VD, 2, 128)
    pk["wkvbvP"] = np.ascontiguousarray(V.transpose(4, 0, 3, 1, 2).reshape(128, 8, 1024))
    W = np.ascontiguousarray(np.asarray(inp["L0_attn_wo"], np.float32).T).reshape(32, 128, 32, 128)
    pk["woP"] = np.ascontiguousarray(W.transpose(1, 2, 0, 3).reshape(128, 32, 4096))
    G = np.ascontiguousarray(np.asarray(inp["L0_ffn_gate_w"], np.float32).T).reshape(32, 128, NIC, 128)
    pk["gateP"] = np.ascontiguousarray(G.transpose(1, 2, 0, 3).reshape(128, NIC, 4096))
    U = np.ascontiguousarray(np.asarray(inp["L0_ffn_up_w"], np.float32).T).reshape(32, 128, NIC, 128)
    pk["upP"] = np.ascontiguousarray(U.transpose(1, 2, 0, 3).reshape(128, NIC, 4096))
    Dw = np.ascontiguousarray(np.asarray(inp["L0_ffn_down_w"], np.float32).T).reshape(NIC, 128, 32, 128)
    pk["downP"] = np.ascontiguousarray(Dw.transpose(1, 2, 0, 3).reshape(128, 32, INTER))
    pk["qgain"] = np.ascontiguousarray(np.asarray(inp["L0_attn_q_a_ln_w"], np.float32).reshape(8, 128).T)
    pk["kvgain"] = np.ascontiguousarray(np.asarray(inp["L0_attn_kv_a_ln_w"], np.float32).reshape(2, 128).T)
    sc = np.array([float(inp["L0_ffn_gate_scale"]), float(inp["L0_ffn_up_scale"]),
                   float(inp["L0_ffn_down_scale"])], np.float32).reshape(1, 3)
    pk["scal"] = np.ascontiguousarray(np.broadcast_to(sc, (128, 3)).astype(np.float32))
    pk["ident"] = np.eye(128, dtype=np.float32)
    return pk


def _core_pack(inp, cos, sin, b, p):
    d = {}
    order = _perm_tiles(p)
    x = np.asarray(inp["L0_attn_x"][b], np.float32)
    xt = x.reshape(16, 128, 4096)[order].reshape(2048, 4096)
    xt = np.ascontiguousarray(xt.T).reshape(32, 128, S)
    d["xP"] = np.ascontiguousarray(xt.transpose(1, 0, 2))
    posk = np.concatenate([np.arange(t * 128, (t + 1) * 128) for t in order])
    ck = np.ascontiguousarray(cos[posk].T)
    sk = np.ascontiguousarray(sin[posk].T).copy()
    sk[0:32] = -sk[0:32]
    d["cosk"], d["sink"] = ck, sk
    posq = np.concatenate([(2 * j + p) * 128 + np.arange(128) for j in range(NSLOT)])
    cq = np.ascontiguousarray(cos[posq].T)
    sq = np.ascontiguousarray(sin[posq].T).copy()
    sq[0:32] = -sq[0:32]
    d["cosq"], d["sinq"] = cq, sq
    i_ = np.arange(128)[:, None]
    c_ = np.arange(128)[None, :]
    diag = np.where(c_ <= i_, np.float32(0.0), np.float32(NEG))
    oth = np.full((128, 128), 0.0 if p == 1 else NEG, np.float32)
    d["masks"] = np.ascontiguousarray(np.concatenate([diag, oth], axis=1))
    return d


ENGS = ("sp", "pe", "act", "dve", "pool")


class _Op:
    __slots__ = ("eng", "fn", "idx", "need_inc", "deps", "dma", "dsem", "duse")

    def __init__(self, eng, fn, dma, idx):
        self.eng, self.fn, self.dma, self.idx = eng, fn, dma, idx
        self.need_inc = False
        self.deps = []
        self.dsem = -1
        self.duse = 0


NDMASEM = 24


class Bld:
    def __init__(self):
        self.eng_ops = {e: [] for e in ENGS}
        self.last_w = {}
        self.readers = {}
        self.key_last = {}
        self.dma_n = 0
        self.dma_uses = [0] * NDMASEM
        self.dma_last_op = [None] * NDMASEM

    def op(self, eng, fn, reads=(), writes=(), dma=False, after=()):
        o = _Op(eng, fn, eng if dma else None, len(self.eng_ops[eng]))
        if dma:
            o.need_inc = True
            o.dsem = self.dma_n % NDMASEM
            self.dma_n += 1
            self.dma_uses[o.dsem] += 1
            o.duse = self.dma_uses[o.dsem]
        deps, seen = [], set()

        def add(pr):
            if pr is None or pr is o or id(pr) in seen:
                return
            if pr.eng == "pe" and eng == "pe" and pr.dma is None and not dma:
                return
            seen.add(id(pr))
            deps.append(pr)
            pr.need_inc = True

        for pr_ in after:
            add(pr_)
        for t in reads:
            add(self.last_w.get(t))
        for t in writes:
            add(self.last_w.get(t))
            for r in self.readers.get(t, ()):
                add(r)
        for t in writes:
            self.last_w[t] = o
            self.readers[t] = []
        for t in reads:
            self.readers.setdefault(t, []).append(o)
        if dma and o.duse > 1:
            prev = self.dma_last_op[o.dsem]
            if prev is not None and id(prev) not in seen:
                seen.add(id(prev))
                deps.append(prev)
        if dma:
            self.dma_last_op[o.dsem] = o
        o.deps = deps
        self.eng_ops[eng].append(o)
        self.key_last[(eng, bool(dma))] = o
        return o

    def barrier(self):
        lasts = [o for o in self.key_last.values() if o.dma is None]
        lasts += [o for o in self.dma_last_op if o is not None]
        for pr in lasts:
            pr.need_inc = True
        for e in ENGS:
            o = _Op(e, None, None, len(self.eng_ops[e]))
            o.deps = lasts[:]
            self.eng_ops[e].append(o)

    def materialize(self, sems, dsems):
        cnt, val = {}, {}
        for ops in self.eng_ops.values():
            for o in ops:
                if o.dma is not None:
                    val[id(o)] = ("D", o.dsem, o.duse * 16)
                else:
                    key = o.eng
                    if o.need_inc:
                        cnt[key] = cnt.get(key, 0) + 1
                    val[id(o)] = ("E", key, cnt.get(key, 0))
        self.final_counts = {("E", k): v for k, v in cnt.items()}
        for i in range(NDMASEM):
            if self.dma_uses[i]:
                self.final_counts[("D", i)] = self.dma_uses[i] * 16

        def emit(engine, ops):
            water = {}
            for o in ops:
                for pr in o.deps:
                    kind, key, v = val[id(pr)]
                    sem = dsems[key] if kind == "D" else sems[key]
                    wk = (kind, key)
                    if water.get(wk, -1) >= v:
                        continue
                    water[wk] = v
                    engine.wait_ge(sem, v)
                if o.fn is None:
                    continue
                ins = o.fn(engine)
                if o.dma is not None:
                    ins.then_inc(dsems[o.dsem], 16)
                elif o.need_inc:
                    ins.then_inc(sems[o.eng], 1)
        return emit


_CACHE = {}
DEBUG = 0
PHASES = 99


def _build(debug=0, phases=99):
    ck_ = (debug, phases)
    if ck_ in _CACHE:
        return _CACHE[ck_]
    nc = bass.Bass("TRN2", target_bir_lowering=False, debug=False)

    ein = lambda n, s_, dt=F32: nc.dram_tensor(n, s_, dt, kind="ExternalInput").ap()
    xP = ein("xP", [128, 32, S])
    masks = ein("masks", [128, 256])
    cosq, sinq = ein("cosq", [64, NQ]), ein("sinq", [64, NQ])
    cosk, sink = ein("cosk", [64, S]), ein("sink", [64, S])
    qgain, kvgain = ein("qgain", [128, 8]), ein("kvgain", [128, 2])
    scal = ein("scal", [128, 3])
    identf = ein("ident", [128, 128])
    wqaP = ein("wqaP", [128, 8, 4096])
    wqbP = ein("wqbP", [128, 32, 1024])
    wkvaP = ein("wkvaP", [128, 32 * 320])
    wkvbkP = ein("wkvbkP", [128, 32, 128])
    wkvbvP = ein("wkvbvP", [128, 8, 1024])
    woP = ein("woP", [128, 32, 4096])
    gateP = ein("gateP", [128, NIC, 4096])
    upP = ein("upP", [128, NIC, 4096])
    downP = ein("downP", [128, 32, INTER])

    outP = nc.dram_tensor("outP", [128, 32, NQ], F32, kind="ExternalOutput").ap()
    attn_d = nc.dram_tensor("attn_d", [128, 32, NQ], BF16, kind="Internal").ap()
    o_d = nc.dram_tensor("o_d", [128, 32, NQ], F32, kind="Internal").ap()

    dbg = {}
    if debug:
        dd = lambda n, s_, dt: nc.dram_tensor(n, s_, dt, kind="ExternalOutput").ap()
        dbg["xn"] = dd("D_xn", [128, 32 * 1024], BF16)
        dbg["ckv"] = dd("D_ckv", [128, 2 * 2048], BF16)
        dbg["kpe"] = dd("D_kpe", [64, 2048], BF16)
        dbg["cqn"] = dd("D_cqn", [128, 8 * 1024], BF16)
        dbg["qh"] = dd("D_qh", [128, 1024], BF16)
        dbg["kf"] = dd("D_kf", [128, 2048], BF16)
        dbg["pb"] = dd("D_pb", [128, PTOT], BF16)
        dbg["at"] = dd("D_at", [128, 32, NQ], BF16)
        dbg["od"] = dd("D_od", [128, 32, NQ], F32)
        dbg["v4"] = dd("D_v4", [128, 8192], BF16)

    XN = nc.alloc_sbuf_tensor("XN", [128, 32 * 1024], BF16).ap()
    CQN = nc.alloc_sbuf_tensor("CQN", [128, 8 * 1024], BF16).ap()
    CKV = nc.alloc_sbuf_tensor("CKV", [128, 2 * 2048], BF16).ap()
    KPE = nc.alloc_sbuf_tensor("KPE", [64, 2048], BF16).ap()
    GQ = nc.alloc_sbuf_tensor("GQ", [128, 8], F32).ap()
    GKV = nc.alloc_sbuf_tensor("GKV", [128, 2], F32).ap()
    SCAL = nc.alloc_sbuf_tensor("SCAL", [128, 3], F32).ap()
    DSEFF = nc.alloc_sbuf_tensor("DSEFF", [128, 1], F32).ap()
    IDENT = nc.alloc_sbuf_tensor("IDENT", [128, 128], BF16).ap()
    ONESB = nc.alloc_sbuf_tensor("ONESB", [128, 1], BF16).ap()
    ONES128 = nc.alloc_sbuf_tensor("ONES128", [128, 128], BF16).ap()
    ONESF = nc.alloc_sbuf_tensor("ONESF", [1, 128], F32).ap()
    SROW = nc.alloc_sbuf_tensor("SROW", [1, 1536], F32).ap()
    BIAS8 = nc.alloc_sbuf_tensor("BIAS8", [128, 16], F32).ap()
    T8 = nc.alloc_sbuf_tensor("T8", [128, 16], F32).ap()
    LROW = nc.alloc_sbuf_tensor("LROW", [128, 48], F32).ap()
    LINV = nc.alloc_sbuf_tensor("LINV", [128, 16], F32).ap()
    K2MAX = nc.alloc_sbuf_tensor("K2MAX", [128, 2], F32).ap()
    BND = nc.alloc_sbuf_tensor("BND", [128, 2], F32).ap()
    F2G = nc.alloc_sbuf_tensor("F2G", [128, 8], F32).ap()
    KB2 = nc.alloc_sbuf_tensor("KB2", [128, 2], F32).ap()
    CT = nc.alloc_sbuf_tensor("CT", [128, 4], F32).ap()

    ARENA = nc.alloc_sbuf_tensor("ARENA", [128, (106 * 1024) // 4], F32).ap()

    def carve(off_b, part, nfree, dtype):
        assert off_b % 4 == 0
        if dtype == F32:
            v = ARENA[:, off_b // 4: off_b // 4 + nfree]
        else:
            assert nfree % 2 == 0
            v = ARENA[:, off_b // 4: off_b // 4 + nfree // 2].bitcast(BF16)
        if part != 128:
            v = v[0:part, :]
        return v

    PSA = nc.alloc_psum_tensor("PSA", [128, 2048], F32).ap()
    PSB = nc.alloc_psum_tensor("PSB", [128, 2048], F32).ap()
    bA = lambda i: f"A{i}"
    bB = lambda i: f"B{i}"

    bld = Bld()

    def mm(dst, banks, lhsT, rhs, reads, start, stop, after=()):
        return bld.op("pe", lambda e, d=dst, l=lhsT, r=rhs, s1=start, s2=stop:
                      e.matmul(d, l, r, start=s1, stop=s2),
                      reads=tuple(reads), writes=tuple(banks), after=after)

    ident_stage = carve(0, 128, 128, F32)
    bld.op("sp", lambda e: e.dma_start(out=GQ, in_=qgain), writes=("gq",), dma=True)
    bld.op("sp", lambda e: e.dma_start(out=GKV, in_=kvgain), writes=("gkv",), dma=True)
    bld.op("sp", lambda e: e.dma_start(out=SCAL, in_=scal), writes=("scal",), dma=True)
    bld.op("sp", lambda e: e.dma_start(out=ident_stage, in_=identf), writes=("idst",), dma=True)
    bld.op("dve", lambda e: e.tensor_copy(IDENT, ident_stage), reads=("idst",), writes=("ident",))
    bld.op("dve", lambda e: e.tensor_mul(DSEFF, SCAL[:, 1:2], SCAL[:, 2:3]),
           reads=("scal",), writes=("dseff",))
    bld.op("pool", lambda e: e.memset(ONESB, 1.0), writes=("onesb",))
    bld.op("pool", lambda e: e.memset(ONES128, 1.0), writes=("ones128",))
    bld.op("pool", lambda e: e.memset(ONESF, 1.0), writes=("onesf",))
    bld.op("pool", lambda e: e.memset(CT[:, 0:1], EPS_LN), writes=("constv",))
    bld.op("pool", lambda e: e.memset(CT[:, 1:2], EPS_RMS), writes=("constv",))
    bld.op("pool", lambda e: e.memset(CT[:, 2:3], 1e-20), writes=("constv",))
    bld.barrier()

    xstage = carve(0, 128, 8192, F32)
    wkva_bf = carve(32 * 1024, 128, 32 * 320, BF16)
    wkva_st = carve(52 * 1024, 128, 2560, F32)
    xbf = [carve(62 * 1024, 128, 1024, BF16), carve(62 * 1024 + 2048, 128, 1024, BF16)]
    sqr = [carve(66 * 1024, 128, 1024, BF16), carve(66 * 1024 + 2048, 128, 1024, BF16)]
    ckt = [[carve(70 * 1024 + (2 * i + k) * 1024, 64, 256, F32) for k in range(2)]
           for i in range(2)]
    rt1 = carve(74 * 1024, 64, 256, F32)
    rt2 = carve(74 * 1024 + 1024, 64, 256, F32)
    tsc = carve(76 * 1024, 128, 1024, F32)
    ckf = carve(80 * 1024, 128, 512, F32)
    rbs = carve(82 * 1024, 128, 256, F32)
    mrbs = carve(82 * 1024 + 1024, 128, 256, F32)
    sq2 = carve(84 * 1024, 128, 512, BF16)
    sxrow = carve(86 * 1024, 1, 2048, F32)

    for pc in range(4):
        bld.op("sp", lambda e, pc=pc: e.dma_start(
            out=wkva_st, in_=wkvaP[:, pc * 2560:(pc + 1) * 2560]),
            reads=("wkvast",), writes=("wkvast",), dma=True)
        bld.op("dve", lambda e, pc=pc: e.tensor_copy(
            wkva_bf[:, pc * 2560:(pc + 1) * 2560], wkva_st),
            reads=("wkvast",), writes=("wkva",))

    xs3 = xstage.rearrange("p (d t) -> p d t", t=256)
    XN3 = XN.rearrange("p (d t) -> p d t", t=1024)
    r_a = SROW[:, 0:256]
    r_b = SROW[:, 256:512]
    r_m = SROW[:, 512:768]
    r_v = SROW[:, 768:1024]
    r_r = SROW[:, 1024:1280]
    r_mr = SROW[:, 1280:1536]

    for c in range(8):
        cb = c % 2
        for half in range(2):
            htag = "xsA" if half == 0 else "xsB"
            bld.op("sp", lambda e, c=c, half=half: e.dma_start(
                out=xs3[:, half * 16:(half + 1) * 16, :],
                in_=xP[:, half * 16:(half + 1) * 16, c * 256:(c + 1) * 256]),
                reads=(htag,), writes=(htag,), dma=True)
        bld.op("sp", lambda e, c=c, cb=cb: e.dma_start(
            out=ckt[cb][0], in_=cosk[:, c * 256:(c + 1) * 256]),
            reads=(f"ckt{cb}",), writes=(f"ckt{cb}",), dma=True)
        bld.op("sp", lambda e, c=c, cb=cb: e.dma_start(
            out=ckt[cb][1], in_=sink[:, c * 256:(c + 1) * 256]),
            reads=(f"ckt{cb}",), writes=(f"ckt{cb}",), dma=True)
        for g in range(8):
            gb = g % 2
            xg = xstage[:, g * 1024:(g + 1) * 1024]
            htag = "xsA" if g < 4 else "xsB"
            bld.op("act", lambda e, xg=xg, gb=gb: e.copy(xbf[gb], xg),
                   reads=(htag,), writes=(f"xbf{gb}",))
            bld.op("act", lambda e, gb=gb: e.activation(sqr[gb], xbf[gb], AF.Square),
                   reads=(f"xbf{gb}",), writes=(f"sq{gb}",))
            for hf in range(2):
                mm(PSB[0:1, hf * 512:(hf + 1) * 512], (bB(hf),),
                   ONESB, xbf[gb][:, hf * 512:(hf + 1) * 512],
                   ("onesb", f"xbf{gb}"), g == 0, g == 7)
                mm(PSB[0:1, 1024 + hf * 512:1024 + (hf + 1) * 512], (bB(2 + hf),),
                   ONESB, sqr[gb][:, hf * 512:(hf + 1) * 512],
                   ("onesb", f"sq{gb}"), g == 0, g == 7)
        bld.op("act", lambda e: e.copy(sxrow[:, 0:1024], PSB[0:1, 0:1024]),
               reads=(bB(0), bB(1)), writes=("sxrow",))
        bld.op("act", lambda e: e.copy(sxrow[:, 1024:2048], PSB[0:1, 1024:2048]),
               reads=(bB(2), bB(3)), writes=("sxrow",))
        sx = sxrow[:, 0:1024]
        sx2 = sxrow[:, 1024:2048]
        bld.op("dve", lambda e: e.tensor_add(r_a, sx[:, 0:256], sx[:, 256:512]),
               reads=("sxrow",), writes=("ra",))
        bld.op("dve", lambda e: e.tensor_add(r_b, sx[:, 512:768], sx[:, 768:1024]),
               reads=("sxrow",), writes=("rb_",))
        bld.op("dve", lambda e: e.tensor_add(r_m, r_a, r_b), reads=("ra", "rb_"), writes=("rm",))
        bld.op("dve", lambda e: e.tensor_scalar_mul(r_m, r_m, 1.0 / D), reads=("rm",), writes=("rm",))
        bld.op("dve", lambda e: e.tensor_add(r_a, sx2[:, 0:256], sx2[:, 256:512]),
               reads=("sxrow",), writes=("ra",))
        bld.op("dve", lambda e: e.tensor_add(r_b, sx2[:, 512:768], sx2[:, 768:1024]),
               reads=("sxrow",), writes=("rb_",))
        bld.op("dve", lambda e: e.tensor_add(r_v, r_a, r_b), reads=("ra", "rb_"), writes=("rv",))
        bld.op("dve", lambda e: e.tensor_scalar_mul(r_v, r_v, 1.0 / D), reads=("rv",), writes=("rv",))
        bld.op("dve", lambda e: e.tensor_mul(r_a, r_m, r_m), reads=("rm",), writes=("ra",))
        bld.op("dve", lambda e: e.tensor_sub(r_v, r_v, r_a), reads=("rv", "ra"), writes=("rv",))
        bld.op("act", lambda e: e.activation(r_v, r_v, AF.Ln, bias=CT[0:1, 0:1]),
               reads=("rv",), writes=("rv",))
        bld.op("act", lambda e: e.activation(r_r, r_v, AF.Exp, scale=-0.5),
               reads=("rv",), writes=("rr",))
        bld.op("dve", lambda e: e.tensor_mul(r_mr, r_m, r_r), reads=("rm", "rr"), writes=("rmr",))
        mm(PSA[:, 0:256], (bA(0),), ONESF, r_r, ("onesf", "rr"), True, True)
        mm(PSA[:, 512:768], (bA(1),), ONESF, r_mr, ("onesf", "rmr"), True, True)
        bld.op("act", lambda e: e.copy(rbs, PSA[:, 0:256]), reads=(bA(0),), writes=("rbs",))
        bld.op("act", lambda e: e.copy(mrbs, PSA[:, 512:768]), reads=(bA(1),), writes=("mrbs",))
        for g in range(8):
            gb = g % 2
            xg3 = xs3[:, 4 * g:4 * (g + 1), :]
            xnr3 = xbf[gb].rearrange("p (d t) -> p d t", t=256)
            t3 = tsc.rearrange("p (d t) -> p d t", t=256)
            rb_b = rbs.unsqueeze(1).broadcast_to((128, 4, 256))
            mrb_b = mrbs.unsqueeze(1).broadcast_to((128, 4, 256))
            htag = "xsA" if g < 4 else "xsB"
            bld.op("pool", lambda e, xg3=xg3, rb_b=rb_b, t3=t3:
                   e.tensor_tensor(t3, xg3, rb_b, ALU.mult),
                   reads=(htag, "rbs"), writes=("tsc",))
            bld.op("dve", lambda e, xnr3=xnr3, mrb_b=mrb_b, t3=t3:
                   e.tensor_sub(xnr3, t3, mrb_b),
                   reads=("tsc", "mrbs"), writes=(f"xbf{gb}",))
            bld.op("dve", lambda e, g=g, c=c, xnr3=xnr3:
                   e.tensor_copy(XN3[:, 4 * g:4 * (g + 1), c * 128:(c + 1) * 128],
                                 xnr3[:, :, 0:128]),
                   reads=(f"xbf{gb}",), writes=("xn",))
            for dd_ in range(4):
                dc = 4 * g + dd_
                rhs = xnr3[:, dd_, :]
                mm(PSA[:, 1024:1280], (bA(2),),
                   wkva_bf[:, dc * 320:dc * 320 + 128], rhs,
                   ("wkva", f"xbf{gb}"), dc == 0, dc == 31)
                mm(PSA[:, 1536:1792], (bA(3),),
                   wkva_bf[:, dc * 320 + 128:dc * 320 + 256], rhs,
                   ("wkva", f"xbf{gb}"), dc == 0, dc == 31)
                mm(PSB[0:64, 512:768], (bB(1),),
                   wkva_bf[:, dc * 320 + 256:dc * 320 + 320], rhs,
                   ("wkva", f"xbf{gb}"), dc == 0, dc == 31)
        bld.op("act", lambda e: e.activation(sq2[:, 0:256], PSA[:, 1024:1280], AF.Square),
               reads=(bA(2),), writes=("sq2",))
        bld.op("act", lambda e: e.activation(sq2[:, 256:512], PSA[:, 1536:1792], AF.Square),
               reads=(bA(3),), writes=("sq2",))
        mm(PSB[0:1, 1024:1280], (bB(2),), ONESB, sq2[:, 0:256], ("onesb", "sq2"), True, False)
        mm(PSB[0:1, 1024:1280], (bB(2),), ONESB, sq2[:, 256:512], ("onesb", "sq2"), False, True)
        r_k = SROW[:, 0:256]
        bld.op("act", lambda e: e.activation(r_k, PSB[0:1, 1024:1280], AF.Ln,
                                             bias=CT[0:1, 1:2], scale=1.0 / KVR),
               reads=(bB(2), "ra"), writes=("ra",))
        bld.op("act", lambda e: e.activation(r_k, r_k, AF.Exp, scale=-0.5),
               reads=("ra",), writes=("ra",))
        mm(PSB[:, 1536:1792], (bB(3),), ONESF, r_k, ("onesf", "ra"), True, True)
        for l in range(2):
            src = PSA[:, 1024:1280] if l == 0 else PSA[:, 1536:1792]
            btag = bA(2) if l == 0 else bA(3)
            bld.op("act", lambda e, src=src: e.copy(ckf[:, 0:256], src),
                   reads=(btag,), writes=("ckf",))
            bld.op("dve", lambda e: e.tensor_tensor(ckf[:, 256:512], ckf[:, 0:256],
                                                    PSB[:, 1536:1792], ALU.mult),
                   reads=("ckf", bB(3)), writes=("ckf2",))
            bld.op("dve", lambda e, l=l, c=c: e.tensor_scalar(
                CKV[:, l * 2048 + c * 256:l * 2048 + (c + 1) * 256],
                ckf[:, 256:512], GKV[:, l:l + 1], None, ALU.mult),
                reads=("ckf2", "gkv"), writes=("ckv",))
        kp = PSB[0:64, 512:768]
        bld.op("dve", lambda e, cb=cb: e.tensor_tensor(rt1, kp, ckt[cb][0], ALU.mult),
               reads=(bB(1), f"ckt{cb}"), writes=("rt1",))
        bld.op("dve", lambda e, cb=cb: e.tensor_tensor(rt2[0:32, :], kp[32:64, :],
                                                       ckt[cb][1][0:32, :], ALU.mult),
               reads=(bB(1), f"ckt{cb}"), writes=("rt2",))
        bld.op("dve", lambda e, cb=cb: e.tensor_tensor(rt2[32:64, :], kp[0:32, :],
                                                       ckt[cb][1][32:64, :], ALU.mult),
               reads=(bB(1), f"ckt{cb}"), writes=("rt2",))
        bld.op("dve", lambda e, c=c: e.tensor_add(KPE[:, c * 256:(c + 1) * 256], rt1, rt2),
               reads=("rt1", "rt2"), writes=("kpe",))

    SQKP = carve(94 * 1024, 128, 2048, BF16)
    SQCK = carve(98 * 1024, 128, 4096, BF16)
    bld.op("pool", lambda e: e.tensor_tensor(SQKP[0:64, :], KPE, KPE, ALU.mult),
           reads=("kpe",), writes=("sqkp",))
    bld.op("pool", lambda e: e.tensor_tensor(SQCK, CKV, CKV, ALU.mult),
           reads=("ckv",), writes=("sqck",))
    for n in range(4):
        mm(PSB[0:1, n * 512:(n + 1) * 512], (bB(n),), ONESB[0:64, :],
           SQKP[0:64, n * 512:(n + 1) * 512], ("onesb", "sqkp"), True, True)
        for l in range(2):
            mm(PSA[0:1, n * 512:(n + 1) * 512], (bA(n),), ONESB,
               SQCK[:, l * 2048 + n * 512:l * 2048 + (n + 1) * 512],
               ("onesb", "sqck"), l == 0, l == 1)
    bld.op("dve", lambda e: e.reduce_max(SROW[:, 0:1], PSB[0:1, 0:2048], AXX),
           reads=(bB(0), bB(1), bB(2), bB(3)), writes=("brow",))
    bld.op("dve", lambda e: e.reduce_max(SROW[:, 1:2], PSA[0:1, 0:2048], AXX),
           reads=(bA(0), bA(1), bA(2), bA(3)), writes=("brow",))
    mm(PSB[:, 0:2], (bB(0),), ONESF, SROW[:, 0:2], ("onesf", "brow"), True, True)
    bld.op("act", lambda e: e.copy(BND, PSB[:, 0:2]), reads=(bB(0),), writes=("bnd",))

    if debug:
        bld.op("act", lambda e: e.dma_start(out=dbg["xn"], in_=XN),
               reads=("xn",), dma=True)
        bld.op("act", lambda e: e.dma_start(out=dbg["ckv"], in_=CKV),
               reads=("ckv",), dma=True)
        bld.op("act", lambda e: e.dma_start(out=dbg["kpe"], in_=KPE),
               reads=("kpe",), dma=True)
    bld.barrier()

    if phases >= 2:
        _p2(nc, bld, locals())
    if phases >= 3:
        _p3(nc, bld, locals())
    if phases >= 4:
        _p4(nc, bld, locals())
    if phases >= 5:
        _p5(nc, bld, locals())

    from contextlib import ExitStack
    es = ExitStack()
    sems = {}
    for e in ENGS:
        sems[e] = es.enter_context(nc.semaphore(f"m_{e}"))
    dsems = [es.enter_context(nc.semaphore(f"d_{i}")) for i in range(NDMASEM)]
    with nc.Block() as blk:
        emit = bld.materialize(sems, dsems)
        blk.sync(lambda eng: emit(eng, bld.eng_ops["sp"]))
        blk.tensor(lambda eng: emit(eng, bld.eng_ops["pe"]))
        blk.scalar(lambda eng: emit(eng, bld.eng_ops["act"]))
        blk.vector(lambda eng: emit(eng, bld.eng_ops["dve"]))
        blk.gpsimd(lambda eng: emit(eng, bld.eng_ops["pool"]))
    fc = bld.final_counts
    with nc.Block() as blk2:
        def fin(eng):
            for key, v in fc.items():
                if key[0] == "D":
                    eng.wait_ge(dsems[key[1]], v)
                else:
                    eng.wait_ge(sems[key[1]], v)
        blk2.sync(fin)
    es.close()
    _CACHE[ck_] = nc
    return nc


def _p2(nc, bld, env):
    carve, mm, PSA, PSB, bA, bB = (env["carve"], env["mm"], env["PSA"], env["PSB"],
                                   env["bA"], env["bB"])
    wqaP, XN, CQN, GQ, ONESB, ONESF, SROW = (env["wqaP"], env["XN"], env["CQN"],
                                             env["GQ"], env["ONESB"], env["ONESF"],
                                             env["SROW"])
    CT = env["CT"]
    debug, dbg = env["debug"], env["dbg"]
    wqa_st = carve(0, 128, 4096, F32)
    wqa_bf = [carve(16 * 1024, 128, 4096, BF16), carve(24 * 1024, 128, 4096, BF16)]
    cqf = carve(32 * 1024, 128, 8192, F32)
    sqc = carve(64 * 1024, 128, 1024, BF16)

    def _wqa_fetch(oc):
        ob = oc % 2
        bld.op("sp", lambda e, oc=oc: e.dma_start(out=wqa_st, in_=wqaP[:, oc, :]),
               reads=("wqast",), writes=("wqast",), dma=True)
        bld.op("act", lambda e, ob=ob: e.copy(wqa_bf[ob], wqa_st),
               reads=("wqast",), writes=(f"wqab{ob}",))
    _wqa_fetch(0)
    for oc in range(8):
        ob = oc % 2
        if oc + 1 < 8:
            _wqa_fetch(oc + 1)
        ps = PSA[:, 0:1024] if ob == 0 else PSA[:, 1024:2048]
        pbt = (bA(0), bA(1)) if ob == 0 else (bA(2), bA(3))
        for dc in range(32):
            for hf in range(2):
                mm(ps[:, hf * 512:(hf + 1) * 512], (pbt[hf],),
                   wqa_bf[ob][:, dc * 128:(dc + 1) * 128],
                   XN[:, dc * 1024 + hf * 512:dc * 1024 + (hf + 1) * 512],
                   (f"wqab{ob}", "xn"), dc == 0, dc == 31)
        bld.op("act", lambda e, ps=ps: e.activation(sqc, ps, AF.Square),
               reads=pbt, writes=("sqc",))
        for hf in range(2):
            mm(PSB[0:1, hf * 512:(hf + 1) * 512], (bB(hf),),
               ONESB, sqc[:, hf * 512:(hf + 1) * 512],
               ("onesb", "sqc"), oc == 0, oc == 7)
        bld.op("act", lambda e, ps=ps, oc=oc: e.copy(cqf[:, oc * 1024:(oc + 1) * 1024], ps),
               reads=pbt, writes=("cqf",))
    for hf in range(2):
        bld.op("act", lambda e, hf=hf: e.activation(
            SROW[:, hf * 512:(hf + 1) * 512], PSB[0:1, hf * 512:(hf + 1) * 512],
            AF.Ln, bias=CT[0:1, 1:2], scale=1.0 / 1024.0),
            reads=(bB(hf),), writes=("qrow",))
        bld.op("act", lambda e, hf=hf: e.activation(
            SROW[:, hf * 512:(hf + 1) * 512], SROW[:, hf * 512:(hf + 1) * 512],
            AF.Exp, scale=-0.5), reads=("qrow",), writes=("qrow",))
        mm(PSB[:, 1024 + hf * 512:1024 + (hf + 1) * 512], (bB(2 + hf),),
           ONESF, SROW[:, hf * 512:(hf + 1) * 512], ("onesf", "qrow"), True, True)
    rqb = carve(66 * 1024, 128, 1024, F32)
    bld.op("act", lambda e: e.copy(rqb, PSB[:, 1024:2048]),
           reads=(bB(2), bB(3)), writes=("rqb",))
    for oc in range(8):
        bld.op("pool", lambda e, oc=oc: e.tensor_tensor(
            cqf[:, oc * 1024:(oc + 1) * 1024], cqf[:, oc * 1024:(oc + 1) * 1024],
            rqb, ALU.mult),
            reads=("cqf", "rqb"), writes=(f"cqfn{oc}",))
        bld.op("dve", lambda e, oc=oc: e.tensor_scalar(
            CQN[:, oc * 1024:(oc + 1) * 1024], cqf[:, oc * 1024:(oc + 1) * 1024],
            GQ[:, oc:oc + 1], None, ALU.mult),
            reads=(f"cqfn{oc}", "gq"), writes=("cqn",))
    if debug:
        bld.op("act", lambda e: e.dma_start(out=dbg["cqn"], in_=CQN),
               reads=("cqn",), dma=True)


def _p3(nc, bld, env):
    carve, mm, PSA, PSB, bA, bB = (env["carve"], env["mm"], env["PSA"], env["PSB"],
                                   env["bA"], env["bB"])
    (masks, cosq, sinq, wqbP, wkvbkP, wkvbvP, attn_d) = (
        env["masks"], env["cosq"], env["sinq"], env["wqbP"],
        env["wkvbkP"], env["wkvbvP"], env["attn_d"])
    (XN, CQN, CKV, KPE, IDENT, ONESB, ONES128, BIAS8, T8, LROW, LINV) = (
        env["XN"], env["CQN"], env["CKV"], env["KPE"], env["IDENT"], env["ONESB"],
        env["ONES128"], env["BIAS8"], env["T8"], env["LROW"], env["LINV"])
    BND, F2G, KB2, CT = env["BND"], env["F2G"], env["KB2"], env["CT"]
    debug, dbg = env["debug"], env["dbg"]

    MASKS = carve(0, 128, 256, F32)
    COSQ = carve(2 * 1024, 64, 1024, F32)
    SINQ = carve(6 * 1024, 64, 1024, F32)
    QH = [carve(10 * 1024, 128, 1024, BF16), carve(12 * 1024, 128, 1024, BF16)]
    KF = [carve(14 * 1024, 128, 2048, BF16), carve(18 * 1024, 128, 2048, BF16)]
    V4 = carve(22 * 1024, 128, 8192, BF16)
    V43 = V4.rearrange("p (k v) -> p k v", v=512)
    PBUF = carve(38 * 1024, 128, PTOT, BF16)
    PT = carve(56 * 1024, 128, 8192, BF16)
    PT4 = PT.rearrange("p (k j c) -> p k j c", j=8, c=128)
    RTA = carve(72 * 1024, 64, 1024, F32)
    RTB = carve(76 * 1024, 64, 1024, F32)
    SQKV = carve(76 * 1024, 128, 512, BF16)
    Q2K = carve(72 * 1024, 128, 2048, BF16)
    WQB_ST = carve(80 * 1024, 128, 1024, F32)
    WQB_BF = [carve(84 * 1024, 128, 1024, BF16), carve(86 * 1024, 128, 1024, BF16)]
    KVK_ST = carve(88 * 1024, 128, 512, F32)
    KVV_ST = carve(90 * 1024, 128, 1024, F32)
    KVK_BF = [carve(94 * 1024, 128, 512, BF16), carve(95 * 1024, 128, 512, BF16)]
    KVV_BF = [carve(96 * 1024, 128, 1024, BF16), carve(98 * 1024, 128, 1024, BF16)]
    ATTS = carve(100 * 1024, 128, 1024, BF16)
    QSTG = carve(102 * 1024, 128, 1024, F32)

    units = []
    for j in range(NSLOT):
        n = LJ[j] * 128
        for k0 in range(0, n, 512):
            units.append((j, k0, min(k0 + 512, n)))
    ufirst = {}
    for ui, (j, _, _k) in enumerate(units):
        ufirst.setdefault(j, ui)

    def _wqb_fetch(h):
        qb = h % 2
        bld.op("sp", lambda e, h=h: e.dma_start(out=WQB_ST, in_=wqbP[:, h, :]),
               reads=("wqbst",), writes=("wqbst",), dma=True)
        bld.op("act", lambda e, qb=qb: e.copy(WQB_BF[qb], WQB_ST),
               reads=("wqbst",), writes=(f"wqbb{qb}",))

    def _kv_fetch(g):
        gb = g % 2
        bld.op("sp", lambda e, g=g: e.dma_start(out=KVK_ST, in_=wkvbkP[:, 4 * g:4 * g + 4, :]),
               reads=("kvkst",), writes=("kvkst",), dma=True)
        bld.op("sp", lambda e, g=g: e.dma_start(out=KVV_ST, in_=wkvbvP[:, g, :]),
               reads=("kvvst",), writes=("kvvst",), dma=True)
        bld.op("pool", lambda e, gb=gb: e.tensor_copy(KVK_BF[gb], KVK_ST),
               reads=("kvkst",), writes=(f"kvkb{gb}",))
        bld.op("pool", lambda e, gb=gb: e.tensor_copy(KVV_BF[gb], KVV_ST),
               reads=("kvvst",), writes=(f"kvvb{gb}",))

    def _vgen(g):
        gb = g % 2
        if g + 1 < 8:
            _kv_fetch(g + 1)
        bld.op("pool", lambda e, gb=gb: e.tensor_tensor(SQKV, KVK_BF[gb], KVK_BF[gb],
                                                        ALU.mult),
               reads=(f"kvkb{gb}",), writes=("rtb",))
        mm(PSB[:, 1024:1536], (bB(2),), ONES128, SQKV, ("ones128", "rtb"), True, True)
        bld.op("dve", lambda e, gb=gb: e.reduce_sum(
            F2G[:, 4 * gb:4 * gb + 4],
            PSB[:, 1024:1536].rearrange("p (h c) -> p h c", c=128), AXX),
            reads=(bB(2),), writes=(f"f2g{gb}",))
        for kt in range(16):
            vb = kt % 2
            for l in range(2):
                mm(PSB[:, vb * 512:(vb + 1) * 512], (bB(vb),),
                   CKV[:, l * 2048 + kt * 128:l * 2048 + (kt + 1) * 128],
                   KVV_BF[gb][:, l * 512:(l + 1) * 512],
                   ("ckv", f"kvvb{gb}"), l == 0, l == 1)
            bld.op("act", lambda e, kt=kt, vb=vb: e.copy(
                V43[:, kt, :], PSB[:, vb * 512:(vb + 1) * 512]),
                reads=(bB(vb),), writes=("v4",))
        if debug and g == 0:
            bld.op("act", lambda e: e.dma_start(out=dbg["v4"], in_=V4),
                   reads=("v4",), dma=True)

    def _gen_a(h):
        g, hh = h // 4, h % 4
        gb = g % 2
        qb = kb = h % 2
        if h + 1 < 32:
            _wqb_fetch(h + 1)
        for dc in range(8):
            for hf in range(2):
                mm(PSA[:, hf * 512:(hf + 1) * 512], (bA(hf),),
                   WQB_BF[qb][:, dc * 128:(dc + 1) * 128],
                   CQN[:, dc * 1024 + hf * 512:dc * 1024 + (hf + 1) * 512],
                   (f"wqbb{qb}", "cqn"), dc == 0, dc == 7)
        bld.op("act", lambda e: e.copy(QSTG, PSA[:, 0:1024]),
               reads=(bA(0), bA(1)), writes=("qstg",))
        for n in range(4):
            pb_ = 2 + (n % 2)
            for l in range(2):
                mm(PSB[0:64, pb_ * 512:(pb_ + 1) * 512], (bB(pb_),),
                   KVK_BF[gb][:, hh * 128 + l * 64:hh * 128 + l * 64 + 64],
                   CKV[:, l * 2048 + n * 512:l * 2048 + (n + 1) * 512],
                   (f"kvkb{gb}", "ckv"), l == 0, l == 1)
            bld.op("dve", lambda e, n=n, kb=kb, pb_=pb_: e.tensor_copy(
                KF[kb][64:128, n * 512:(n + 1) * 512],
                PSB[0:64, pb_ * 512:(pb_ + 1) * 512]),
                reads=(bB(pb_),), writes=(f"kf{kb}",))
    def _gen_b1(h):
        g, hh = h // 4, h % 4
        gb = g % 2
        qb = hb = h % 2
        bld.op("act", lambda e: e.copy(RTA[0:32, :], QSTG[32:64, :]),
               reads=("qstg",), writes=("rta", "q2k"))
        bld.op("act", lambda e: e.copy(RTA[32:64, :], QSTG[0:32, :]),
               reads=("qstg",), writes=("rta", "q2k"))
        bld.op("dve", lambda e: e.tensor_tensor(RTB, RTA, SINQ, ALU.mult),
               reads=("rta", "sinq"), writes=("rtb",))
        bld.op("dve", lambda e: e.tensor_tensor(RTA, QSTG[0:64, :], COSQ, ALU.mult),
               reads=("qstg", "cosq", "rtb"), writes=("rta", "q2k"))
        bld.op("dve", lambda e, qb=qb: e.tensor_add(QH[qb][0:64, :], RTA, RTB),
               reads=("rta", "rtb"), writes=(f"qh{qb}",))
        bld.op("act", lambda e, qb=qb: e.copy(QH[qb][64:128, :], QSTG[64:128, :]),
               reads=("qstg",), writes=(f"qh{qb}",))
        bld.op("act", lambda e: e.activation(Q2K[:, 0:1024], QSTG, AF.Square),
               reads=("qstg",), writes=("q2k", "rta"))

    def _gen_b2(h):
        g, hh = h // 4, h % 4
        gb = g % 2
        qb = hb = h % 2
        for j in range(8):
            mm(PSB[:, 1536 + 8 * hb + j:1537 + 8 * hb + j], (bB(3),),
               Q2K[:, j * 128:(j + 1) * 128], ONESB,
               ("q2k", "onesb"), True, True)
        bld.op("dve", lambda e, gb=gb, hh=hh, hb=hb: e.tensor_tensor(
            KB2[:, hb:hb + 1], F2G[:, 4 * gb + hh:4 * gb + hh + 1], BND[:, 1:2],
            ALU.mult),
            reads=(f"f2g{gb}", "bnd"), writes=(f"kb2{hb}",))
        bld.op("dve", lambda e, hb=hb: e.tensor_add(
            KB2[:, hb:hb + 1], KB2[:, hb:hb + 1], BND[:, 0:1]),
            reads=(f"kb2{hb}", "bnd"), writes=(f"kb2{hb}",))
        t8v = T8[:, 8 * hb:8 * hb + 8]
        bld.op("dve", lambda e, hb=hb, t8v=t8v: e.tensor_scalar(
            t8v, PSB[:, 1536 + 8 * hb:1544 + 8 * hb], KB2[:, hb:hb + 1], None,
            ALU.mult),
            reads=(bB(3), f"kb2{hb}"), writes=(f"t8{hb}",))
        bld.op("act", lambda e, t8v=t8v: e.activation(t8v, t8v, AF.Ln, bias=CT[:, 2:3]),
               reads=(f"t8{hb}",), writes=(f"t8{hb}",))
        bld.op("act", lambda e, t8v=t8v: e.activation(t8v, t8v, AF.Exp, scale=0.5),
               reads=(f"t8{hb}",), writes=(f"t8{hb}",))
        bld.op("dve", lambda e, hb=hb, t8v=t8v: e.tensor_scalar(
            BIAS8[:, 8 * hb:8 * hb + 8], t8v, -SOFTMAX_SCALE, MARGIN,
            ALU.mult, ALU.add),
            reads=(f"t8{hb}",), writes=(f"bias8{hb}",))

    def _att_units(h):
        g, hh = h // 4, h % 4
        qb = kb = hb = h % 2
        pending = []
        for ui, (j, klo, khi) in enumerate(units):
            rg = ui % 4
            reg = PSA[:, rg * 512:(rg + 1) * 512]
            rtags = (bA(rg),)
            n = khi - klo
            mm(reg[:, 0:n], rtags,
               QH[qb][:, j * 128:(j + 1) * 128],
               KF[kb][:, klo:khi],
               (f"qh{qb}", f"kf{kb}"), True, True)
            while len(pending) > 2:
                pending.pop(0)()
            last = (khi == LJ[j] * 128)
            if last:
                moff = n - 256
                bld.op("dve", lambda e, reg=reg, moff=moff: e.tensor_tensor(
                    reg[:, moff:moff + 256], reg[:, moff:moff + 256],
                    MASKS, ALU.add),
                    reads=rtags + ("masks",), writes=rtags)
            uoff = OFFJ[j] + klo
            uix = 24 * hb + ui
            bld.op("act", lambda e, reg=reg, n=n, uoff=uoff, j=j, uix=uix, hb=hb:
                   e.activation(PBUF[:, uoff:uoff + n], reg[:, 0:n], AF.Exp,
                                bias=BIAS8[:, 8 * hb + j:8 * hb + j + 1],
                                scale=SOFTMAX_SCALE,
                                accum_out=LROW[:, uix:uix + 1]),
                   reads=rtags + (f"bias8{hb}",), writes=(f"pb{j}", f"lrow{uix}"))
            if last:
                u0 = 24 * hb + ufirst[j]
                for ux in range(u0, uix):
                    bld.op("dve", lambda e, uix=uix, ux=ux: e.tensor_add(
                        LROW[:, uix:uix + 1], LROW[:, uix:uix + 1],
                        LROW[:, ux:ux + 1]),
                        reads=(f"lrow{uix}", f"lrow{ux}"), writes=(f"lrow{uix}",))
                jx = 8 * hb + j
                bld.op("dve", lambda e, jx=jx, uix=uix: e.reciprocal(
                    LINV[:, jx:jx + 1], LROW[:, uix:uix + 1]),
                    reads=(f"lrow{uix}",), writes=(f"linv{jx}",))
                bld.op("dve", lambda e, j=j, jx=jx: e.tensor_single_scalar(
                    PBUF[:, OFFJ[j]:OFFJ[j] + LJ[j] * 128],
                    PBUF[:, OFFJ[j]:OFFJ[j] + LJ[j] * 128],
                    LINV[:, jx:jx + 1], ALU.mult),
                    reads=(f"pb{j}", f"linv{jx}"), writes=(f"pb{j}",))

                def _mk_tr(j=j):
                    def go():
                        c0 = min(LJ[j], 8)
                        ptv = PSB[:, 0:512].bitcast(BF16)
                        for kt in range(c0):
                            bld.op("pe", lambda e, j=j, kt=kt, ptv=ptv: e.transpose(
                                ptv[:, kt * 128:(kt + 1) * 128],
                                PBUF[:, OFFJ[j] + kt * 128:OFFJ[j] + (kt + 1) * 128],
                                IDENT),
                                reads=(f"pb{j}", "ident"), writes=(bB(0),))
                        ev_eng = "dve" if (j % 2 == 0) else "act"

                        def _ev0(e, j=j, c0=c0, ptv=ptv):
                            s_ = ptv.rearrange("p (k c) -> p k c", c=128)[:, 0:c0, :]
                            d_ = PT4[:, 0:c0, j, :]
                            return (e.tensor_copy(d_, s_) if hasattr(e, "tensor_copy")
                                    else e.copy(d_, s_))
                        bld.op(ev_eng, _ev0, reads=(bB(0),), writes=(f"ptg{j}",))
                    return go
                pending.append(_mk_tr())
        for fn_ in pending:
            fn_()

    def _att_av(h):
        g, hh = h // 4, h % 4
        qb = kb = hb = h % 2
        for kt in range(8):
            j0 = kt // 2
            rd = tuple(f"ptg{j}" for j in range(j0, 8)) + ("v4",)
            lhs = V43[:, kt, hh * 128:(hh + 1) * 128]
            base = kt * 1024
            if j0 * 128 < 512:
                mm(PSB[:, 1024 + j0 * 128:1536], (bB(2),),
                   lhs, PT[:, base + j0 * 128:base + 512], rd, kt == 0, kt == 7)
            mm(PSB[:, 1536:2048], (bB(3),),
               lhs, PT[:, base + 512:base + 1024], rd, kt == 0, False)
        for j in range(4, 8):
            c1 = LJ[j] - 8
            ptv = PSB[:, 0:512].bitcast(BF16)
            for kt in range(c1):
                bld.op("pe", lambda e, j=j, kt=kt, ptv=ptv: e.transpose(
                    ptv[:, kt * 128:(kt + 1) * 128],
                    PBUF[:, OFFJ[j] + (8 + kt) * 128:OFFJ[j] + (9 + kt) * 128],
                    IDENT),
                    reads=(f"pb{j}", "ident"), writes=(bB(0),))
            ev_eng = "dve" if (j % 2 == 0) else "act"

            def _ev1(e, j=j, c1=c1, ptv=ptv):
                s_ = ptv.rearrange("p (k c) -> p k c", c=128)[:, 0:c1, :]
                d_ = PT4[:, 0:c1, j, :]
                return (e.tensor_copy(d_, s_) if hasattr(e, "tensor_copy")
                        else e.copy(d_, s_))
            bld.op(ev_eng, _ev1, reads=(bB(0),), writes=(f"ptg{j}",))
        for kt in range(8, 16):
            j0 = kt // 2
            rd = tuple(f"ptg{j}" for j in range(j0, 8)) + ("v4",)
            lhs = V43[:, kt, hh * 128:(hh + 1) * 128]
            base = (kt - 8) * 1024
            mm(PSB[:, 1024 + j0 * 128:2048], (bB(3),),
               lhs, PT[:, base + j0 * 128:base + 1024], rd, False, kt == 15)
        bld.op("dve", lambda e: e.tensor_copy(ATTS, PSB[:, 1024:2048]),
               reads=(bB(2), bB(3)), writes=("atts",))
        bld.op("act", lambda e, h=h: e.dma_start(out=attn_d[:, h, :], in_=ATTS),
               reads=("atts",), writes=(f"attnd{h}",), dma=True)
        if debug and h == 0:
            bld.op("act", lambda e: e.dma_start(out=dbg["qh"], in_=QH[0]),
                   reads=("qh0",), dma=True)
            bld.op("act", lambda e: e.dma_start(out=dbg["kf"], in_=KF[0]),
                   reads=("kf0",), dma=True)
            bld.op("act", lambda e: e.dma_start(out=dbg["pb"], in_=PBUF),
                   reads=tuple(f"pb{j}" for j in range(8)), dma=True)

    _kv_fetch(0)
    _wqb_fetch(0)
    bld.barrier()
    bld.op("sp", lambda e: e.dma_start(out=MASKS, in_=masks), writes=("masks",), dma=True)
    bld.op("sp", lambda e: e.dma_start(out=COSQ, in_=cosq), writes=("cosq",), dma=True)
    bld.op("sp", lambda e: e.dma_start(out=SINQ, in_=sinq), writes=("sinq",), dma=True)
    for kb in range(2):
        bld.op("pool", lambda e, kb=kb: e.tensor_copy(KF[kb][0:64, :], KPE),
               reads=("kpe",), writes=(f"kf{kb}",))
    _vgen(0)
    _gen_a(0)
    _gen_b1(0)
    _gen_b2(0)
    for h in range(32):
        nh = h + 1
        boundary = (nh < 32 and nh % 4 == 0)
        if nh < 32 and not boundary:
            _gen_a(nh)
        _att_units(h)
        if nh < 32 and not boundary:
            _gen_b1(nh)
        _att_av(h)
        if nh < 32:
            if boundary:
                _vgen(nh // 4)
                _gen_a(nh)
                _gen_b1(nh)
            _gen_b2(nh)
    if debug:
        bld.op("sp", lambda e: e.dma_start(out=dbg["at"], in_=attn_d),
               reads=tuple(f"attnd{h}" for h in range(32)), dma=True)
    bld.barrier()


def _p4(nc, bld, env):
    carve, mm, PSA, PSB, bA, bB = (env["carve"], env["mm"], env["PSA"], env["PSB"],
                                   env["bA"], env["bB"])
    woP, attn_d, o_d = env["woP"], env["attn_d"], env["o_d"]
    debug, dbg = env["debug"], env["dbg"]
    ATTN = carve(0, 128, 32 * 1024, BF16)
    at3 = ATTN.rearrange("p (h t) -> p h t", t=1024)
    WO_ST = [carve(64 * 1024, 128, 2048, F32), carve(72 * 1024, 128, 2048, F32)]
    WO_BF = [carve(80 * 1024, 128, 4096, BF16), carve(88 * 1024, 128, 4096, BF16)]
    OSTG = carve(96 * 1024, 128, 1024, F32)

    for q in range(4):
        bld.op("sp", lambda e, q=q: e.dma_start(
            out=at3[:, q * 8:(q + 1) * 8, :], in_=attn_d[:, q * 8:(q + 1) * 8, :]),
            reads=tuple(f"attnd{h2}" for h2 in range(q * 8, q * 8 + 8)),
            writes=("attn",), dma=True)
    def _wo_fetch(oc):
        ob = oc % 2
        for hf in range(2):
            dq = "sp" if hf == 0 else "act"
            bld.op(dq, lambda e, oc=oc, hf=hf: e.dma_start(
                out=WO_ST[hf], in_=woP[:, oc, hf * 2048:(hf + 1) * 2048]),
                reads=(f"wost{hf}",), writes=(f"wost{hf}",), dma=True)
            eng = "act" if hf == 0 else "dve"
            def _woc(e, hf=hf, ob=ob):
                d_ = WO_BF[ob][:, hf * 2048:(hf + 1) * 2048]
                return e.copy(d_, WO_ST[hf]) if hf == 0 else e.tensor_copy(d_, WO_ST[hf])
            bld.op(eng, _woc, reads=(f"wost{hf}",), writes=(f"wob{ob}",))
    _wo_fetch(0)
    for oc in range(32):
        ob = oc % 2
        if oc + 1 < 32:
            _wo_fetch(oc + 1)
        ps = PSA[:, 0:1024] if ob == 0 else PSA[:, 1024:2048]
        pbt = (bA(0), bA(1)) if ob == 0 else (bA(2), bA(3))
        for hv in range(32):
            for hf in range(2):
                mm(ps[:, hf * 512:(hf + 1) * 512], (pbt[hf],),
                   WO_BF[ob][:, hv * 128:(hv + 1) * 128],
                   at3[:, hv, hf * 512:(hf + 1) * 512],
                   (f"wob{ob}", "attn"), hv == 0, hv == 31)
        bld.op("act", lambda e, ps=ps: e.copy(OSTG, ps), reads=pbt, writes=("ostg",))
        bld.op("act", lambda e, oc=oc: e.dma_start(out=o_d[:, oc, :], in_=OSTG),
               reads=("ostg",), writes=(f"od{oc}",), dma=True)
    if debug:
        bld.op("sp", lambda e: e.dma_start(out=dbg["od"], in_=o_d),
               reads=tuple(f"od{oc}" for oc in range(32)), dma=True)
    bld.barrier()


def _p5(nc, bld, env):
    carve, mm, PSA, PSB, bA, bB = (env["carve"], env["mm"], env["PSA"], env["PSB"],
                                   env["bA"], env["bB"])
    (xP, gateP, upP, downP, o_d, outP, XN, SCAL, DSEFF) = (
        env["xP"], env["gateP"], env["upP"], env["downP"], env["o_d"], env["outP"],
        env["XN"], env["SCAL"], env["DSEFF"])
    ICMAX = max(ICB_SPLIT)
    INTERB = carve(0, 128, ICMAX * 512, BF16)
    G_ST = carve(30 * 1024, 128, 4096, F32)
    U_ST = carve(46 * 1024, 128, 4096, F32)
    G_BF = [carve(62 * 1024, 128, 4096, BF16), carve(70 * 1024, 128, 4096, BF16)]
    U_BF = [carve(78 * 1024, 128, 4096, BF16), carve(86 * 1024, 128, 4096, BF16)]
    SILU = carve(94 * 1024, 128, 512, F32)
    D_ST = [carve(30 * 1024, 128, ICMAX * 128, F32),
            carve(45 * 1024, 128, ICMAX * 128, F32)]
    D_BF = [carve(60 * 1024, 128, ICMAX * 128, BF16),
            carve(68 * 1024, 128, ICMAX * 128, BF16)]
    XOC = [carve(76 * 1024, 128, 512, F32), carve(78 * 1024, 128, 512, F32)]
    OOC = [carve(80 * 1024, 128, 512, F32), carve(82 * 1024, 128, 512, F32)]
    RIN = [carve(84 * 1024, 128, 512, F32), carve(86 * 1024, 128, 512, F32)]
    OUTS = [carve(88 * 1024, 128, 512, F32), carve(90 * 1024, 128, 512, F32)]

    def _gu_fetch(ic, ib, after_dma=(), after_cast=()):
        o1 = bld.op("sp", lambda e, ic=ic: e.dma_start(out=G_ST, in_=gateP[:, ic, :]),
                    reads=("gst",), writes=("gst",), dma=True, after=after_dma)
        o2 = bld.op("act", lambda e, ib=ib: e.copy(G_BF[ib], G_ST),
                    reads=("gst",), writes=(f"gbf{ib}",), after=after_cast)
        o3 = bld.op("act", lambda e, ic=ic: e.dma_start(out=U_ST, in_=upP[:, ic, :]),
                    reads=("ust",), writes=("ust",), dma=True, after=after_dma)
        o4 = bld.op("dve", lambda e, ib=ib: e.tensor_copy(U_BF[ib], U_ST),
                    reads=("ust",), writes=(f"ubf{ib}",), after=after_cast)
        return o2, o4

    trans = {"mm": None, "dacts": (), "stt": None, "silu": None}
    for th in range(2):
        for icb in range(3):
            ic0 = sum(ICB_SPLIT[:icb])
            nic = ICB_SPLIT[icb]
            ad = tuple(x for x in trans["dacts"] if x is not None)
            am = (trans["mm"],) if trans["mm"] is not None else ()
            lastg, lastu = _gu_fetch(ic0, 0, after_dma=ad + am, after_cast=am + ad)
            lastmm = None
            lastsilu = None
            laststt = None
            for i in range(nic):
                ic = ic0 + i
                ib = i % 2
                if i + 1 < nic:
                    xa = (am + ad) if i == 0 else ()
                    lastg, lastu = _gu_fetch(ic + 1, (i + 1) % 2,
                                             after_dma=xa, after_cast=xa)
                pg = PSA[:, 0:512] if ib == 0 else PSA[:, 1024:1536]
                pu = PSA[:, 512:1024] if ib == 0 else PSA[:, 1536:2048]
                tg = bA(0) if ib == 0 else bA(2)
                tu = bA(1) if ib == 0 else bA(3)
                af = am if i == 0 else ()
                for dc in range(32):
                    mm(pg, (tg,), G_BF[ib][:, dc * 128:(dc + 1) * 128],
                       XN[:, dc * 1024 + th * 512:dc * 1024 + (th + 1) * 512],
                       (f"gbf{ib}", "xn"), dc == 0, dc == 31)
                    lastmm = mm(pu, (tu,), U_BF[ib][:, dc * 128:(dc + 1) * 128],
                                XN[:, dc * 1024 + th * 512:dc * 1024 + (th + 1) * 512],
                                (f"ubf{ib}", "xn"), dc == 0, dc == 31)
                lastsilu = bld.op("act", lambda e, pg=pg, tg=tg: e.activation(
                    SILU, pg, AF.Silu, scale=SCAL[:, 0:1]),
                    reads=(tg, "scal"), writes=("silu",))
                laststt = bld.op("dve", lambda e, pu=pu, i=i: e.scalar_tensor_tensor(
                    INTERB[:, i * 512:(i + 1) * 512], SILU, DSEFF, pu,
                    ALU.mult, ALU.mult),
                    reads=("silu", "dseff", tu), writes=("inter",), after=af)
            def _dn_dma(oc, ic0=ic0, nic=nic, after=()):
                ob = oc % 2
                bld.op("sp", lambda e, oc=oc, ob=ob: e.dma_start(
                    out=D_ST[ob][:, 0:nic * 128],
                    in_=downP[:, oc, ic0 * 128:(ic0 + nic) * 128]),
                    reads=(f"dst{ob}",), writes=(f"dst{ob}",), dma=True, after=after)

            dn_acts = []

            def _dn_fetch(oc, th=th, icb=icb, ic0=ic0, nic=nic, after=()):
                ob = oc % 2
                nh = (nic * 128) // 2
                dn_acts.append(bld.op("act", lambda e, ob=ob, nh=nh: e.copy(
                    D_BF[ob][:, 0:nh], D_ST[ob][:, 0:nh]),
                    reads=(f"dst{ob}",), writes=(f"dbf{ob}",), after=after))
                dn_acts.append(bld.op("dve", lambda e, ob=ob, nh=nh: e.tensor_copy(
                    D_BF[ob][:, nh:nic * 128], D_ST[ob][:, nh:nic * 128]),
                    reads=(f"dst{ob}",), writes=(f"dbf{ob}",), after=after))
                if icb == 0:
                    xoc3 = XOC[ob].rearrange("p (s t) -> p s t", t=128)
                    bld.op("act", lambda e, oc=oc, xoc3=xoc3: e.dma_start(
                        out=xoc3,
                        in_=xP[:, oc, th * 1024:(th + 1) * 1024].rearrange(
                            "p (s t) -> p s t", t=256)[:, :, 0:128]),
                        reads=(f"xoc{ob}",), writes=(f"xoc{ob}",), dma=True, after=after)
                    bld.op("act", lambda e, oc=oc, ob=ob: e.dma_start(
                        out=OOC[ob], in_=o_d[:, oc, th * 512:(th + 1) * 512]),
                        reads=(f"ooc{ob}", f"od{oc}"), writes=(f"ooc{ob}",), dma=True,
                        after=after)
                else:
                    bld.op("act", lambda e, oc=oc, ob=ob: e.dma_start(
                        out=RIN[ob], in_=outP[:, oc, th * 512:(th + 1) * 512]),
                        reads=(f"rin{ob}", f"out_{oc}_{th}"), writes=(f"rin{ob}",),
                        dma=True, after=after)

            _dn_dma(0, after=(lastg, lastu))
            _dn_dma(1, after=(lastg, lastu))
            _dn_fetch(0, after=(lastmm, lastsilu, laststt))
            out_slots = {}
            for oc in range(32):
                ob = oc % 2
                if oc + 2 < 32:
                    _dn_dma(oc + 2)
                if oc + 1 < 32:
                    _dn_fetch(oc + 1, after=(lastmm,) if oc == 0 else ())
                ps = PSB[:, 0:512] if ob == 0 else PSB[:, 512:1024]
                pt = bB(0) if ob == 0 else bB(1)
                for i in range(nic):
                    lastmm = mm(ps, (pt,), D_BF[ob][:, i * 128:(i + 1) * 128],
                                INTERB[:, i * 512:(i + 1) * 512],
                                (f"dbf{ob}", "inter"), i == 0, i == nic - 1)
                if icb == 0:
                    bld.op("dve", lambda e, ps=ps, ob=ob: e.tensor_tensor(
                        OUTS[ob], ps, OOC[ob], ALU.add),
                        reads=(pt, f"ooc{ob}", f"outs{ob}"), writes=(f"outs{ob}",)),
                    bld.op("dve", lambda e, ob=ob: e.tensor_tensor(
                        OUTS[ob], OUTS[ob], XOC[ob], ALU.add),
                        reads=(f"outs{ob}", f"xoc{ob}"), writes=(f"outs{ob}",))
                else:
                    bld.op("dve", lambda e, ps=ps, ob=ob: e.tensor_tensor(
                        OUTS[ob], ps, RIN[ob], ALU.add),
                        reads=(pt, f"rin{ob}", f"outs{ob}"), writes=(f"outs{ob}",))
                oo = bld.op("act", lambda e, oc=oc, th=th, ob=ob: e.dma_start(
                    out=outP[:, oc, th * 512:(th + 1) * 512], in_=OUTS[ob]),
                    reads=(f"outs{ob}",), writes=(f"out_{oc}_{th}",), dma=True)
                out_slots[oo.dsem] = oo
            coll = bld.op("sp", lambda e: e.nop(),
                          after=tuple(dn_acts[-4:]) + tuple(out_slots.values()))
            trans = {"mm": lastmm, "dacts": (coll,),
                     "stt": laststt, "silu": lastsilu}
            dn_acts.clear()


def kernel(**inputs):
    pk = _pack_shared(inputs)
    cos, sin = _yarn_cos_sin()
    nc = _build(DEBUG, PHASES)
    in_maps = []
    for c in range(8):
        b, p = c // 2, c % 2
        m = dict(pk)
        m.update(_core_pack(inputs, cos, sin, b, p))
        in_maps.append(m)
    res = run_bass_kernel_spmd(nc, in_maps, core_ids=list(range(8)))
    out = np.empty((B, S, D), np.float32)
    for c in range(8):
        b, p = c // 2, c % 2
        O = np.asarray(res.results[c]["outP"])
        t = O.reshape(128, 32, 8, 128).transpose(2, 3, 1, 0).reshape(8, 128, 4096)
        ov = out[b].reshape(16, 128, 4096)
        for j in range(8):
            ov[2 * j + p] = t[j]
    return out


# version 22
# speedup vs baseline: 1.1841x; 1.0133x over previous
import math

import numpy as np

import concourse.bass as bass
import concourse.mybir as mybir
from concourse.bass_utils import run_bass_kernel_spmd

F32 = mybir.dt.float32
BF16 = mybir.dt.bfloat16
AF = mybir.ActivationFunctionType
ALU = mybir.AluOpType
AXX = mybir.AxisListType.X

H = 32
NOPE = 64
VD = 128
QHD = 128
KVR = 256
D = 4096
S = 2048
B = 4
INTER = 11008
NIC = INTER // 128
ICB_SPLIT = [29, 29, 28]
EPS_LN = 1e-5
EPS_RMS = 1e-6
_m = 0.1 * 1.0 * math.log(128.0) + 1.0
SOFTMAX_SCALE = (QHD ** -0.5) * _m * _m
MARGIN = 55.0
NEG = -1.0e9

NQ = 1024
NSLOT = 8
LJ = [2 * (j + 1) for j in range(NSLOT)]
OFFJ = [j * (j + 1) * 128 for j in range(NSLOT)]
PTOT = OFFJ[-1] + LJ[-1] * 128


def _yarn_cos_sin(seq_len=S, dim=64, base=10000.0, factor=128.0, beta_fast=32.0,
                  beta_slow=1.0, orig_max=8192, mscale=1.0, mscale_all_dim=1.0):
    def corr_dim(r):
        return dim * math.log(orig_max / (r * 2 * math.pi)) / (2 * math.log(base))
    low = max(math.floor(corr_dim(beta_fast)), 0)
    high = min(math.ceil(corr_dim(beta_slow)), dim - 1)
    hi = high + 0.001 if low == high else float(high)
    ramp = np.clip((np.arange(dim // 2, dtype=np.float32) - low) / (hi - low), 0.0, 1.0)
    inv_freq_mask = 1.0 - ramp
    freq_extra = 1.0 / base ** (np.arange(0, dim, 2, dtype=np.float32) / dim)
    freq_inter = freq_extra / factor
    inv_freq = freq_inter * (1.0 - inv_freq_mask) + freq_extra * inv_freq_mask
    t = np.arange(seq_len, dtype=np.float32)
    freqs = np.outer(t, inv_freq)
    emb = np.concatenate([freqs, freqs], axis=-1)

    def gm(s_, m_):
        return 1.0 if s_ <= 1 else 0.1 * m_ * math.log(s_) + 1.0
    ms = gm(factor, mscale) / gm(factor, mscale_all_dim)
    return (np.cos(emb) * ms).astype(np.float32), (np.sin(emb) * ms).astype(np.float32)


def _perm_tiles(p):
    order = []
    for j in range(8):
        order.append(2 * j + p)
        order.append(2 * j + 1 - p)
    return order


def _pack_shared(inp):
    pk = {}
    A = np.ascontiguousarray(np.asarray(inp["L0_attn_wq_a"], np.float32).T).reshape(32, 128, 8, 128)
    pk["wqaP"] = np.ascontiguousarray(A.transpose(1, 2, 0, 3).reshape(128, 8, 4096))
    perm = np.empty(4096, dtype=np.int64)
    for h in range(H):
        b0 = h * 128
        perm[b0:b0 + 32] = b0 + 64 + 2 * np.arange(32)
        perm[b0 + 32:b0 + 64] = b0 + 64 + 2 * np.arange(32) + 1
        perm[b0 + 64:b0 + 128] = b0 + np.arange(64)
    W = np.asarray(inp["L0_attn_wq_b"], np.float32)[perm].reshape(32, 128, 8, 128)
    pk["wqbP"] = np.ascontiguousarray(W.transpose(3, 0, 2, 1).reshape(128, 32, 1024))
    permk = np.concatenate([np.arange(256), 256 + 2 * np.arange(32),
                            256 + 2 * np.arange(32) + 1])
    A = np.asarray(inp["L0_attn_wkv_a"], np.float32)[permk].reshape(320, 32, 128)
    pk["wkvaP"] = np.ascontiguousarray(A.transpose(2, 1, 0).reshape(128, 32 * 320))
    Wb = np.asarray(inp["L0_attn_wkv_b"], np.float32).reshape(H, NOPE + VD, KVR)
    K = Wb[:, 0:NOPE, :].reshape(H, NOPE, 2, 128)
    pk["wkvbkP"] = np.ascontiguousarray(K.transpose(3, 0, 2, 1).reshape(128, 32, 128))
    V = Wb[:, NOPE:, :].reshape(8, 4, VD, 2, 128)
    pk["wkvbvP"] = np.ascontiguousarray(V.transpose(4, 0, 3, 1, 2).reshape(128, 8, 1024))
    W = np.ascontiguousarray(np.asarray(inp["L0_attn_wo"], np.float32).T).reshape(32, 128, 32, 128)
    pk["woP"] = np.ascontiguousarray(W.transpose(1, 2, 0, 3).reshape(128, 32, 4096))
    G = np.ascontiguousarray(np.asarray(inp["L0_ffn_gate_w"], np.float32).T).reshape(32, 128, NIC, 128)
    pk["gateP"] = np.ascontiguousarray(G.transpose(1, 2, 0, 3).reshape(128, NIC, 4096))
    U = np.ascontiguousarray(np.asarray(inp["L0_ffn_up_w"], np.float32).T).reshape(32, 128, NIC, 128)
    pk["upP"] = np.ascontiguousarray(U.transpose(1, 2, 0, 3).reshape(128, NIC, 4096))
    Dw = np.ascontiguousarray(np.asarray(inp["L0_ffn_down_w"], np.float32).T).reshape(NIC, 128, 32, 128)
    pk["downP"] = np.ascontiguousarray(Dw.transpose(1, 2, 0, 3).reshape(128, 32, INTER))
    pk["qgain"] = np.ascontiguousarray(np.asarray(inp["L0_attn_q_a_ln_w"], np.float32).reshape(8, 128).T)
    pk["kvgain"] = np.ascontiguousarray(np.asarray(inp["L0_attn_kv_a_ln_w"], np.float32).reshape(2, 128).T)
    sc = np.array([float(inp["L0_ffn_gate_scale"]), float(inp["L0_ffn_up_scale"]),
                   float(inp["L0_ffn_down_scale"])], np.float32).reshape(1, 3)
    pk["scal"] = np.ascontiguousarray(np.broadcast_to(sc, (128, 3)).astype(np.float32))
    pk["ident"] = np.eye(128, dtype=np.float32)
    return pk


def _core_pack(inp, cos, sin, b, p):
    d = {}
    order = _perm_tiles(p)
    x = np.asarray(inp["L0_attn_x"][b], np.float32)
    xt = x.reshape(16, 128, 4096)[order].reshape(2048, 4096)
    xt = np.ascontiguousarray(xt.T).reshape(32, 128, S)
    d["xP"] = np.ascontiguousarray(xt.transpose(1, 0, 2))
    posk = np.concatenate([np.arange(t * 128, (t + 1) * 128) for t in order])
    ck = np.ascontiguousarray(cos[posk].T)
    sk = np.ascontiguousarray(sin[posk].T).copy()
    sk[0:32] = -sk[0:32]
    d["cosk"], d["sink"] = ck, sk
    posq = np.concatenate([(2 * j + p) * 128 + np.arange(128) for j in range(NSLOT)])
    cq = np.ascontiguousarray(cos[posq].T)
    sq = np.ascontiguousarray(sin[posq].T).copy()
    sq[0:32] = -sq[0:32]
    d["cosq"], d["sinq"] = cq, sq
    i_ = np.arange(128)[:, None]
    c_ = np.arange(128)[None, :]
    diag = np.where(c_ <= i_, np.float32(0.0), np.float32(NEG))
    oth = np.full((128, 128), 0.0 if p == 1 else NEG, np.float32)
    d["masks"] = np.ascontiguousarray(np.concatenate([diag, oth], axis=1))
    return d


ENGS = ("sp", "pe", "act", "dve", "pool")


class _Op:
    __slots__ = ("eng", "fn", "idx", "need_inc", "deps", "dma", "dsem", "duse")

    def __init__(self, eng, fn, dma, idx):
        self.eng, self.fn, self.dma, self.idx = eng, fn, dma, idx
        self.need_inc = False
        self.deps = []
        self.dsem = -1
        self.duse = 0


NDMASEM = 24


class Bld:
    def __init__(self):
        self.eng_ops = {e: [] for e in ENGS}
        self.last_w = {}
        self.readers = {}
        self.key_last = {}
        self.dma_n = 0
        self.dma_uses = [0] * NDMASEM
        self.dma_last_op = [None] * NDMASEM

    def op(self, eng, fn, reads=(), writes=(), dma=False, after=()):
        o = _Op(eng, fn, eng if dma else None, len(self.eng_ops[eng]))
        if dma:
            o.need_inc = True
            o.dsem = self.dma_n % NDMASEM
            self.dma_n += 1
            self.dma_uses[o.dsem] += 1
            o.duse = self.dma_uses[o.dsem]
        deps, seen = [], set()

        def add(pr):
            if pr is None or pr is o or id(pr) in seen:
                return
            if pr.eng == "pe" and eng == "pe" and pr.dma is None and not dma:
                return
            seen.add(id(pr))
            deps.append(pr)
            pr.need_inc = True

        for pr_ in after:
            add(pr_)
        for t in reads:
            add(self.last_w.get(t))
        for t in writes:
            add(self.last_w.get(t))
            for r in self.readers.get(t, ()):
                add(r)
        for t in writes:
            self.last_w[t] = o
            self.readers[t] = []
        for t in reads:
            self.readers.setdefault(t, []).append(o)
        if dma and o.duse > 1:
            prev = self.dma_last_op[o.dsem]
            if prev is not None and id(prev) not in seen:
                seen.add(id(prev))
                deps.append(prev)
        if dma:
            self.dma_last_op[o.dsem] = o
        o.deps = deps
        self.eng_ops[eng].append(o)
        self.key_last[(eng, bool(dma))] = o
        return o

    def barrier(self):
        lasts = [o for o in self.key_last.values() if o.dma is None]
        lasts += [o for o in self.dma_last_op if o is not None]
        for pr in lasts:
            pr.need_inc = True
        for e in ENGS:
            o = _Op(e, None, None, len(self.eng_ops[e]))
            o.deps = lasts[:]
            self.eng_ops[e].append(o)

    def materialize(self, sems, dsems):
        cnt, val = {}, {}
        for ops in self.eng_ops.values():
            for o in ops:
                if o.dma is not None:
                    val[id(o)] = ("D", o.dsem, o.duse * 16)
                else:
                    key = o.eng
                    if o.need_inc:
                        cnt[key] = cnt.get(key, 0) + 1
                    val[id(o)] = ("E", key, cnt.get(key, 0))
        self.final_counts = {("E", k): v for k, v in cnt.items()}
        for i in range(NDMASEM):
            if self.dma_uses[i]:
                self.final_counts[("D", i)] = self.dma_uses[i] * 16

        def emit(engine, ops):
            water = {}
            for o in ops:
                for pr in o.deps:
                    kind, key, v = val[id(pr)]
                    sem = dsems[key] if kind == "D" else sems[key]
                    wk = (kind, key)
                    if water.get(wk, -1) >= v:
                        continue
                    water[wk] = v
                    engine.wait_ge(sem, v)
                if o.fn is None:
                    continue
                ins = o.fn(engine)
                if o.dma is not None:
                    ins.then_inc(dsems[o.dsem], 16)
                elif o.need_inc:
                    ins.then_inc(sems[o.eng], 1)
        return emit


_CACHE = {}
DEBUG = 0
PHASES = 99


def _build(debug=0, phases=99):
    ck_ = (debug, phases)
    if ck_ in _CACHE:
        return _CACHE[ck_]
    nc = bass.Bass("TRN2", target_bir_lowering=False, debug=False)

    ein = lambda n, s_, dt=F32: nc.dram_tensor(n, s_, dt, kind="ExternalInput").ap()
    xP = ein("xP", [128, 32, S])
    masks = ein("masks", [128, 256])
    cosq, sinq = ein("cosq", [64, NQ]), ein("sinq", [64, NQ])
    cosk, sink = ein("cosk", [64, S]), ein("sink", [64, S])
    qgain, kvgain = ein("qgain", [128, 8]), ein("kvgain", [128, 2])
    scal = ein("scal", [128, 3])
    identf = ein("ident", [128, 128])
    wqaP = ein("wqaP", [128, 8, 4096])
    wqbP = ein("wqbP", [128, 32, 1024])
    wkvaP = ein("wkvaP", [128, 32 * 320])
    wkvbkP = ein("wkvbkP", [128, 32, 128])
    wkvbvP = ein("wkvbvP", [128, 8, 1024])
    woP = ein("woP", [128, 32, 4096])
    gateP = ein("gateP", [128, NIC, 4096])
    upP = ein("upP", [128, NIC, 4096])
    downP = ein("downP", [128, 32, INTER])

    outP = nc.dram_tensor("outP", [128, 32, NQ], F32, kind="ExternalOutput").ap()
    attn_d = nc.dram_tensor("attn_d", [128, 32, NQ], BF16, kind="Internal").ap()
    o_d = nc.dram_tensor("o_d", [128, 32, NQ], F32, kind="Internal").ap()

    dbg = {}
    if debug:
        dd = lambda n, s_, dt: nc.dram_tensor(n, s_, dt, kind="ExternalOutput").ap()
        dbg["xn"] = dd("D_xn", [128, 32 * 1024], BF16)
        dbg["ckv"] = dd("D_ckv", [128, 2 * 2048], BF16)
        dbg["kpe"] = dd("D_kpe", [64, 2048], BF16)
        dbg["cqn"] = dd("D_cqn", [128, 8 * 1024], BF16)
        dbg["qh"] = dd("D_qh", [128, 1024], BF16)
        dbg["kf"] = dd("D_kf", [128, 2048], BF16)
        dbg["pb"] = dd("D_pb", [128, PTOT], BF16)
        dbg["at"] = dd("D_at", [128, 32, NQ], BF16)
        dbg["od"] = dd("D_od", [128, 32, NQ], F32)
        dbg["v4"] = dd("D_v4", [128, 8192], BF16)

    XN = nc.alloc_sbuf_tensor("XN", [128, 32 * 1024], BF16).ap()
    CQN = nc.alloc_sbuf_tensor("CQN", [128, 8 * 1024], BF16).ap()
    CKV = nc.alloc_sbuf_tensor("CKV", [128, 2 * 2048], BF16).ap()
    KPE = nc.alloc_sbuf_tensor("KPE", [64, 2048], BF16).ap()
    GQ = nc.alloc_sbuf_tensor("GQ", [128, 8], F32).ap()
    GKV = nc.alloc_sbuf_tensor("GKV", [128, 2], F32).ap()
    SCAL = nc.alloc_sbuf_tensor("SCAL", [128, 3], F32).ap()
    DSEFF = nc.alloc_sbuf_tensor("DSEFF", [128, 1], F32).ap()
    IDENT = nc.alloc_sbuf_tensor("IDENT", [128, 128], BF16).ap()
    ONESB = nc.alloc_sbuf_tensor("ONESB", [128, 1], BF16).ap()
    ONES128 = nc.alloc_sbuf_tensor("ONES128", [128, 128], BF16).ap()
    ONESF = nc.alloc_sbuf_tensor("ONESF", [1, 128], F32).ap()
    SROW = nc.alloc_sbuf_tensor("SROW", [1, 1536], F32).ap()
    BIAS8 = nc.alloc_sbuf_tensor("BIAS8", [128, 16], F32).ap()
    T8 = nc.alloc_sbuf_tensor("T8", [128, 16], F32).ap()
    LROW = nc.alloc_sbuf_tensor("LROW", [128, 48], F32).ap()
    LINV = nc.alloc_sbuf_tensor("LINV", [128, 16], F32).ap()
    K2MAX = nc.alloc_sbuf_tensor("K2MAX", [128, 2], F32).ap()
    BND = nc.alloc_sbuf_tensor("BND", [128, 2], F32).ap()
    F2G = nc.alloc_sbuf_tensor("F2G", [128, 8], F32).ap()
    KB2 = nc.alloc_sbuf_tensor("KB2", [128, 2], F32).ap()
    CT = nc.alloc_sbuf_tensor("CT", [128, 4], F32).ap()

    ARENA = nc.alloc_sbuf_tensor("ARENA", [128, (106 * 1024) // 4], F32).ap()

    def carve(off_b, part, nfree, dtype):
        assert off_b % 4 == 0
        if dtype == F32:
            v = ARENA[:, off_b // 4: off_b // 4 + nfree]
        else:
            assert nfree % 2 == 0
            v = ARENA[:, off_b // 4: off_b // 4 + nfree // 2].bitcast(BF16)
        if part != 128:
            v = v[0:part, :]
        return v

    PSA = nc.alloc_psum_tensor("PSA", [128, 2048], F32).ap()
    PSB = nc.alloc_psum_tensor("PSB", [128, 2048], F32).ap()
    bA = lambda i: f"A{i}"
    bB = lambda i: f"B{i}"

    bld = Bld()

    def mm(dst, banks, lhsT, rhs, reads, start, stop, after=()):
        return bld.op("pe", lambda e, d=dst, l=lhsT, r=rhs, s1=start, s2=stop:
                      e.matmul(d, l, r, start=s1, stop=s2),
                      reads=tuple(reads), writes=tuple(banks), after=after)

    ident_stage = carve(0, 128, 128, F32)
    bld.op("sp", lambda e: e.dma_start(out=GQ, in_=qgain), writes=("gq",), dma=True)
    bld.op("sp", lambda e: e.dma_start(out=GKV, in_=kvgain), writes=("gkv",), dma=True)
    bld.op("sp", lambda e: e.dma_start(out=SCAL, in_=scal), writes=("scal",), dma=True)
    bld.op("sp", lambda e: e.dma_start(out=ident_stage, in_=identf), writes=("idst",), dma=True)
    bld.op("dve", lambda e: e.tensor_copy(IDENT, ident_stage), reads=("idst",), writes=("ident",))
    bld.op("dve", lambda e: e.tensor_mul(DSEFF, SCAL[:, 1:2], SCAL[:, 2:3]),
           reads=("scal",), writes=("dseff",))
    bld.op("pool", lambda e: e.memset(ONESB, 1.0), writes=("onesb",))
    bld.op("pool", lambda e: e.memset(ONES128, 1.0), writes=("ones128",))
    bld.op("pool", lambda e: e.memset(ONESF, 1.0), writes=("onesf",))
    bld.op("pool", lambda e: e.memset(CT[:, 0:1], EPS_LN), writes=("constv",))
    bld.op("pool", lambda e: e.memset(CT[:, 1:2], EPS_RMS), writes=("constv",))
    bld.op("pool", lambda e: e.memset(CT[:, 2:3], 1e-20), writes=("constv",))
    bld.barrier()
    xstage = carve(0, 128, 8192, F32)
    XBFF = [carve(32 * 1024, 128, 8192, BF16), carve(48 * 1024, 128, 8192, BF16)]
    wkva_bf = carve(64 * 1024, 128, 32 * 320, BF16)
    ckt = [[carve(86 * 1024 + (2 * i + k) * 1024, 64, 256, F32) for k in range(2)]
           for i in range(2)]
    rt1 = carve(90 * 1024, 64, 256, F32)
    rt2 = carve(91 * 1024, 64, 256, F32)
    tsc = carve(92 * 1024, 128, 1024, F32)
    ckf = carve(96 * 1024, 128, 512, F32)
    rbs = [carve(98 * 1024, 128, 256, F32), carve(100 * 1024, 128, 256, F32)]
    mrbs = [carve(99 * 1024, 128, 256, F32), carve(101 * 1024, 128, 256, F32)]
    sq2 = carve(102 * 1024, 128, 512, BF16)
    sqr = [carve(103 * 1024, 128, 512, BF16), carve(104 * 1024, 128, 512, BF16)]
    wkva_st = carve(32 * 1024, 128, 2560, F32)

    for pc in range(4):
        bld.op("sp", lambda e, pc=pc: e.dma_start(
            out=wkva_st, in_=wkvaP[:, pc * 2560:(pc + 1) * 2560]),
            reads=("wkvast",), writes=("wkvast",), dma=True)
        bld.op("dve", lambda e, pc=pc: e.tensor_copy(
            wkva_bf[:, pc * 2560:(pc + 1) * 2560], wkva_st),
            reads=("wkvast",), writes=("wkva",))

    xs3 = xstage.rearrange("p (d t) -> p d t", t=256)
    XN3 = XN.rearrange("p (d t) -> p d t", t=1024)
    r_m = SROW[:, 0:256]
    r_s2 = SROW[:, 256:512]
    r_v = SROW[:, 512:768]
    r_r = SROW[:, 768:1024]
    r_mr = SROW[:, 1024:1280]
    r_k = SROW[:, 1280:1536]

    def _p1_dma(c):
        for half in range(2):
            bld.op("sp", lambda e, c=c, half=half: e.dma_start(
                out=xs3[:, half * 16:(half + 1) * 16, :],
                in_=xP[:, half * 16:(half + 1) * 16, c * 256:(c + 1) * 256]),
                reads=(f"xs{half}",), writes=(f"xs{half}",), dma=True)

    def _ckt_dma(c):
        cb = c % 2
        bld.op("sp", lambda e, c=c, cb=cb: e.dma_start(
            out=ckt[cb][0], in_=cosk[:, c * 256:(c + 1) * 256]),
            reads=(f"ckt{cb}",), writes=(f"ckt{cb}",), dma=True)
        bld.op("sp", lambda e, c=c, cb=cb: e.dma_start(
            out=ckt[cb][1], in_=sink[:, c * 256:(c + 1) * 256]),
            reads=(f"ckt{cb}",), writes=(f"ckt{cb}",), dma=True)

    def _p1_s(c, first=False):
        pb = c % 2
        xb = XBFF[pb]
        for g in range(8):
            wt = (f"xb{pb}g{g}", "wkvast") if (first and g < 5) else (f"xb{pb}g{g}",)
            bld.op("act", lambda e, g=g, xb=xb: e.copy(
                xb[:, g * 1024:(g + 1) * 1024], xstage[:, g * 1024:(g + 1) * 1024]),
                reads=(f"xs{0 if g < 4 else 1}",), writes=wt)
            for hh in range(2):
                d2 = 2 * g + hh
                gb = d2 % 2
                bld.op("act", lambda e, d2=d2, gb=gb, xb=xb: e.activation(
                    sqr[gb], xb[:, d2 * 512:(d2 + 1) * 512], AF.Square),
                    reads=(f"xb{pb}g{g}",), writes=(f"sq{gb}",))
                for q in range(2):
                    dc = 2 * d2 + q
                    mm(PSB[0:1, 0:256], (bB(0),),
                       ONESB, xb[:, dc * 256:(dc + 1) * 256],
                       ("onesb", f"xb{pb}g{g}"), dc == 0, dc == 31)
                    mm(PSB[0:1, 512:768], (bB(1),),
                       ONESB, sqr[gb][:, q * 256:(q + 1) * 256],
                       ("onesb", f"sq{gb}"), dc == 0, dc == 31)
        if c + 1 < 8:
            _p1_dma(c + 1)
        bld.op("dve", lambda e: e.tensor_scalar_mul(r_m, PSB[0:1, 0:256], 1.0 / D),
               reads=(bB(0),), writes=("rm",))
        bld.op("dve", lambda e: e.tensor_scalar_mul(r_s2, PSB[0:1, 512:768], 1.0 / D),
               reads=(bB(1),), writes=("rs2",))
        bld.op("dve", lambda e: e.tensor_mul(r_v, r_m, r_m), reads=("rm",), writes=("rv",))
        bld.op("dve", lambda e: e.tensor_sub(r_v, r_s2, r_v), reads=("rs2", "rv"), writes=("rv",))
        bld.op("act", lambda e: e.activation(r_v, r_v, AF.Ln, bias=CT[0:1, 0:1]),
               reads=("rv",), writes=("rv",))
        bld.op("act", lambda e: e.activation(r_r, r_v, AF.Exp, scale=-0.5),
               reads=("rv",), writes=("rr",))
        bld.op("dve", lambda e: e.tensor_mul(r_mr, r_m, r_r), reads=("rm", "rr"), writes=("rmr",))
        mm(PSB[:, 1024:1280], (bB(2),), ONESF, r_r, ("onesf", "rr"), True, True)
        mm(PSB[:, 1536:1792], (bB(3),), ONESF, r_mr, ("onesf", "rmr"), True, True)
        bld.op("act", lambda e, pb=pb: e.copy(rbs[pb], PSB[:, 1024:1280]),
               reads=(bB(2),), writes=(f"rbs{pb}",))
        bld.op("act", lambda e, pb=pb: e.copy(mrbs[pb], PSB[:, 1536:1792]),
               reads=(bB(3),), writes=(f"mrbs{pb}",))

    def _p1_x(c):
        pb = c % 2
        cb = c % 2
        if c + 1 < 8:
            _ckt_dma(c + 1)
        xb = XBFF[pb]
        xb3 = xb.rearrange("p (d t) -> p d t", t=256)
        for g in range(8):
            xg3 = xb3[:, 4 * g:4 * (g + 1), :]
            t3 = tsc.rearrange("p (d t) -> p d t", t=256)
            rb_b = rbs[pb].unsqueeze(1).broadcast_to((128, 4, 256))
            mrb_b = mrbs[pb].unsqueeze(1).broadcast_to((128, 4, 256))
            bld.op("pool", lambda e, xg3=xg3, rb_b=rb_b, t3=t3:
                   e.tensor_tensor(t3, xg3, rb_b, ALU.mult),
                   reads=(f"xb{pb}g{g}", f"rbs{pb}"), writes=("tsc",))
            bld.op("dve", lambda e, xg3=xg3, mrb_b=mrb_b, t3=t3:
                   e.tensor_sub(xg3, t3, mrb_b),
                   reads=("tsc", f"mrbs{pb}"), writes=(f"xb{pb}g{g}",))
            bld.op("dve", lambda e, g=g, c=c, xg3=xg3:
                   e.tensor_copy(XN3[:, 4 * g:4 * (g + 1), c * 128:(c + 1) * 128],
                                 xg3[:, :, 0:128]),
                   reads=(f"xb{pb}g{g}",), writes=("xn",))
            for dd_ in range(4):
                dc = 4 * g + dd_
                rhs = xg3[:, dd_, :]
                mm(PSA[:, 1024:1280], (bA(2),),
                   wkva_bf[:, dc * 320:dc * 320 + 128], rhs,
                   ("wkva", f"xb{pb}g{g}"), dc == 0, dc == 31)
                mm(PSA[:, 1536:1792], (bA(3),),
                   wkva_bf[:, dc * 320 + 128:dc * 320 + 256], rhs,
                   ("wkva", f"xb{pb}g{g}"), dc == 0, dc == 31)
                mm(PSA[0:64, 512:768], (bA(1),),
                   wkva_bf[:, dc * 320 + 256:dc * 320 + 320], rhs,
                   ("wkva", f"xb{pb}g{g}"), dc == 0, dc == 31)
        bld.op("act", lambda e: e.activation(sq2[:, 0:256], PSA[:, 1024:1280], AF.Square),
               reads=(bA(2),), writes=("sq2",))
        bld.op("act", lambda e: e.activation(sq2[:, 256:512], PSA[:, 1536:1792], AF.Square),
               reads=(bA(3),), writes=("sq2",))
        mm(PSA[0:1, 0:256], (bA(0),), ONESB, sq2[:, 0:256], ("onesb", "sq2"), True, False)
        mm(PSA[0:1, 0:256], (bA(0),), ONESB, sq2[:, 256:512], ("onesb", "sq2"), False, True)
        bld.op("act", lambda e: e.activation(r_k, PSA[0:1, 0:256], AF.Ln,
                                             bias=CT[0:1, 1:2], scale=1.0 / KVR),
               reads=(bA(0),), writes=("rk",))
        bld.op("act", lambda e: e.activation(r_k, r_k, AF.Exp, scale=-0.5),
               reads=("rk",), writes=("rk",))
        mm(PSA[:, 256:512], (bA(0),), ONESF, r_k, ("onesf", "rk"), True, True)
        for l in range(2):
            src = PSA[:, 1024:1280] if l == 0 else PSA[:, 1536:1792]
            btag = bA(2) if l == 0 else bA(3)
            bld.op("act", lambda e, src=src: e.copy(ckf[:, 0:256], src),
                   reads=(btag,), writes=("ckf",))
            bld.op("dve", lambda e: e.tensor_tensor(ckf[:, 256:512], ckf[:, 0:256],
                                                    PSA[:, 256:512], ALU.mult),
                   reads=("ckf", bA(0)), writes=("ckf2",))
            bld.op("dve", lambda e, l=l, c=c: e.tensor_scalar(
                CKV[:, l * 2048 + c * 256:l * 2048 + (c + 1) * 256],
                ckf[:, 256:512], GKV[:, l:l + 1], None, ALU.mult),
                reads=("ckf2", "gkv"), writes=("ckv",))
        kp = PSA[0:64, 512:768]
        bld.op("dve", lambda e, cb=cb: e.tensor_tensor(rt1, kp, ckt[cb][0], ALU.mult),
               reads=(bA(1), f"ckt{cb}"), writes=("rt1",))
        bld.op("dve", lambda e, cb=cb: e.tensor_tensor(rt2[0:32, :], kp[32:64, :],
                                                       ckt[cb][1][0:32, :], ALU.mult),
               reads=(bA(1), f"ckt{cb}"), writes=("rt2",))
        bld.op("dve", lambda e, cb=cb: e.tensor_tensor(rt2[32:64, :], kp[0:32, :],
                                                       ckt[cb][1][32:64, :], ALU.mult),
               reads=(bA(1), f"ckt{cb}"), writes=("rt2",))
        bld.op("dve", lambda e, c=c: e.tensor_add(KPE[:, c * 256:(c + 1) * 256], rt1, rt2),
               reads=("rt1", "rt2"), writes=("kpe",))

    _p1_dma(0)
    _ckt_dma(0)
    _p1_s(0, first=True)
    for c in range(8):
        if c + 1 < 8:
            _p1_s(c + 1)
        _p1_x(c)

    SQKP = carve(0, 128, 2048, BF16)
    SQCK = carve(4 * 1024, 128, 4096, BF16)
    bld.op("pool", lambda e: e.tensor_tensor(SQKP[0:64, :], KPE, KPE, ALU.mult),
           reads=("kpe",), writes=("sqkp", "xs0"))
    bld.op("pool", lambda e: e.tensor_tensor(SQCK, CKV, CKV, ALU.mult),
           reads=("ckv",), writes=("sqck", "xs0"))
    for n in range(4):
        mm(PSB[0:1, n * 512:(n + 1) * 512], (bB(n),), ONESB[0:64, :],
           SQKP[0:64, n * 512:(n + 1) * 512], ("onesb", "sqkp"), True, True)
        for l in range(2):
            mm(PSA[0:1, n * 512:(n + 1) * 512], (bA(n),), ONESB,
               SQCK[:, l * 2048 + n * 512:l * 2048 + (n + 1) * 512],
               ("onesb", "sqck"), l == 0, l == 1)
    bld.op("dve", lambda e: e.reduce_max(SROW[:, 0:1], PSB[0:1, 0:2048], AXX),
           reads=(bB(0), bB(1), bB(2), bB(3)), writes=("brow",))
    bld.op("dve", lambda e: e.reduce_max(SROW[:, 1:2], PSA[0:1, 0:2048], AXX),
           reads=(bA(0), bA(1), bA(2), bA(3)), writes=("brow",))
    mm(PSB[:, 0:2], (bB(0),), ONESF, SROW[:, 0:2], ("onesf", "brow"), True, True)
    bld.op("act", lambda e: e.copy(BND, PSB[:, 0:2]), reads=(bB(0),), writes=("bnd",))

    if debug:
        bld.op("act", lambda e: e.dma_start(out=dbg["xn"], in_=XN),
               reads=("xn",), dma=True)
        bld.op("act", lambda e: e.dma_start(out=dbg["ckv"], in_=CKV),
               reads=("ckv",), dma=True)
        bld.op("act", lambda e: e.dma_start(out=dbg["kpe"], in_=KPE),
               reads=("kpe",), dma=True)
    bld.barrier()

    if phases >= 2:
        _p2(nc, bld, locals())
    if phases >= 3:
        _p3(nc, bld, locals())
    if phases >= 4:
        _p4(nc, bld, locals())
    if phases >= 5:
        _p5(nc, bld, locals())

    from contextlib import ExitStack
    es = ExitStack()
    sems = {}
    for e in ENGS:
        sems[e] = es.enter_context(nc.semaphore(f"m_{e}"))
    dsems = [es.enter_context(nc.semaphore(f"d_{i}")) for i in range(NDMASEM)]
    with nc.Block() as blk:
        emit = bld.materialize(sems, dsems)
        blk.sync(lambda eng: emit(eng, bld.eng_ops["sp"]))
        blk.tensor(lambda eng: emit(eng, bld.eng_ops["pe"]))
        blk.scalar(lambda eng: emit(eng, bld.eng_ops["act"]))
        blk.vector(lambda eng: emit(eng, bld.eng_ops["dve"]))
        blk.gpsimd(lambda eng: emit(eng, bld.eng_ops["pool"]))
    fc = bld.final_counts
    with nc.Block() as blk2:
        def fin(eng):
            for key, v in fc.items():
                if key[0] == "D":
                    eng.wait_ge(dsems[key[1]], v)
                else:
                    eng.wait_ge(sems[key[1]], v)
        blk2.sync(fin)
    es.close()
    _CACHE[ck_] = nc
    return nc


def _p2(nc, bld, env):
    carve, mm, PSA, PSB, bA, bB = (env["carve"], env["mm"], env["PSA"], env["PSB"],
                                   env["bA"], env["bB"])
    wqaP, XN, CQN, GQ, ONESB, ONESF, SROW = (env["wqaP"], env["XN"], env["CQN"],
                                             env["GQ"], env["ONESB"], env["ONESF"],
                                             env["SROW"])
    CT = env["CT"]
    debug, dbg = env["debug"], env["dbg"]
    wqa_st = carve(0, 128, 4096, F32)
    wqa_bf = [carve(16 * 1024, 128, 4096, BF16), carve(24 * 1024, 128, 4096, BF16)]
    cqf = carve(32 * 1024, 128, 8192, F32)
    sqc = carve(64 * 1024, 128, 1024, BF16)

    def _wqa_fetch(oc):
        ob = oc % 2
        bld.op("sp", lambda e, oc=oc: e.dma_start(out=wqa_st, in_=wqaP[:, oc, :]),
               reads=("wqast",), writes=("wqast",), dma=True)
        bld.op("act", lambda e, ob=ob: e.copy(wqa_bf[ob], wqa_st),
               reads=("wqast",), writes=(f"wqab{ob}",))
    _wqa_fetch(0)
    for oc in range(8):
        ob = oc % 2
        if oc + 1 < 8:
            _wqa_fetch(oc + 1)
        ps = PSA[:, 0:1024] if ob == 0 else PSA[:, 1024:2048]
        pbt = (bA(0), bA(1)) if ob == 0 else (bA(2), bA(3))
        for dc in range(32):
            for hf in range(2):
                mm(ps[:, hf * 512:(hf + 1) * 512], (pbt[hf],),
                   wqa_bf[ob][:, dc * 128:(dc + 1) * 128],
                   XN[:, dc * 1024 + hf * 512:dc * 1024 + (hf + 1) * 512],
                   (f"wqab{ob}", "xn"), dc == 0, dc == 31)
        bld.op("act", lambda e, ps=ps: e.activation(sqc, ps, AF.Square),
               reads=pbt, writes=("sqc",))
        for hf in range(2):
            mm(PSB[0:1, hf * 512:(hf + 1) * 512], (bB(hf),),
               ONESB, sqc[:, hf * 512:(hf + 1) * 512],
               ("onesb", "sqc"), oc == 0, oc == 7)
        bld.op("act", lambda e, ps=ps, oc=oc: e.copy(cqf[:, oc * 1024:(oc + 1) * 1024], ps),
               reads=pbt, writes=("cqf",))
    for hf in range(2):
        bld.op("act", lambda e, hf=hf: e.activation(
            SROW[:, hf * 512:(hf + 1) * 512], PSB[0:1, hf * 512:(hf + 1) * 512],
            AF.Ln, bias=CT[0:1, 1:2], scale=1.0 / 1024.0),
            reads=(bB(hf),), writes=("qrow",))
        bld.op("act", lambda e, hf=hf: e.activation(
            SROW[:, hf * 512:(hf + 1) * 512], SROW[:, hf * 512:(hf + 1) * 512],
            AF.Exp, scale=-0.5), reads=("qrow",), writes=("qrow",))
        mm(PSB[:, 1024 + hf * 512:1024 + (hf + 1) * 512], (bB(2 + hf),),
           ONESF, SROW[:, hf * 512:(hf + 1) * 512], ("onesf", "qrow"), True, True)
    rqb = carve(66 * 1024, 128, 1024, F32)
    bld.op("act", lambda e: e.copy(rqb, PSB[:, 1024:2048]),
           reads=(bB(2), bB(3)), writes=("rqb",))
    for oc in range(8):
        bld.op("pool", lambda e, oc=oc: e.tensor_tensor(
            cqf[:, oc * 1024:(oc + 1) * 1024], cqf[:, oc * 1024:(oc + 1) * 1024],
            rqb, ALU.mult),
            reads=("cqf", "rqb"), writes=(f"cqfn{oc}",))
        bld.op("dve", lambda e, oc=oc: e.tensor_scalar(
            CQN[:, oc * 1024:(oc + 1) * 1024], cqf[:, oc * 1024:(oc + 1) * 1024],
            GQ[:, oc:oc + 1], None, ALU.mult),
            reads=(f"cqfn{oc}", "gq"), writes=("cqn",))
    if debug:
        bld.op("act", lambda e: e.dma_start(out=dbg["cqn"], in_=CQN),
               reads=("cqn",), dma=True)


def _p3(nc, bld, env):
    carve, mm, PSA, PSB, bA, bB = (env["carve"], env["mm"], env["PSA"], env["PSB"],
                                   env["bA"], env["bB"])
    (masks, cosq, sinq, wqbP, wkvbkP, wkvbvP, attn_d) = (
        env["masks"], env["cosq"], env["sinq"], env["wqbP"],
        env["wkvbkP"], env["wkvbvP"], env["attn_d"])
    (XN, CQN, CKV, KPE, IDENT, ONESB, ONES128, BIAS8, T8, LROW, LINV) = (
        env["XN"], env["CQN"], env["CKV"], env["KPE"], env["IDENT"], env["ONESB"],
        env["ONES128"], env["BIAS8"], env["T8"], env["LROW"], env["LINV"])
    BND, F2G, KB2, CT = env["BND"], env["F2G"], env["KB2"], env["CT"]
    debug, dbg = env["debug"], env["dbg"]

    MASKS = carve(0, 128, 256, F32)
    COSQ = carve(2 * 1024, 64, 1024, F32)
    SINQ = carve(6 * 1024, 64, 1024, F32)
    QH = [carve(10 * 1024, 128, 1024, BF16), carve(12 * 1024, 128, 1024, BF16)]
    KF = [carve(14 * 1024, 128, 2048, BF16), carve(18 * 1024, 128, 2048, BF16)]
    V4 = carve(22 * 1024, 128, 8192, BF16)
    V43 = V4.rearrange("p (k v) -> p k v", v=512)
    PBUF = carve(38 * 1024, 128, PTOT, BF16)
    PT = carve(56 * 1024, 128, 8192, BF16)
    PT4 = PT.rearrange("p (k j c) -> p k j c", j=8, c=128)
    RTA = carve(72 * 1024, 64, 1024, F32)
    RTB = carve(76 * 1024, 64, 1024, F32)
    SQKV = carve(76 * 1024, 128, 512, BF16)
    Q2K = carve(72 * 1024, 128, 2048, BF16)
    WQB_ST = carve(80 * 1024, 128, 1024, F32)
    WQB_BF = [carve(84 * 1024, 128, 1024, BF16), carve(86 * 1024, 128, 1024, BF16)]
    KVK_ST = carve(88 * 1024, 128, 512, F32)
    KVV_ST = carve(90 * 1024, 128, 1024, F32)
    KVK_BF = [carve(94 * 1024, 128, 512, BF16), carve(95 * 1024, 128, 512, BF16)]
    KVV_BF = [carve(96 * 1024, 128, 1024, BF16), carve(98 * 1024, 128, 1024, BF16)]
    ATTS = carve(100 * 1024, 128, 1024, BF16)
    QSTG = carve(102 * 1024, 128, 1024, F32)

    units = []
    for j in range(NSLOT):
        n = LJ[j] * 128
        for k0 in range(0, n, 512):
            units.append((j, k0, min(k0 + 512, n)))
    ufirst = {}
    for ui, (j, _, _k) in enumerate(units):
        ufirst.setdefault(j, ui)

    def _wqb_fetch(h):
        qb = h % 2
        bld.op("sp", lambda e, h=h: e.dma_start(out=WQB_ST, in_=wqbP[:, h, :]),
               reads=("wqbst",), writes=("wqbst",), dma=True)
        bld.op("act", lambda e, qb=qb: e.copy(WQB_BF[qb], WQB_ST),
               reads=("wqbst",), writes=(f"wqbb{qb}",))

    def _kv_fetch(g):
        gb = g % 2
        bld.op("sp", lambda e, g=g: e.dma_start(out=KVK_ST, in_=wkvbkP[:, 4 * g:4 * g + 4, :]),
               reads=("kvkst",), writes=("kvkst",), dma=True)
        bld.op("sp", lambda e, g=g: e.dma_start(out=KVV_ST, in_=wkvbvP[:, g, :]),
               reads=("kvvst",), writes=("kvvst",), dma=True)
        bld.op("pool", lambda e, gb=gb: e.tensor_copy(KVK_BF[gb], KVK_ST),
               reads=("kvkst",), writes=(f"kvkb{gb}",))
        bld.op("pool", lambda e, gb=gb: e.tensor_copy(KVV_BF[gb], KVV_ST),
               reads=("kvvst",), writes=(f"kvvb{gb}",))

    def _vgen(g):
        gb = g % 2
        if g + 1 < 8:
            _kv_fetch(g + 1)
        bld.op("pool", lambda e, gb=gb: e.tensor_tensor(SQKV, KVK_BF[gb], KVK_BF[gb],
                                                        ALU.mult),
               reads=(f"kvkb{gb}",), writes=("rtb",))
        mm(PSB[:, 1024:1536], (bB(2),), ONES128, SQKV, ("ones128", "rtb"), True, True)
        bld.op("dve", lambda e, gb=gb: e.reduce_sum(
            F2G[:, 4 * gb:4 * gb + 4],
            PSB[:, 1024:1536].rearrange("p (h c) -> p h c", c=128), AXX),
            reads=(bB(2),), writes=(f"f2g{gb}",))
        for kt in range(16):
            vb = kt % 2
            for l in range(2):
                mm(PSB[:, vb * 512:(vb + 1) * 512], (bB(vb),),
                   CKV[:, l * 2048 + kt * 128:l * 2048 + (kt + 1) * 128],
                   KVV_BF[gb][:, l * 512:(l + 1) * 512],
                   ("ckv", f"kvvb{gb}"), l == 0, l == 1)
            bld.op("act", lambda e, kt=kt, vb=vb: e.copy(
                V43[:, kt, :], PSB[:, vb * 512:(vb + 1) * 512]),
                reads=(bB(vb),), writes=("v4",))
        if debug and g == 0:
            bld.op("act", lambda e: e.dma_start(out=dbg["v4"], in_=V4),
                   reads=("v4",), dma=True)

    def _gen_a(h):
        g, hh = h // 4, h % 4
        gb = g % 2
        qb = kb = h % 2
        if h + 1 < 32:
            _wqb_fetch(h + 1)
        for dc in range(8):
            for hf in range(2):
                mm(PSA[:, hf * 512:(hf + 1) * 512], (bA(hf),),
                   WQB_BF[qb][:, dc * 128:(dc + 1) * 128],
                   CQN[:, dc * 1024 + hf * 512:dc * 1024 + (hf + 1) * 512],
                   (f"wqbb{qb}", "cqn"), dc == 0, dc == 7)
        bld.op("act", lambda e: e.copy(QSTG, PSA[:, 0:1024]),
               reads=(bA(0), bA(1)), writes=("qstg",))
        for n in range(4):
            pb_ = 2 + (n % 2)
            for l in range(2):
                mm(PSB[0:64, pb_ * 512:(pb_ + 1) * 512], (bB(pb_),),
                   KVK_BF[gb][:, hh * 128 + l * 64:hh * 128 + l * 64 + 64],
                   CKV[:, l * 2048 + n * 512:l * 2048 + (n + 1) * 512],
                   (f"kvkb{gb}", "ckv"), l == 0, l == 1)
            bld.op("dve", lambda e, n=n, kb=kb, pb_=pb_: e.tensor_copy(
                KF[kb][64:128, n * 512:(n + 1) * 512],
                PSB[0:64, pb_ * 512:(pb_ + 1) * 512]),
                reads=(bB(pb_),), writes=(f"kf{kb}",))
    def _gen_b1(h):
        g, hh = h // 4, h % 4
        gb = g % 2
        qb = hb = h % 2
        bld.op("act", lambda e: e.copy(RTA[0:32, :], QSTG[32:64, :]),
               reads=("qstg",), writes=("rta", "q2k"))
        bld.op("act", lambda e: e.copy(RTA[32:64, :], QSTG[0:32, :]),
               reads=("qstg",), writes=("rta", "q2k"))
        bld.op("dve", lambda e: e.tensor_tensor(RTB, RTA, SINQ, ALU.mult),
               reads=("rta", "sinq"), writes=("rtb",))
        bld.op("dve", lambda e: e.tensor_tensor(RTA, QSTG[0:64, :], COSQ, ALU.mult),
               reads=("qstg", "cosq", "rtb"), writes=("rta", "q2k"))
        bld.op("dve", lambda e, qb=qb: e.tensor_add(QH[qb][0:64, :], RTA, RTB),
               reads=("rta", "rtb"), writes=(f"qh{qb}",))
        bld.op("act", lambda e, qb=qb: e.copy(QH[qb][64:128, :], QSTG[64:128, :]),
               reads=("qstg",), writes=(f"qh{qb}",))
        bld.op("act", lambda e: e.activation(Q2K[:, 0:1024], QSTG, AF.Square),
               reads=("qstg",), writes=("q2k", "rta"))

    def _gen_b2(h):
        g, hh = h // 4, h % 4
        gb = g % 2
        qb = hb = h % 2
        for j in range(8):
            mm(PSB[:, 1536 + 8 * hb + j:1537 + 8 * hb + j], (bB(3),),
               Q2K[:, j * 128:(j + 1) * 128], ONESB,
               ("q2k", "onesb"), True, True)
        bld.op("dve", lambda e, gb=gb, hh=hh, hb=hb: e.tensor_tensor(
            KB2[:, hb:hb + 1], F2G[:, 4 * gb + hh:4 * gb + hh + 1], BND[:, 1:2],
            ALU.mult),
            reads=(f"f2g{gb}", "bnd"), writes=(f"kb2{hb}",))
        bld.op("dve", lambda e, hb=hb: e.tensor_add(
            KB2[:, hb:hb + 1], KB2[:, hb:hb + 1], BND[:, 0:1]),
            reads=(f"kb2{hb}", "bnd"), writes=(f"kb2{hb}",))
        t8v = T8[:, 8 * hb:8 * hb + 8]
        bld.op("dve", lambda e, hb=hb, t8v=t8v: e.tensor_scalar(
            t8v, PSB[:, 1536 + 8 * hb:1544 + 8 * hb], KB2[:, hb:hb + 1], None,
            ALU.mult),
            reads=(bB(3), f"kb2{hb}"), writes=(f"t8{hb}",))
        bld.op("act", lambda e, t8v=t8v: e.activation(t8v, t8v, AF.Ln, bias=CT[:, 2:3]),
               reads=(f"t8{hb}",), writes=(f"t8{hb}",))
        bld.op("act", lambda e, t8v=t8v: e.activation(t8v, t8v, AF.Exp, scale=0.5),
               reads=(f"t8{hb}",), writes=(f"t8{hb}",))
        bld.op("dve", lambda e, hb=hb, t8v=t8v: e.tensor_scalar(
            BIAS8[:, 8 * hb:8 * hb + 8], t8v, -SOFTMAX_SCALE, MARGIN,
            ALU.mult, ALU.add),
            reads=(f"t8{hb}",), writes=(f"bias8{hb}",))

    def _att_units(h):
        g, hh = h // 4, h % 4
        qb = kb = hb = h % 2
        pending = []
        for ui, (j, klo, khi) in enumerate(units):
            rg = ui % 4
            reg = PSA[:, rg * 512:(rg + 1) * 512]
            rtags = (bA(rg),)
            n = khi - klo
            mm(reg[:, 0:n], rtags,
               QH[qb][:, j * 128:(j + 1) * 128],
               KF[kb][:, klo:khi],
               (f"qh{qb}", f"kf{kb}"), True, True)
            while len(pending) > 2:
                pending.pop(0)()
            last = (khi == LJ[j] * 128)
            if last:
                moff = n - 256
                bld.op("dve", lambda e, reg=reg, moff=moff: e.tensor_tensor(
                    reg[:, moff:moff + 256], reg[:, moff:moff + 256],
                    MASKS, ALU.add),
                    reads=rtags + ("masks",), writes=rtags)
            uoff = OFFJ[j] + klo
            uix = 24 * hb + ui
            bld.op("act", lambda e, reg=reg, n=n, uoff=uoff, j=j, uix=uix, hb=hb:
                   e.activation(PBUF[:, uoff:uoff + n], reg[:, 0:n], AF.Exp,
                                bias=BIAS8[:, 8 * hb + j:8 * hb + j + 1],
                                scale=SOFTMAX_SCALE,
                                accum_out=LROW[:, uix:uix + 1]),
                   reads=rtags + (f"bias8{hb}",), writes=(f"pb{j}", f"lrow{uix}"))
            if last:
                u0 = 24 * hb + ufirst[j]
                for ux in range(u0, uix):
                    bld.op("dve", lambda e, uix=uix, ux=ux: e.tensor_add(
                        LROW[:, uix:uix + 1], LROW[:, uix:uix + 1],
                        LROW[:, ux:ux + 1]),
                        reads=(f"lrow{uix}", f"lrow{ux}"), writes=(f"lrow{uix}",))
                jx = 8 * hb + j
                bld.op("dve", lambda e, jx=jx, uix=uix: e.reciprocal(
                    LINV[:, jx:jx + 1], LROW[:, uix:uix + 1]),
                    reads=(f"lrow{uix}",), writes=(f"linv{jx}",))
                bld.op("dve", lambda e, j=j, jx=jx: e.tensor_single_scalar(
                    PBUF[:, OFFJ[j]:OFFJ[j] + LJ[j] * 128],
                    PBUF[:, OFFJ[j]:OFFJ[j] + LJ[j] * 128],
                    LINV[:, jx:jx + 1], ALU.mult),
                    reads=(f"pb{j}", f"linv{jx}"), writes=(f"pb{j}",))

                def _mk_tr(j=j):
                    def go():
                        c0 = min(LJ[j], 8)
                        ptv = PSB[:, 0:512].bitcast(BF16)
                        for kt in range(c0):
                            bld.op("pe", lambda e, j=j, kt=kt, ptv=ptv: e.transpose(
                                ptv[:, kt * 128:(kt + 1) * 128],
                                PBUF[:, OFFJ[j] + kt * 128:OFFJ[j] + (kt + 1) * 128],
                                IDENT),
                                reads=(f"pb{j}", "ident"), writes=(bB(0),))
                        ev_eng = "dve" if (j % 2 == 0) else "act"

                        def _ev0(e, j=j, c0=c0, ptv=ptv):
                            s_ = ptv.rearrange("p (k c) -> p k c", c=128)[:, 0:c0, :]
                            d_ = PT4[:, 0:c0, j, :]
                            return (e.tensor_copy(d_, s_) if hasattr(e, "tensor_copy")
                                    else e.copy(d_, s_))
                        bld.op(ev_eng, _ev0, reads=(bB(0),), writes=(f"ptg{j}",))
                    return go
                pending.append(_mk_tr())
        for fn_ in pending:
            fn_()

    def _att_av(h):
        g, hh = h // 4, h % 4
        qb = kb = hb = h % 2
        for kt in range(8):
            j0 = kt // 2
            rd = tuple(f"ptg{j}" for j in range(j0, 8)) + ("v4",)
            lhs = V43[:, kt, hh * 128:(hh + 1) * 128]
            base = kt * 1024
            if j0 * 128 < 512:
                mm(PSB[:, 1024 + j0 * 128:1536], (bB(2),),
                   lhs, PT[:, base + j0 * 128:base + 512], rd, kt == 0, kt == 7)
            mm(PSB[:, 1536:2048], (bB(3),),
               lhs, PT[:, base + 512:base + 1024], rd, kt == 0, False)
        for j in range(4, 8):
            c1 = LJ[j] - 8
            ptv = PSB[:, 0:512].bitcast(BF16)
            for kt in range(c1):
                bld.op("pe", lambda e, j=j, kt=kt, ptv=ptv: e.transpose(
                    ptv[:, kt * 128:(kt + 1) * 128],
                    PBUF[:, OFFJ[j] + (8 + kt) * 128:OFFJ[j] + (9 + kt) * 128],
                    IDENT),
                    reads=(f"pb{j}", "ident"), writes=(bB(0),))
            ev_eng = "dve" if (j % 2 == 0) else "act"

            def _ev1(e, j=j, c1=c1, ptv=ptv):
                s_ = ptv.rearrange("p (k c) -> p k c", c=128)[:, 0:c1, :]
                d_ = PT4[:, 0:c1, j, :]
                return (e.tensor_copy(d_, s_) if hasattr(e, "tensor_copy")
                        else e.copy(d_, s_))
            bld.op(ev_eng, _ev1, reads=(bB(0),), writes=(f"ptg{j}",))
        for kt in range(8, 16):
            j0 = kt // 2
            rd = tuple(f"ptg{j}" for j in range(j0, 8)) + ("v4",)
            lhs = V43[:, kt, hh * 128:(hh + 1) * 128]
            base = (kt - 8) * 1024
            mm(PSB[:, 1024 + j0 * 128:2048], (bB(3),),
               lhs, PT[:, base + j0 * 128:base + 1024], rd, False, kt == 15)
        bld.op("dve", lambda e: e.tensor_copy(ATTS, PSB[:, 1024:2048]),
               reads=(bB(2), bB(3)), writes=("atts",))
        bld.op("act", lambda e, h=h: e.dma_start(out=attn_d[:, h, :], in_=ATTS),
               reads=("atts",), writes=(f"attnd{h}",), dma=True)
        if debug and h == 0:
            bld.op("act", lambda e: e.dma_start(out=dbg["qh"], in_=QH[0]),
                   reads=("qh0",), dma=True)
            bld.op("act", lambda e: e.dma_start(out=dbg["kf"], in_=KF[0]),
                   reads=("kf0",), dma=True)
            bld.op("act", lambda e: e.dma_start(out=dbg["pb"], in_=PBUF),
                   reads=tuple(f"pb{j}" for j in range(8)), dma=True)

    _kv_fetch(0)
    _wqb_fetch(0)
    bld.barrier()
    bld.op("sp", lambda e: e.dma_start(out=MASKS, in_=masks), writes=("masks",), dma=True)
    bld.op("sp", lambda e: e.dma_start(out=COSQ, in_=cosq), writes=("cosq",), dma=True)
    bld.op("sp", lambda e: e.dma_start(out=SINQ, in_=sinq), writes=("sinq",), dma=True)
    for kb in range(2):
        bld.op("pool", lambda e, kb=kb: e.tensor_copy(KF[kb][0:64, :], KPE),
               reads=("kpe",), writes=(f"kf{kb}",))
    _vgen(0)
    _gen_a(0)
    _gen_b1(0)
    _gen_b2(0)
    for h in range(32):
        nh = h + 1
        boundary = (nh < 32 and nh % 4 == 0)
        if nh < 32 and not boundary:
            _gen_a(nh)
        _att_units(h)
        if nh < 32 and not boundary:
            _gen_b1(nh)
        _att_av(h)
        if nh < 32:
            if boundary:
                _vgen(nh // 4)
                _gen_a(nh)
                _gen_b1(nh)
            _gen_b2(nh)
    if debug:
        bld.op("sp", lambda e: e.dma_start(out=dbg["at"], in_=attn_d),
               reads=tuple(f"attnd{h}" for h in range(32)), dma=True)
    bld.barrier()


def _p4(nc, bld, env):
    carve, mm, PSA, PSB, bA, bB = (env["carve"], env["mm"], env["PSA"], env["PSB"],
                                   env["bA"], env["bB"])
    woP, attn_d, o_d = env["woP"], env["attn_d"], env["o_d"]
    debug, dbg = env["debug"], env["dbg"]
    ATTN = carve(0, 128, 32 * 1024, BF16)
    at3 = ATTN.rearrange("p (h t) -> p h t", t=1024)
    WO_ST = [carve(64 * 1024, 128, 2048, F32), carve(72 * 1024, 128, 2048, F32)]
    WO_BF = [carve(80 * 1024, 128, 4096, BF16), carve(88 * 1024, 128, 4096, BF16)]
    OSTG = carve(96 * 1024, 128, 1024, F32)

    for q in range(4):
        bld.op("sp", lambda e, q=q: e.dma_start(
            out=at3[:, q * 8:(q + 1) * 8, :], in_=attn_d[:, q * 8:(q + 1) * 8, :]),
            reads=tuple(f"attnd{h2}" for h2 in range(q * 8, q * 8 + 8)),
            writes=("attn",), dma=True)
    def _wo_fetch(oc):
        ob = oc % 2
        for hf in range(2):
            dq = "sp" if hf == 0 else "act"
            bld.op(dq, lambda e, oc=oc, hf=hf: e.dma_start(
                out=WO_ST[hf], in_=woP[:, oc, hf * 2048:(hf + 1) * 2048]),
                reads=(f"wost{hf}",), writes=(f"wost{hf}",), dma=True)
            eng = "act" if hf == 0 else "dve"
            def _woc(e, hf=hf, ob=ob):
                d_ = WO_BF[ob][:, hf * 2048:(hf + 1) * 2048]
                return e.copy(d_, WO_ST[hf]) if hf == 0 else e.tensor_copy(d_, WO_ST[hf])
            bld.op(eng, _woc, reads=(f"wost{hf}",), writes=(f"wob{ob}",))
    _wo_fetch(0)
    for oc in range(32):
        ob = oc % 2
        if oc + 1 < 32:
            _wo_fetch(oc + 1)
        ps = PSA[:, 0:1024] if ob == 0 else PSA[:, 1024:2048]
        pbt = (bA(0), bA(1)) if ob == 0 else (bA(2), bA(3))
        for hv in range(32):
            for hf in range(2):
                mm(ps[:, hf * 512:(hf + 1) * 512], (pbt[hf],),
                   WO_BF[ob][:, hv * 128:(hv + 1) * 128],
                   at3[:, hv, hf * 512:(hf + 1) * 512],
                   (f"wob{ob}", "attn"), hv == 0, hv == 31)
        bld.op("act", lambda e, ps=ps: e.copy(OSTG, ps), reads=pbt, writes=("ostg",))
        bld.op("act", lambda e, oc=oc: e.dma_start(out=o_d[:, oc, :], in_=OSTG),
               reads=("ostg",), writes=(f"od{oc}",), dma=True)
    if debug:
        bld.op("sp", lambda e: e.dma_start(out=dbg["od"], in_=o_d),
               reads=tuple(f"od{oc}" for oc in range(32)), dma=True)
    bld.barrier()


def _p5(nc, bld, env):
    carve, mm, PSA, PSB, bA, bB = (env["carve"], env["mm"], env["PSA"], env["PSB"],
                                   env["bA"], env["bB"])
    (xP, gateP, upP, downP, o_d, outP, XN, SCAL, DSEFF) = (
        env["xP"], env["gateP"], env["upP"], env["downP"], env["o_d"], env["outP"],
        env["XN"], env["SCAL"], env["DSEFF"])
    ICMAX = max(ICB_SPLIT)
    INTERB = carve(0, 128, ICMAX * 512, BF16)
    G_ST = carve(30 * 1024, 128, 4096, F32)
    U_ST = carve(46 * 1024, 128, 4096, F32)
    G_BF = [carve(62 * 1024, 128, 4096, BF16), carve(70 * 1024, 128, 4096, BF16)]
    U_BF = [carve(78 * 1024, 128, 4096, BF16), carve(86 * 1024, 128, 4096, BF16)]
    SILU = carve(94 * 1024, 128, 512, F32)
    D_ST = [carve(30 * 1024, 128, ICMAX * 128, F32),
            carve(45 * 1024, 128, ICMAX * 128, F32)]
    D_BF = [carve(60 * 1024, 128, ICMAX * 128, BF16),
            carve(68 * 1024, 128, ICMAX * 128, BF16)]
    XOC = [carve(76 * 1024, 128, 512, F32), carve(78 * 1024, 128, 512, F32)]
    OOC = [carve(80 * 1024, 128, 512, F32), carve(82 * 1024, 128, 512, F32)]
    RIN = [carve(84 * 1024, 128, 512, F32), carve(86 * 1024, 128, 512, F32)]
    OUTS = [carve(88 * 1024, 128, 512, F32), carve(90 * 1024, 128, 512, F32)]

    def _gu_fetch(ic, ib, after_dma=(), after_cast=()):
        o1 = bld.op("sp", lambda e, ic=ic: e.dma_start(out=G_ST, in_=gateP[:, ic, :]),
                    reads=("gst",), writes=("gst",), dma=True, after=after_dma)
        o2 = bld.op("act", lambda e, ib=ib: e.copy(G_BF[ib], G_ST),
                    reads=("gst",), writes=(f"gbf{ib}",), after=after_cast)
        o3 = bld.op("act", lambda e, ic=ic: e.dma_start(out=U_ST, in_=upP[:, ic, :]),
                    reads=("ust",), writes=("ust",), dma=True, after=after_dma)
        o4 = bld.op("dve", lambda e, ib=ib: e.tensor_copy(U_BF[ib], U_ST),
                    reads=("ust",), writes=(f"ubf{ib}",), after=after_cast)
        return o2, o4

    trans = {"mm": None, "dacts": (), "stt": None, "silu": None}
    for th in range(2):
        for icb in range(3):
            ic0 = sum(ICB_SPLIT[:icb])
            nic = ICB_SPLIT[icb]
            ad = tuple(x for x in trans["dacts"] if x is not None)
            am = (trans["mm"],) if trans["mm"] is not None else ()
            lastg, lastu = _gu_fetch(ic0, 0, after_dma=ad + am, after_cast=am + ad)
            lastmm = None
            lastsilu = None
            laststt = None
            for i in range(nic):
                ic = ic0 + i
                ib = i % 2
                if i + 1 < nic:
                    xa = (am + ad) if i == 0 else ()
                    lastg, lastu = _gu_fetch(ic + 1, (i + 1) % 2,
                                             after_dma=xa, after_cast=xa)
                pg = PSA[:, 0:512] if ib == 0 else PSA[:, 1024:1536]
                pu = PSA[:, 512:1024] if ib == 0 else PSA[:, 1536:2048]
                tg = bA(0) if ib == 0 else bA(2)
                tu = bA(1) if ib == 0 else bA(3)
                af = am if i == 0 else ()
                for dc in range(32):
                    mm(pg, (tg,), G_BF[ib][:, dc * 128:(dc + 1) * 128],
                       XN[:, dc * 1024 + th * 512:dc * 1024 + (th + 1) * 512],
                       (f"gbf{ib}", "xn"), dc == 0, dc == 31)
                    lastmm = mm(pu, (tu,), U_BF[ib][:, dc * 128:(dc + 1) * 128],
                                XN[:, dc * 1024 + th * 512:dc * 1024 + (th + 1) * 512],
                                (f"ubf{ib}", "xn"), dc == 0, dc == 31)
                lastsilu = bld.op("act", lambda e, pg=pg, tg=tg: e.activation(
                    SILU, pg, AF.Silu, scale=SCAL[:, 0:1]),
                    reads=(tg, "scal"), writes=("silu",))
                laststt = bld.op("dve", lambda e, pu=pu, i=i: e.scalar_tensor_tensor(
                    INTERB[:, i * 512:(i + 1) * 512], SILU, DSEFF, pu,
                    ALU.mult, ALU.mult),
                    reads=("silu", "dseff", tu), writes=("inter",), after=af)
            def _dn_dma(oc, ic0=ic0, nic=nic, after=()):
                ob = oc % 2
                bld.op("sp", lambda e, oc=oc, ob=ob: e.dma_start(
                    out=D_ST[ob][:, 0:nic * 128],
                    in_=downP[:, oc, ic0 * 128:(ic0 + nic) * 128]),
                    reads=(f"dst{ob}",), writes=(f"dst{ob}",), dma=True, after=after)

            dn_acts = []

            def _dn_fetch(oc, th=th, icb=icb, ic0=ic0, nic=nic, after=()):
                ob = oc % 2
                nh = (nic * 128) // 2
                dn_acts.append(bld.op("act", lambda e, ob=ob, nh=nh: e.copy(
                    D_BF[ob][:, 0:nh], D_ST[ob][:, 0:nh]),
                    reads=(f"dst{ob}",), writes=(f"dbf{ob}",), after=after))
                dn_acts.append(bld.op("dve", lambda e, ob=ob, nh=nh: e.tensor_copy(
                    D_BF[ob][:, nh:nic * 128], D_ST[ob][:, nh:nic * 128]),
                    reads=(f"dst{ob}",), writes=(f"dbf{ob}",), after=after))
                if icb == 0:
                    xoc3 = XOC[ob].rearrange("p (s t) -> p s t", t=128)
                    bld.op("act", lambda e, oc=oc, xoc3=xoc3: e.dma_start(
                        out=xoc3,
                        in_=xP[:, oc, th * 1024:(th + 1) * 1024].rearrange(
                            "p (s t) -> p s t", t=256)[:, :, 0:128]),
                        reads=(f"xoc{ob}",), writes=(f"xoc{ob}",), dma=True, after=after)
                    bld.op("act", lambda e, oc=oc, ob=ob: e.dma_start(
                        out=OOC[ob], in_=o_d[:, oc, th * 512:(th + 1) * 512]),
                        reads=(f"ooc{ob}", f"od{oc}"), writes=(f"ooc{ob}",), dma=True,
                        after=after)
                else:
                    bld.op("act", lambda e, oc=oc, ob=ob: e.dma_start(
                        out=RIN[ob], in_=outP[:, oc, th * 512:(th + 1) * 512]),
                        reads=(f"rin{ob}", f"out_{oc}_{th}"), writes=(f"rin{ob}",),
                        dma=True, after=after)

            _dn_dma(0, after=(lastg, lastu))
            _dn_dma(1, after=(lastg, lastu))
            _dn_fetch(0, after=(lastmm, lastsilu, laststt))
            out_slots = {}
            for oc in range(32):
                ob = oc % 2
                if oc + 2 < 32:
                    _dn_dma(oc + 2)
                if oc + 1 < 32:
                    _dn_fetch(oc + 1, after=(lastmm,) if oc == 0 else ())
                ps = PSB[:, 0:512] if ob == 0 else PSB[:, 512:1024]
                pt = bB(0) if ob == 0 else bB(1)
                for i in range(nic):
                    lastmm = mm(ps, (pt,), D_BF[ob][:, i * 128:(i + 1) * 128],
                                INTERB[:, i * 512:(i + 1) * 512],
                                (f"dbf{ob}", "inter"), i == 0, i == nic - 1)
                if icb == 0:
                    bld.op("dve", lambda e, ps=ps, ob=ob: e.tensor_tensor(
                        OUTS[ob], ps, OOC[ob], ALU.add),
                        reads=(pt, f"ooc{ob}", f"outs{ob}"), writes=(f"outs{ob}",)),
                    bld.op("dve", lambda e, ob=ob: e.tensor_tensor(
                        OUTS[ob], OUTS[ob], XOC[ob], ALU.add),
                        reads=(f"outs{ob}", f"xoc{ob}"), writes=(f"outs{ob}",))
                else:
                    bld.op("dve", lambda e, ps=ps, ob=ob: e.tensor_tensor(
                        OUTS[ob], ps, RIN[ob], ALU.add),
                        reads=(pt, f"rin{ob}", f"outs{ob}"), writes=(f"outs{ob}",))
                oo = bld.op("act", lambda e, oc=oc, th=th, ob=ob: e.dma_start(
                    out=outP[:, oc, th * 512:(th + 1) * 512], in_=OUTS[ob]),
                    reads=(f"outs{ob}",), writes=(f"out_{oc}_{th}",), dma=True)
                out_slots[oo.dsem] = oo
            coll = bld.op("sp", lambda e: e.nop(),
                          after=tuple(dn_acts[-4:]) + tuple(out_slots.values()))
            trans = {"mm": lastmm, "dacts": (coll,),
                     "stt": laststt, "silu": lastsilu}
            dn_acts.clear()


def kernel(**inputs):
    pk = _pack_shared(inputs)
    cos, sin = _yarn_cos_sin()
    nc = _build(DEBUG, PHASES)
    in_maps = []
    for c in range(8):
        b, p = c // 2, c % 2
        m = dict(pk)
        m.update(_core_pack(inputs, cos, sin, b, p))
        in_maps.append(m)
    res = run_bass_kernel_spmd(nc, in_maps, core_ids=list(range(8)))
    out = np.empty((B, S, D), np.float32)
    for c in range(8):
        b, p = c // 2, c % 2
        O = np.asarray(res.results[c]["outP"])
        t = O.reshape(128, 32, 8, 128).transpose(2, 3, 1, 0).reshape(8, 128, 4096)
        ov = out[b].reshape(16, 128, 4096)
        for j in range(8):
            ov[2 * j + p] = t[j]
    return out


# version 23 (final)
# speedup vs baseline: 1.2232x; 1.0330x over previous
import math

import numpy as np

import concourse.bass as bass
import concourse.mybir as mybir
from concourse.bass_utils import run_bass_kernel_spmd

F32 = mybir.dt.float32
BF16 = mybir.dt.bfloat16
AF = mybir.ActivationFunctionType
ALU = mybir.AluOpType
AXX = mybir.AxisListType.X

H = 32
NOPE = 64
VD = 128
QHD = 128
KVR = 256
D = 4096
S = 2048
B = 4
INTER = 11008
NIC = INTER // 128
ICB_SPLIT = [18, 17, 17, 17, 17]
EPS_LN = 1e-5
EPS_RMS = 1e-6
_m = 0.1 * 1.0 * math.log(128.0) + 1.0
SOFTMAX_SCALE = (QHD ** -0.5) * _m * _m
MARGIN = 55.0
NEG = -1.0e9

NQ = 1024
NSLOT = 8
LJ = [2 * (j + 1) for j in range(NSLOT)]
OFFJ = [j * (j + 1) * 128 for j in range(NSLOT)]
PTOT = OFFJ[-1] + LJ[-1] * 128


def _yarn_cos_sin(seq_len=S, dim=64, base=10000.0, factor=128.0, beta_fast=32.0,
                  beta_slow=1.0, orig_max=8192, mscale=1.0, mscale_all_dim=1.0):
    def corr_dim(r):
        return dim * math.log(orig_max / (r * 2 * math.pi)) / (2 * math.log(base))
    low = max(math.floor(corr_dim(beta_fast)), 0)
    high = min(math.ceil(corr_dim(beta_slow)), dim - 1)
    hi = high + 0.001 if low == high else float(high)
    ramp = np.clip((np.arange(dim // 2, dtype=np.float32) - low) / (hi - low), 0.0, 1.0)
    inv_freq_mask = 1.0 - ramp
    freq_extra = 1.0 / base ** (np.arange(0, dim, 2, dtype=np.float32) / dim)
    freq_inter = freq_extra / factor
    inv_freq = freq_inter * (1.0 - inv_freq_mask) + freq_extra * inv_freq_mask
    t = np.arange(seq_len, dtype=np.float32)
    freqs = np.outer(t, inv_freq)
    emb = np.concatenate([freqs, freqs], axis=-1)

    def gm(s_, m_):
        return 1.0 if s_ <= 1 else 0.1 * m_ * math.log(s_) + 1.0
    ms = gm(factor, mscale) / gm(factor, mscale_all_dim)
    return (np.cos(emb) * ms).astype(np.float32), (np.sin(emb) * ms).astype(np.float32)


def _perm_tiles(p):
    order = []
    for j in range(8):
        order.append(2 * j + p)
        order.append(2 * j + 1 - p)
    return order


def _pack_shared(inp):
    pk = {}
    A = np.ascontiguousarray(np.asarray(inp["L0_attn_wq_a"], np.float32).T).reshape(32, 128, 8, 128)
    pk["wqaP"] = np.ascontiguousarray(A.transpose(1, 2, 0, 3).reshape(128, 8, 4096))
    perm = np.empty(4096, dtype=np.int64)
    for h in range(H):
        b0 = h * 128
        perm[b0:b0 + 32] = b0 + 64 + 2 * np.arange(32)
        perm[b0 + 32:b0 + 64] = b0 + 64 + 2 * np.arange(32) + 1
        perm[b0 + 64:b0 + 128] = b0 + np.arange(64)
    W = np.asarray(inp["L0_attn_wq_b"], np.float32)[perm].reshape(32, 128, 8, 128)
    pk["wqbP"] = np.ascontiguousarray(W.transpose(3, 0, 2, 1).reshape(128, 32, 1024))
    permk = np.concatenate([np.arange(256), 256 + 2 * np.arange(32),
                            256 + 2 * np.arange(32) + 1])
    A = np.asarray(inp["L0_attn_wkv_a"], np.float32)[permk].reshape(320, 32, 128)
    pk["wkvaP"] = np.ascontiguousarray(A.transpose(2, 1, 0).reshape(128, 32 * 320))
    Wb = np.asarray(inp["L0_attn_wkv_b"], np.float32).reshape(H, NOPE + VD, KVR)
    K = Wb[:, 0:NOPE, :].reshape(H, NOPE, 2, 128)
    pk["wkvbkP"] = np.ascontiguousarray(K.transpose(3, 0, 2, 1).reshape(128, 32, 128))
    V = Wb[:, NOPE:, :].reshape(8, 4, VD, 2, 128)
    pk["wkvbvP"] = np.ascontiguousarray(V.transpose(4, 0, 3, 1, 2).reshape(128, 8, 1024))
    W = np.ascontiguousarray(np.asarray(inp["L0_attn_wo"], np.float32).T).reshape(32, 128, 32, 128)
    pk["woP"] = np.ascontiguousarray(W.transpose(1, 2, 0, 3).reshape(128, 32, 4096))
    G = np.ascontiguousarray(np.asarray(inp["L0_ffn_gate_w"], np.float32).T).reshape(32, 128, NIC, 128)
    pk["gateP"] = np.ascontiguousarray(G.transpose(1, 2, 0, 3).reshape(128, NIC, 4096))
    U = np.ascontiguousarray(np.asarray(inp["L0_ffn_up_w"], np.float32).T).reshape(32, 128, NIC, 128)
    pk["upP"] = np.ascontiguousarray(U.transpose(1, 2, 0, 3).reshape(128, NIC, 4096))
    Dw = np.ascontiguousarray(np.asarray(inp["L0_ffn_down_w"], np.float32).T).reshape(NIC, 128, 32, 128)
    pk["downP"] = np.ascontiguousarray(Dw.transpose(1, 2, 0, 3).reshape(128, 32, INTER))
    pk["qgain"] = np.ascontiguousarray(np.asarray(inp["L0_attn_q_a_ln_w"], np.float32).reshape(8, 128).T)
    pk["kvgain"] = np.ascontiguousarray(np.asarray(inp["L0_attn_kv_a_ln_w"], np.float32).reshape(2, 128).T)
    sc = np.array([float(inp["L0_ffn_gate_scale"]), float(inp["L0_ffn_up_scale"]),
                   float(inp["L0_ffn_down_scale"])], np.float32).reshape(1, 3)
    pk["scal"] = np.ascontiguousarray(np.broadcast_to(sc, (128, 3)).astype(np.float32))
    pk["ident"] = np.eye(128, dtype=np.float32)
    return pk


def _core_pack(inp, cos, sin, b, p):
    d = {}
    order = _perm_tiles(p)
    x = np.asarray(inp["L0_attn_x"][b], np.float32)
    xt = x.reshape(16, 128, 4096)[order].reshape(2048, 4096)
    xt = np.ascontiguousarray(xt.T).reshape(32, 128, S)
    d["xP"] = np.ascontiguousarray(xt.transpose(1, 0, 2))
    posk = np.concatenate([np.arange(t * 128, (t + 1) * 128) for t in order])
    ck = np.ascontiguousarray(cos[posk].T)
    sk = np.ascontiguousarray(sin[posk].T).copy()
    sk[0:32] = -sk[0:32]
    d["cosk"], d["sink"] = ck, sk
    posq = np.concatenate([(2 * j + p) * 128 + np.arange(128) for j in range(NSLOT)])
    cq = np.ascontiguousarray(cos[posq].T)
    sq = np.ascontiguousarray(sin[posq].T).copy()
    sq[0:32] = -sq[0:32]
    d["cosq"], d["sinq"] = cq, sq
    i_ = np.arange(128)[:, None]
    c_ = np.arange(128)[None, :]
    diag = np.where(c_ <= i_, np.float32(0.0), np.float32(NEG))
    oth = np.full((128, 128), 0.0 if p == 1 else NEG, np.float32)
    d["masks"] = np.ascontiguousarray(np.concatenate([diag, oth], axis=1))
    return d


ENGS = ("sp", "pe", "act", "dve", "pool")


class _Op:
    __slots__ = ("eng", "fn", "idx", "need_inc", "deps", "dma", "dsem", "duse")

    def __init__(self, eng, fn, dma, idx):
        self.eng, self.fn, self.dma, self.idx = eng, fn, dma, idx
        self.need_inc = False
        self.deps = []
        self.dsem = -1
        self.duse = 0


NDMASEM = 24


class Bld:
    def __init__(self):
        self.eng_ops = {e: [] for e in ENGS}
        self.last_w = {}
        self.readers = {}
        self.key_last = {}
        self.dma_n = 0
        self.dma_uses = [0] * NDMASEM
        self.dma_last_op = [None] * NDMASEM

    def op(self, eng, fn, reads=(), writes=(), dma=False, after=()):
        o = _Op(eng, fn, eng if dma else None, len(self.eng_ops[eng]))
        if dma:
            o.need_inc = True
            o.dsem = self.dma_n % NDMASEM
            self.dma_n += 1
            self.dma_uses[o.dsem] += 1
            o.duse = self.dma_uses[o.dsem]
        deps, seen = [], set()

        def add(pr):
            if pr is None or pr is o or id(pr) in seen:
                return
            if pr.eng == "pe" and eng == "pe" and pr.dma is None and not dma:
                return
            seen.add(id(pr))
            deps.append(pr)
            pr.need_inc = True

        for pr_ in after:
            add(pr_)
        for t in reads:
            add(self.last_w.get(t))
        for t in writes:
            add(self.last_w.get(t))
            for r in self.readers.get(t, ()):
                add(r)
        for t in writes:
            self.last_w[t] = o
            self.readers[t] = []
        for t in reads:
            self.readers.setdefault(t, []).append(o)
        if dma and o.duse > 1:
            prev = self.dma_last_op[o.dsem]
            if prev is not None and id(prev) not in seen:
                seen.add(id(prev))
                deps.append(prev)
        if dma:
            self.dma_last_op[o.dsem] = o
        o.deps = deps
        self.eng_ops[eng].append(o)
        self.key_last[(eng, bool(dma))] = o
        return o

    def barrier(self):
        lasts = [o for o in self.key_last.values() if o.dma is None]
        lasts += [o for o in self.dma_last_op if o is not None]
        for pr in lasts:
            pr.need_inc = True
        for e in ENGS:
            o = _Op(e, None, None, len(self.eng_ops[e]))
            o.deps = lasts[:]
            self.eng_ops[e].append(o)

    def materialize(self, sems, dsems):
        cnt, val = {}, {}
        for ops in self.eng_ops.values():
            for o in ops:
                if o.dma is not None:
                    val[id(o)] = ("D", o.dsem, o.duse * 16)
                else:
                    key = o.eng
                    if o.need_inc:
                        cnt[key] = cnt.get(key, 0) + 1
                    val[id(o)] = ("E", key, cnt.get(key, 0))
        self.final_counts = {("E", k): v for k, v in cnt.items()}
        for i in range(NDMASEM):
            if self.dma_uses[i]:
                self.final_counts[("D", i)] = self.dma_uses[i] * 16

        def emit(engine, ops):
            water = {}
            for o in ops:
                for pr in o.deps:
                    kind, key, v = val[id(pr)]
                    sem = dsems[key] if kind == "D" else sems[key]
                    wk = (kind, key)
                    if water.get(wk, -1) >= v:
                        continue
                    water[wk] = v
                    engine.wait_ge(sem, v)
                if o.fn is None:
                    continue
                ins = o.fn(engine)
                if o.dma is not None:
                    ins.then_inc(dsems[o.dsem], 16)
                elif o.need_inc:
                    ins.then_inc(sems[o.eng], 1)
        return emit


_CACHE = {}
DEBUG = 0
PHASES = 99


def _build(debug=0, phases=99):
    ck_ = (debug, phases)
    if ck_ in _CACHE:
        return _CACHE[ck_]
    nc = bass.Bass("TRN2", target_bir_lowering=False, debug=False)

    ein = lambda n, s_, dt=F32: nc.dram_tensor(n, s_, dt, kind="ExternalInput").ap()
    xP = ein("xP", [128, 32, S])
    masks = ein("masks", [128, 256])
    cosq, sinq = ein("cosq", [64, NQ]), ein("sinq", [64, NQ])
    cosk, sink = ein("cosk", [64, S]), ein("sink", [64, S])
    qgain, kvgain = ein("qgain", [128, 8]), ein("kvgain", [128, 2])
    scal = ein("scal", [128, 3])
    identf = ein("ident", [128, 128])
    wqaP = ein("wqaP", [128, 8, 4096])
    wqbP = ein("wqbP", [128, 32, 1024])
    wkvaP = ein("wkvaP", [128, 32 * 320])
    wkvbkP = ein("wkvbkP", [128, 32, 128])
    wkvbvP = ein("wkvbvP", [128, 8, 1024])
    woP = ein("woP", [128, 32, 4096])
    gateP = ein("gateP", [128, NIC, 4096])
    upP = ein("upP", [128, NIC, 4096])
    downP = ein("downP", [128, 32, INTER])

    outP = nc.dram_tensor("outP", [128, 32, NQ], F32, kind="ExternalOutput").ap()
    attn_d = nc.dram_tensor("attn_d", [128, 32, NQ], BF16, kind="Internal").ap()
    o_d = nc.dram_tensor("o_d", [128, 32, NQ], F32, kind="Internal").ap()

    dbg = {}
    if debug:
        dd = lambda n, s_, dt: nc.dram_tensor(n, s_, dt, kind="ExternalOutput").ap()
        dbg["xn"] = dd("D_xn", [128, 32 * 1024], BF16)
        dbg["ckv"] = dd("D_ckv", [128, 2 * 2048], BF16)
        dbg["kpe"] = dd("D_kpe", [64, 2048], BF16)
        dbg["cqn"] = dd("D_cqn", [128, 8 * 1024], BF16)
        dbg["qh"] = dd("D_qh", [128, 1024], BF16)
        dbg["kf"] = dd("D_kf", [128, 2048], BF16)
        dbg["pb"] = dd("D_pb", [128, PTOT], BF16)
        dbg["at"] = dd("D_at", [128, 32, NQ], BF16)
        dbg["od"] = dd("D_od", [128, 32, NQ], F32)
        dbg["v4"] = dd("D_v4", [128, 8192], BF16)

    XN = nc.alloc_sbuf_tensor("XN", [128, 32 * 1024], BF16).ap()
    CQN = nc.alloc_sbuf_tensor("CQN", [128, 8 * 1024], BF16).ap()
    CKV = nc.alloc_sbuf_tensor("CKV", [128, 2 * 2048], BF16).ap()
    KPE = nc.alloc_sbuf_tensor("KPE", [64, 2048], BF16).ap()
    GQ = nc.alloc_sbuf_tensor("GQ", [128, 8], F32).ap()
    GKV = nc.alloc_sbuf_tensor("GKV", [128, 2], F32).ap()
    SCAL = nc.alloc_sbuf_tensor("SCAL", [128, 3], F32).ap()
    DSEFF = nc.alloc_sbuf_tensor("DSEFF", [128, 1], F32).ap()
    IDENT = nc.alloc_sbuf_tensor("IDENT", [128, 128], BF16).ap()
    ONESB = nc.alloc_sbuf_tensor("ONESB", [128, 1], BF16).ap()
    ONES128 = nc.alloc_sbuf_tensor("ONES128", [128, 128], BF16).ap()
    ONESF = nc.alloc_sbuf_tensor("ONESF", [1, 128], F32).ap()
    SROW = nc.alloc_sbuf_tensor("SROW", [1, 1536], F32).ap()
    BIAS8 = nc.alloc_sbuf_tensor("BIAS8", [128, 16], F32).ap()
    T8 = nc.alloc_sbuf_tensor("T8", [128, 16], F32).ap()
    LROW = nc.alloc_sbuf_tensor("LROW", [128, 48], F32).ap()
    LINV = nc.alloc_sbuf_tensor("LINV", [128, 16], F32).ap()
    K2MAX = nc.alloc_sbuf_tensor("K2MAX", [128, 2], F32).ap()
    BND = nc.alloc_sbuf_tensor("BND", [128, 2], F32).ap()
    F2G = nc.alloc_sbuf_tensor("F2G", [128, 8], F32).ap()
    KB2 = nc.alloc_sbuf_tensor("KB2", [128, 2], F32).ap()
    CT = nc.alloc_sbuf_tensor("CT", [128, 4], F32).ap()

    ARENA = nc.alloc_sbuf_tensor("ARENA", [128, (106 * 1024) // 4], F32).ap()

    def carve(off_b, part, nfree, dtype):
        assert off_b % 4 == 0
        if dtype == F32:
            v = ARENA[:, off_b // 4: off_b // 4 + nfree]
        else:
            assert nfree % 2 == 0
            v = ARENA[:, off_b // 4: off_b // 4 + nfree // 2].bitcast(BF16)
        if part != 128:
            v = v[0:part, :]
        return v

    PSA = nc.alloc_psum_tensor("PSA", [128, 2048], F32).ap()
    PSB = nc.alloc_psum_tensor("PSB", [128, 2048], F32).ap()
    bA = lambda i: f"A{i}"
    bB = lambda i: f"B{i}"

    bld = Bld()

    def mm(dst, banks, lhsT, rhs, reads, start, stop, after=()):
        return bld.op("pe", lambda e, d=dst, l=lhsT, r=rhs, s1=start, s2=stop:
                      e.matmul(d, l, r, start=s1, stop=s2),
                      reads=tuple(reads), writes=tuple(banks), after=after)

    ident_stage = carve(0, 128, 128, F32)
    bld.op("sp", lambda e: e.dma_start(out=GQ, in_=qgain), writes=("gq",), dma=True)
    bld.op("sp", lambda e: e.dma_start(out=GKV, in_=kvgain), writes=("gkv",), dma=True)
    bld.op("sp", lambda e: e.dma_start(out=SCAL, in_=scal), writes=("scal",), dma=True)
    bld.op("sp", lambda e: e.dma_start(out=ident_stage, in_=identf), writes=("idst",), dma=True)
    bld.op("dve", lambda e: e.tensor_copy(IDENT, ident_stage), reads=("idst",), writes=("ident",))
    bld.op("dve", lambda e: e.tensor_mul(DSEFF, SCAL[:, 1:2], SCAL[:, 2:3]),
           reads=("scal",), writes=("dseff",))
    bld.op("pool", lambda e: e.memset(ONESB, 1.0), writes=("onesb",))
    bld.op("pool", lambda e: e.memset(ONES128, 1.0), writes=("ones128",))
    bld.op("pool", lambda e: e.memset(ONESF, 1.0), writes=("onesf",))
    bld.op("pool", lambda e: e.memset(CT[:, 0:1], EPS_LN), writes=("constv",))
    bld.op("pool", lambda e: e.memset(CT[:, 1:2], EPS_RMS), writes=("constv",))
    bld.op("pool", lambda e: e.memset(CT[:, 2:3], 1e-20), writes=("constv",))
    bld.barrier()
    xstage = carve(0, 128, 8192, F32)
    XBFF = [carve(32 * 1024, 128, 8192, BF16), carve(48 * 1024, 128, 8192, BF16)]
    wkva_bf = carve(64 * 1024, 128, 32 * 320, BF16)
    ckt = [[carve(86 * 1024 + (2 * i + k) * 1024, 64, 256, F32) for k in range(2)]
           for i in range(2)]
    rt1 = carve(90 * 1024, 64, 256, F32)
    rt2 = carve(91 * 1024, 64, 256, F32)
    tsc = carve(92 * 1024, 128, 1024, F32)
    ckf = carve(96 * 1024, 128, 512, F32)
    rbs = [carve(98 * 1024, 128, 256, F32), carve(100 * 1024, 128, 256, F32)]
    mrbs = [carve(99 * 1024, 128, 256, F32), carve(101 * 1024, 128, 256, F32)]
    sq2 = carve(102 * 1024, 128, 512, BF16)
    sqr = [carve(103 * 1024, 128, 512, BF16), carve(104 * 1024, 128, 512, BF16)]
    wkva_st = carve(32 * 1024, 128, 2560, F32)

    for pc in range(4):
        bld.op("sp", lambda e, pc=pc: e.dma_start(
            out=wkva_st, in_=wkvaP[:, pc * 2560:(pc + 1) * 2560]),
            reads=("wkvast",), writes=("wkvast",), dma=True)
        bld.op("dve", lambda e, pc=pc: e.tensor_copy(
            wkva_bf[:, pc * 2560:(pc + 1) * 2560], wkva_st),
            reads=("wkvast",), writes=("wkva",))

    xs3 = xstage.rearrange("p (d t) -> p d t", t=256)
    XN3 = XN.rearrange("p (d t) -> p d t", t=1024)
    r_m = SROW[:, 0:256]
    r_s2 = SROW[:, 256:512]
    r_v = SROW[:, 512:768]
    r_r = SROW[:, 768:1024]
    r_mr = SROW[:, 1024:1280]
    r_k = SROW[:, 1280:1536]

    def _p1_dma(c):
        for half in range(2):
            bld.op("sp", lambda e, c=c, half=half: e.dma_start(
                out=xs3[:, half * 16:(half + 1) * 16, :],
                in_=xP[:, half * 16:(half + 1) * 16, c * 256:(c + 1) * 256]),
                reads=(f"xs{half}",), writes=(f"xs{half}",), dma=True)

    def _ckt_dma(c):
        cb = c % 2
        bld.op("sp", lambda e, c=c, cb=cb: e.dma_start(
            out=ckt[cb][0], in_=cosk[:, c * 256:(c + 1) * 256]),
            reads=(f"ckt{cb}",), writes=(f"ckt{cb}",), dma=True)
        bld.op("sp", lambda e, c=c, cb=cb: e.dma_start(
            out=ckt[cb][1], in_=sink[:, c * 256:(c + 1) * 256]),
            reads=(f"ckt{cb}",), writes=(f"ckt{cb}",), dma=True)

    def _p1_s(c, first=False):
        pb = c % 2
        xb = XBFF[pb]
        for g in range(8):
            wt = (f"xb{pb}g{g}", "wkvast") if (first and g < 5) else (f"xb{pb}g{g}",)
            bld.op("act", lambda e, g=g, xb=xb: e.copy(
                xb[:, g * 1024:(g + 1) * 1024], xstage[:, g * 1024:(g + 1) * 1024]),
                reads=(f"xs{0 if g < 4 else 1}",), writes=wt)
            for hh in range(2):
                d2 = 2 * g + hh
                gb = d2 % 2
                bld.op("act", lambda e, d2=d2, gb=gb, xb=xb: e.activation(
                    sqr[gb], xb[:, d2 * 512:(d2 + 1) * 512], AF.Square),
                    reads=(f"xb{pb}g{g}",), writes=(f"sq{gb}",))
                for q in range(2):
                    dc = 2 * d2 + q
                    mm(PSB[0:1, 0:256], (bB(0),),
                       ONESB, xb[:, dc * 256:(dc + 1) * 256],
                       ("onesb", f"xb{pb}g{g}"), dc == 0, dc == 31)
                    mm(PSB[0:1, 512:768], (bB(1),),
                       ONESB, sqr[gb][:, q * 256:(q + 1) * 256],
                       ("onesb", f"sq{gb}"), dc == 0, dc == 31)
        if c + 1 < 8:
            _p1_dma(c + 1)
        bld.op("dve", lambda e: e.tensor_scalar_mul(r_m, PSB[0:1, 0:256], 1.0 / D),
               reads=(bB(0),), writes=("rm",))
        bld.op("dve", lambda e: e.tensor_scalar_mul(r_s2, PSB[0:1, 512:768], 1.0 / D),
               reads=(bB(1),), writes=("rs2",))
        bld.op("dve", lambda e: e.tensor_mul(r_v, r_m, r_m), reads=("rm",), writes=("rv",))
        bld.op("dve", lambda e: e.tensor_sub(r_v, r_s2, r_v), reads=("rs2", "rv"), writes=("rv",))
        bld.op("act", lambda e: e.activation(r_v, r_v, AF.Ln, bias=CT[0:1, 0:1]),
               reads=("rv",), writes=("rv",))
        bld.op("act", lambda e: e.activation(r_r, r_v, AF.Exp, scale=-0.5),
               reads=("rv",), writes=("rr",))
        bld.op("dve", lambda e: e.tensor_mul(r_mr, r_m, r_r), reads=("rm", "rr"), writes=("rmr",))
        mm(PSB[:, 1024:1280], (bB(2),), ONESF, r_r, ("onesf", "rr"), True, True)
        mm(PSB[:, 1536:1792], (bB(3),), ONESF, r_mr, ("onesf", "rmr"), True, True)
        bld.op("act", lambda e, pb=pb: e.copy(rbs[pb], PSB[:, 1024:1280]),
               reads=(bB(2),), writes=(f"rbs{pb}",))
        bld.op("act", lambda e, pb=pb: e.copy(mrbs[pb], PSB[:, 1536:1792]),
               reads=(bB(3),), writes=(f"mrbs{pb}",))

    def _p1_x(c):
        pb = c % 2
        cb = c % 2
        if c + 1 < 8:
            _ckt_dma(c + 1)
        xb = XBFF[pb]
        xb3 = xb.rearrange("p (d t) -> p d t", t=256)
        for g in range(8):
            xg3 = xb3[:, 4 * g:4 * (g + 1), :]
            t3 = tsc.rearrange("p (d t) -> p d t", t=256)
            rb_b = rbs[pb].unsqueeze(1).broadcast_to((128, 4, 256))
            mrb_b = mrbs[pb].unsqueeze(1).broadcast_to((128, 4, 256))
            bld.op("pool", lambda e, xg3=xg3, rb_b=rb_b, t3=t3:
                   e.tensor_tensor(t3, xg3, rb_b, ALU.mult),
                   reads=(f"xb{pb}g{g}", f"rbs{pb}"), writes=("tsc",))
            bld.op("dve", lambda e, xg3=xg3, mrb_b=mrb_b, t3=t3:
                   e.tensor_sub(xg3, t3, mrb_b),
                   reads=("tsc", f"mrbs{pb}"), writes=(f"xb{pb}g{g}",))
            bld.op("dve", lambda e, g=g, c=c, xg3=xg3:
                   e.tensor_copy(XN3[:, 4 * g:4 * (g + 1), c * 128:(c + 1) * 128],
                                 xg3[:, :, 0:128]),
                   reads=(f"xb{pb}g{g}",), writes=("xn",))
            for dd_ in range(4):
                dc = 4 * g + dd_
                rhs = xg3[:, dd_, :]
                mm(PSA[:, 1024:1280], (bA(2),),
                   wkva_bf[:, dc * 320:dc * 320 + 128], rhs,
                   ("wkva", f"xb{pb}g{g}"), dc == 0, dc == 31)
                mm(PSA[:, 1536:1792], (bA(3),),
                   wkva_bf[:, dc * 320 + 128:dc * 320 + 256], rhs,
                   ("wkva", f"xb{pb}g{g}"), dc == 0, dc == 31)
                mm(PSA[0:64, 512:768], (bA(1),),
                   wkva_bf[:, dc * 320 + 256:dc * 320 + 320], rhs,
                   ("wkva", f"xb{pb}g{g}"), dc == 0, dc == 31)
        bld.op("act", lambda e: e.activation(sq2[:, 0:256], PSA[:, 1024:1280], AF.Square),
               reads=(bA(2),), writes=("sq2",))
        bld.op("act", lambda e: e.activation(sq2[:, 256:512], PSA[:, 1536:1792], AF.Square),
               reads=(bA(3),), writes=("sq2",))
        mm(PSA[0:1, 0:256], (bA(0),), ONESB, sq2[:, 0:256], ("onesb", "sq2"), True, False)
        mm(PSA[0:1, 0:256], (bA(0),), ONESB, sq2[:, 256:512], ("onesb", "sq2"), False, True)
        bld.op("act", lambda e: e.activation(r_k, PSA[0:1, 0:256], AF.Ln,
                                             bias=CT[0:1, 1:2], scale=1.0 / KVR),
               reads=(bA(0),), writes=("rk",))
        bld.op("act", lambda e: e.activation(r_k, r_k, AF.Exp, scale=-0.5),
               reads=("rk",), writes=("rk",))
        mm(PSA[:, 256:512], (bA(0),), ONESF, r_k, ("onesf", "rk"), True, True)
        for l in range(2):
            src = PSA[:, 1024:1280] if l == 0 else PSA[:, 1536:1792]
            btag = bA(2) if l == 0 else bA(3)
            bld.op("act", lambda e, src=src: e.copy(ckf[:, 0:256], src),
                   reads=(btag,), writes=("ckf",))
            bld.op("dve", lambda e: e.tensor_tensor(ckf[:, 256:512], ckf[:, 0:256],
                                                    PSA[:, 256:512], ALU.mult),
                   reads=("ckf", bA(0)), writes=("ckf2",))
            bld.op("dve", lambda e, l=l, c=c: e.tensor_scalar(
                CKV[:, l * 2048 + c * 256:l * 2048 + (c + 1) * 256],
                ckf[:, 256:512], GKV[:, l:l + 1], None, ALU.mult),
                reads=("ckf2", "gkv"), writes=("ckv",))
        kp = PSA[0:64, 512:768]
        bld.op("dve", lambda e, cb=cb: e.tensor_tensor(rt1, kp, ckt[cb][0], ALU.mult),
               reads=(bA(1), f"ckt{cb}"), writes=("rt1",))
        bld.op("dve", lambda e, cb=cb: e.tensor_tensor(rt2[0:32, :], kp[32:64, :],
                                                       ckt[cb][1][0:32, :], ALU.mult),
               reads=(bA(1), f"ckt{cb}"), writes=("rt2",))
        bld.op("dve", lambda e, cb=cb: e.tensor_tensor(rt2[32:64, :], kp[0:32, :],
                                                       ckt[cb][1][32:64, :], ALU.mult),
               reads=(bA(1), f"ckt{cb}"), writes=("rt2",))
        bld.op("dve", lambda e, c=c: e.tensor_add(KPE[:, c * 256:(c + 1) * 256], rt1, rt2),
               reads=("rt1", "rt2"), writes=("kpe",))

    _p1_dma(0)
    _ckt_dma(0)
    _p1_s(0, first=True)
    for c in range(8):
        if c + 1 < 8:
            _p1_s(c + 1)
        _p1_x(c)

    SQKP = carve(0, 128, 2048, BF16)
    SQCK = carve(4 * 1024, 128, 4096, BF16)
    bld.op("pool", lambda e: e.tensor_tensor(SQKP[0:64, :], KPE, KPE, ALU.mult),
           reads=("kpe",), writes=("sqkp", "xs0"))
    bld.op("pool", lambda e: e.tensor_tensor(SQCK, CKV, CKV, ALU.mult),
           reads=("ckv",), writes=("sqck", "xs0"))
    for n in range(4):
        mm(PSB[0:1, n * 512:(n + 1) * 512], (bB(n),), ONESB[0:64, :],
           SQKP[0:64, n * 512:(n + 1) * 512], ("onesb", "sqkp"), True, True)
        for l in range(2):
            mm(PSA[0:1, n * 512:(n + 1) * 512], (bA(n),), ONESB,
               SQCK[:, l * 2048 + n * 512:l * 2048 + (n + 1) * 512],
               ("onesb", "sqck"), l == 0, l == 1)
    bld.op("dve", lambda e: e.reduce_max(SROW[:, 0:1], PSB[0:1, 0:2048], AXX),
           reads=(bB(0), bB(1), bB(2), bB(3)), writes=("brow",))
    bld.op("dve", lambda e: e.reduce_max(SROW[:, 1:2], PSA[0:1, 0:2048], AXX),
           reads=(bA(0), bA(1), bA(2), bA(3)), writes=("brow",))
    mm(PSB[:, 0:2], (bB(0),), ONESF, SROW[:, 0:2], ("onesf", "brow"), True, True)
    bld.op("act", lambda e: e.copy(BND, PSB[:, 0:2]), reads=(bB(0),), writes=("bnd",))

    if debug:
        bld.op("act", lambda e: e.dma_start(out=dbg["xn"], in_=XN),
               reads=("xn",), dma=True)
        bld.op("act", lambda e: e.dma_start(out=dbg["ckv"], in_=CKV),
               reads=("ckv",), dma=True)
        bld.op("act", lambda e: e.dma_start(out=dbg["kpe"], in_=KPE),
               reads=("kpe",), dma=True)
    bld.barrier()

    if phases >= 2:
        _p2(nc, bld, locals())
    if phases >= 3:
        _p3(nc, bld, locals())
    if phases >= 4:
        _p4(nc, bld, locals())
    if phases >= 5:
        _p5(nc, bld, locals())

    from contextlib import ExitStack
    es = ExitStack()
    sems = {}
    for e in ENGS:
        sems[e] = es.enter_context(nc.semaphore(f"m_{e}"))
    dsems = [es.enter_context(nc.semaphore(f"d_{i}")) for i in range(NDMASEM)]
    with nc.Block() as blk:
        emit = bld.materialize(sems, dsems)
        blk.sync(lambda eng: emit(eng, bld.eng_ops["sp"]))
        blk.tensor(lambda eng: emit(eng, bld.eng_ops["pe"]))
        blk.scalar(lambda eng: emit(eng, bld.eng_ops["act"]))
        blk.vector(lambda eng: emit(eng, bld.eng_ops["dve"]))
        blk.gpsimd(lambda eng: emit(eng, bld.eng_ops["pool"]))
    fc = bld.final_counts
    with nc.Block() as blk2:
        def fin(eng):
            for key, v in fc.items():
                if key[0] == "D":
                    eng.wait_ge(dsems[key[1]], v)
                else:
                    eng.wait_ge(sems[key[1]], v)
        blk2.sync(fin)
    es.close()
    _CACHE[ck_] = nc
    return nc


def _p2(nc, bld, env):
    carve, mm, PSA, PSB, bA, bB = (env["carve"], env["mm"], env["PSA"], env["PSB"],
                                   env["bA"], env["bB"])
    wqaP, XN, CQN, GQ, ONESB, ONESF, SROW = (env["wqaP"], env["XN"], env["CQN"],
                                             env["GQ"], env["ONESB"], env["ONESF"],
                                             env["SROW"])
    CT = env["CT"]
    debug, dbg = env["debug"], env["dbg"]
    wqa_st = carve(0, 128, 4096, F32)
    wqa_bf = [carve(16 * 1024, 128, 4096, BF16), carve(24 * 1024, 128, 4096, BF16)]
    cqf = carve(32 * 1024, 128, 8192, F32)
    sqc = carve(64 * 1024, 128, 1024, BF16)

    def _wqa_fetch(oc):
        ob = oc % 2
        bld.op("sp", lambda e, oc=oc: e.dma_start(out=wqa_st, in_=wqaP[:, oc, :]),
               reads=("wqast",), writes=("wqast",), dma=True)
        bld.op("act", lambda e, ob=ob: e.copy(wqa_bf[ob], wqa_st),
               reads=("wqast",), writes=(f"wqab{ob}",))
    _wqa_fetch(0)
    for oc in range(8):
        ob = oc % 2
        if oc + 1 < 8:
            _wqa_fetch(oc + 1)
        ps = PSA[:, 0:1024] if ob == 0 else PSA[:, 1024:2048]
        pbt = (bA(0), bA(1)) if ob == 0 else (bA(2), bA(3))
        for dc in range(32):
            for hf in range(2):
                mm(ps[:, hf * 512:(hf + 1) * 512], (pbt[hf],),
                   wqa_bf[ob][:, dc * 128:(dc + 1) * 128],
                   XN[:, dc * 1024 + hf * 512:dc * 1024 + (hf + 1) * 512],
                   (f"wqab{ob}", "xn"), dc == 0, dc == 31)
        bld.op("act", lambda e, ps=ps: e.activation(sqc, ps, AF.Square),
               reads=pbt, writes=("sqc",))
        for hf in range(2):
            mm(PSB[0:1, hf * 512:(hf + 1) * 512], (bB(hf),),
               ONESB, sqc[:, hf * 512:(hf + 1) * 512],
               ("onesb", "sqc"), oc == 0, oc == 7)
        bld.op("act", lambda e, ps=ps, oc=oc: e.copy(cqf[:, oc * 1024:(oc + 1) * 1024], ps),
               reads=pbt, writes=("cqf",))
    for hf in range(2):
        bld.op("act", lambda e, hf=hf: e.activation(
            SROW[:, hf * 512:(hf + 1) * 512], PSB[0:1, hf * 512:(hf + 1) * 512],
            AF.Ln, bias=CT[0:1, 1:2], scale=1.0 / 1024.0),
            reads=(bB(hf),), writes=("qrow",))
        bld.op("act", lambda e, hf=hf: e.activation(
            SROW[:, hf * 512:(hf + 1) * 512], SROW[:, hf * 512:(hf + 1) * 512],
            AF.Exp, scale=-0.5), reads=("qrow",), writes=("qrow",))
        mm(PSB[:, 1024 + hf * 512:1024 + (hf + 1) * 512], (bB(2 + hf),),
           ONESF, SROW[:, hf * 512:(hf + 1) * 512], ("onesf", "qrow"), True, True)
    rqb = carve(66 * 1024, 128, 1024, F32)
    bld.op("act", lambda e: e.copy(rqb, PSB[:, 1024:2048]),
           reads=(bB(2), bB(3)), writes=("rqb",))
    for oc in range(8):
        bld.op("pool", lambda e, oc=oc: e.tensor_tensor(
            cqf[:, oc * 1024:(oc + 1) * 1024], cqf[:, oc * 1024:(oc + 1) * 1024],
            rqb, ALU.mult),
            reads=("cqf", "rqb"), writes=(f"cqfn{oc}",))
        bld.op("dve", lambda e, oc=oc: e.tensor_scalar(
            CQN[:, oc * 1024:(oc + 1) * 1024], cqf[:, oc * 1024:(oc + 1) * 1024],
            GQ[:, oc:oc + 1], None, ALU.mult),
            reads=(f"cqfn{oc}", "gq"), writes=("cqn",))
    if debug:
        bld.op("act", lambda e: e.dma_start(out=dbg["cqn"], in_=CQN),
               reads=("cqn",), dma=True)


def _p3(nc, bld, env):
    carve, mm, PSA, PSB, bA, bB = (env["carve"], env["mm"], env["PSA"], env["PSB"],
                                   env["bA"], env["bB"])
    (masks, cosq, sinq, wqbP, wkvbkP, wkvbvP, attn_d) = (
        env["masks"], env["cosq"], env["sinq"], env["wqbP"],
        env["wkvbkP"], env["wkvbvP"], env["attn_d"])
    (XN, CQN, CKV, KPE, IDENT, ONESB, ONES128, BIAS8, T8, LROW, LINV) = (
        env["XN"], env["CQN"], env["CKV"], env["KPE"], env["IDENT"], env["ONESB"],
        env["ONES128"], env["BIAS8"], env["T8"], env["LROW"], env["LINV"])
    BND, F2G, KB2, CT = env["BND"], env["F2G"], env["KB2"], env["CT"]
    debug, dbg = env["debug"], env["dbg"]

    MASKS = carve(0, 128, 256, F32)
    COSQ = carve(2 * 1024, 64, 1024, F32)
    SINQ = carve(6 * 1024, 64, 1024, F32)
    QH = [carve(10 * 1024, 128, 1024, BF16), carve(12 * 1024, 128, 1024, BF16)]
    KF = [carve(14 * 1024, 128, 2048, BF16), carve(18 * 1024, 128, 2048, BF16)]
    V4 = carve(22 * 1024, 128, 8192, BF16)
    V43 = V4.rearrange("p (k v) -> p k v", v=512)
    PBUF = carve(38 * 1024, 128, PTOT, BF16)
    PT = carve(56 * 1024, 128, 8192, BF16)
    PT4 = PT.rearrange("p (k j c) -> p k j c", j=8, c=128)
    RTA = carve(72 * 1024, 64, 1024, F32)
    RTB = carve(76 * 1024, 64, 1024, F32)
    SQKV = carve(76 * 1024, 128, 512, BF16)
    Q2K = carve(72 * 1024, 128, 2048, BF16)
    WQB_ST = carve(80 * 1024, 128, 1024, F32)
    WQB_BF = [carve(84 * 1024, 128, 1024, BF16), carve(86 * 1024, 128, 1024, BF16)]
    KVK_ST = carve(88 * 1024, 128, 512, F32)
    KVV_ST = carve(90 * 1024, 128, 1024, F32)
    KVK_BF = [carve(94 * 1024, 128, 512, BF16), carve(95 * 1024, 128, 512, BF16)]
    KVV_BF = [carve(96 * 1024, 128, 1024, BF16), carve(98 * 1024, 128, 1024, BF16)]
    ATTS = carve(100 * 1024, 128, 1024, BF16)
    QSTG = carve(102 * 1024, 128, 1024, F32)

    units = []
    for j in range(NSLOT):
        n = LJ[j] * 128
        for k0 in range(0, n, 512):
            units.append((j, k0, min(k0 + 512, n)))
    ufirst = {}
    for ui, (j, _, _k) in enumerate(units):
        ufirst.setdefault(j, ui)

    def _wqb_fetch(h):
        qb = h % 2
        bld.op("sp", lambda e, h=h: e.dma_start(out=WQB_ST, in_=wqbP[:, h, :]),
               reads=("wqbst",), writes=("wqbst",), dma=True)
        bld.op("act", lambda e, qb=qb: e.copy(WQB_BF[qb], WQB_ST),
               reads=("wqbst",), writes=(f"wqbb{qb}",))

    def _kv_fetch(g):
        gb = g % 2
        bld.op("sp", lambda e, g=g: e.dma_start(out=KVK_ST, in_=wkvbkP[:, 4 * g:4 * g + 4, :]),
               reads=("kvkst",), writes=("kvkst",), dma=True)
        bld.op("sp", lambda e, g=g: e.dma_start(out=KVV_ST, in_=wkvbvP[:, g, :]),
               reads=("kvvst",), writes=("kvvst",), dma=True)
        bld.op("pool", lambda e, gb=gb: e.tensor_copy(KVK_BF[gb], KVK_ST),
               reads=("kvkst",), writes=(f"kvkb{gb}",))
        bld.op("pool", lambda e, gb=gb: e.tensor_copy(KVV_BF[gb], KVV_ST),
               reads=("kvvst",), writes=(f"kvvb{gb}",))

    def _vgen(g):
        gb = g % 2
        if g + 1 < 8:
            _kv_fetch(g + 1)
        bld.op("pool", lambda e, gb=gb: e.tensor_tensor(SQKV, KVK_BF[gb], KVK_BF[gb],
                                                        ALU.mult),
               reads=(f"kvkb{gb}",), writes=("rtb",))
        mm(PSB[:, 1024:1536], (bB(2),), ONES128, SQKV, ("ones128", "rtb"), True, True)
        bld.op("dve", lambda e, gb=gb: e.reduce_sum(
            F2G[:, 4 * gb:4 * gb + 4],
            PSB[:, 1024:1536].rearrange("p (h c) -> p h c", c=128), AXX),
            reads=(bB(2),), writes=(f"f2g{gb}",))
        for kt in range(16):
            vb = kt % 2
            for l in range(2):
                mm(PSB[:, vb * 512:(vb + 1) * 512], (bB(vb),),
                   CKV[:, l * 2048 + kt * 128:l * 2048 + (kt + 1) * 128],
                   KVV_BF[gb][:, l * 512:(l + 1) * 512],
                   ("ckv", f"kvvb{gb}"), l == 0, l == 1)
            bld.op("act", lambda e, kt=kt, vb=vb: e.copy(
                V43[:, kt, :], PSB[:, vb * 512:(vb + 1) * 512]),
                reads=(bB(vb),), writes=("v4",))
        if debug and g == 0:
            bld.op("act", lambda e: e.dma_start(out=dbg["v4"], in_=V4),
                   reads=("v4",), dma=True)

    def _gen_a(h):
        g, hh = h // 4, h % 4
        gb = g % 2
        qb = kb = h % 2
        if h + 1 < 32:
            _wqb_fetch(h + 1)
        for dc in range(8):
            for hf in range(2):
                mm(PSA[:, hf * 512:(hf + 1) * 512], (bA(hf),),
                   WQB_BF[qb][:, dc * 128:(dc + 1) * 128],
                   CQN[:, dc * 1024 + hf * 512:dc * 1024 + (hf + 1) * 512],
                   (f"wqbb{qb}", "cqn"), dc == 0, dc == 7)
        bld.op("act", lambda e: e.copy(QSTG, PSA[:, 0:1024]),
               reads=(bA(0), bA(1)), writes=("qstg",))
        for n in range(4):
            pb_ = 2 + (n % 2)
            for l in range(2):
                mm(PSB[0:64, pb_ * 512:(pb_ + 1) * 512], (bB(pb_),),
                   KVK_BF[gb][:, hh * 128 + l * 64:hh * 128 + l * 64 + 64],
                   CKV[:, l * 2048 + n * 512:l * 2048 + (n + 1) * 512],
                   (f"kvkb{gb}", "ckv"), l == 0, l == 1)
            bld.op("dve", lambda e, n=n, kb=kb, pb_=pb_: e.tensor_copy(
                KF[kb][64:128, n * 512:(n + 1) * 512],
                PSB[0:64, pb_ * 512:(pb_ + 1) * 512]),
                reads=(bB(pb_),), writes=(f"kf{kb}",))
    def _gen_b1(h):
        g, hh = h // 4, h % 4
        gb = g % 2
        qb = hb = h % 2
        bld.op("act", lambda e: e.copy(RTA[0:32, :], QSTG[32:64, :]),
               reads=("qstg",), writes=("rta", "q2k"))
        bld.op("act", lambda e: e.copy(RTA[32:64, :], QSTG[0:32, :]),
               reads=("qstg",), writes=("rta", "q2k"))
        bld.op("dve", lambda e: e.tensor_tensor(RTB, RTA, SINQ, ALU.mult),
               reads=("rta", "sinq"), writes=("rtb",))
        bld.op("dve", lambda e: e.tensor_tensor(RTA, QSTG[0:64, :], COSQ, ALU.mult),
               reads=("qstg", "cosq", "rtb"), writes=("rta", "q2k"))
        bld.op("dve", lambda e, qb=qb: e.tensor_add(QH[qb][0:64, :], RTA, RTB),
               reads=("rta", "rtb"), writes=(f"qh{qb}",))
        bld.op("act", lambda e, qb=qb: e.copy(QH[qb][64:128, :], QSTG[64:128, :]),
               reads=("qstg",), writes=(f"qh{qb}",))
        bld.op("act", lambda e: e.activation(Q2K[:, 0:1024], QSTG, AF.Square),
               reads=("qstg",), writes=("q2k", "rta"))

    def _gen_b2(h):
        g, hh = h // 4, h % 4
        gb = g % 2
        qb = hb = h % 2
        for j in range(8):
            mm(PSB[:, 1536 + 8 * hb + j:1537 + 8 * hb + j], (bB(3),),
               Q2K[:, j * 128:(j + 1) * 128], ONESB,
               ("q2k", "onesb"), True, True)
        bld.op("dve", lambda e, gb=gb, hh=hh, hb=hb: e.tensor_tensor(
            KB2[:, hb:hb + 1], F2G[:, 4 * gb + hh:4 * gb + hh + 1], BND[:, 1:2],
            ALU.mult),
            reads=(f"f2g{gb}", "bnd"), writes=(f"kb2{hb}",))
        bld.op("dve", lambda e, hb=hb: e.tensor_add(
            KB2[:, hb:hb + 1], KB2[:, hb:hb + 1], BND[:, 0:1]),
            reads=(f"kb2{hb}", "bnd"), writes=(f"kb2{hb}",))
        t8v = T8[:, 8 * hb:8 * hb + 8]
        bld.op("dve", lambda e, hb=hb, t8v=t8v: e.tensor_scalar(
            t8v, PSB[:, 1536 + 8 * hb:1544 + 8 * hb], KB2[:, hb:hb + 1], None,
            ALU.mult),
            reads=(bB(3), f"kb2{hb}"), writes=(f"t8{hb}",))
        bld.op("act", lambda e, t8v=t8v: e.activation(t8v, t8v, AF.Ln, bias=CT[:, 2:3]),
               reads=(f"t8{hb}",), writes=(f"t8{hb}",))
        bld.op("act", lambda e, t8v=t8v: e.activation(t8v, t8v, AF.Exp, scale=0.5),
               reads=(f"t8{hb}",), writes=(f"t8{hb}",))
        bld.op("dve", lambda e, hb=hb, t8v=t8v: e.tensor_scalar(
            BIAS8[:, 8 * hb:8 * hb + 8], t8v, -SOFTMAX_SCALE, MARGIN,
            ALU.mult, ALU.add),
            reads=(f"t8{hb}",), writes=(f"bias8{hb}",))

    def _att_units(h):
        g, hh = h // 4, h % 4
        qb = kb = hb = h % 2
        pending = []
        for ui, (j, klo, khi) in enumerate(units):
            rg = ui % 4
            reg = PSA[:, rg * 512:(rg + 1) * 512]
            rtags = (bA(rg),)
            n = khi - klo
            mm(reg[:, 0:n], rtags,
               QH[qb][:, j * 128:(j + 1) * 128],
               KF[kb][:, klo:khi],
               (f"qh{qb}", f"kf{kb}"), True, True)
            while len(pending) > 2:
                pending.pop(0)()
            last = (khi == LJ[j] * 128)
            if last:
                moff = n - 256
                bld.op("dve", lambda e, reg=reg, moff=moff: e.tensor_tensor(
                    reg[:, moff:moff + 256], reg[:, moff:moff + 256],
                    MASKS, ALU.add),
                    reads=rtags + ("masks",), writes=rtags)
            uoff = OFFJ[j] + klo
            uix = 24 * hb + ui
            bld.op("act", lambda e, reg=reg, n=n, uoff=uoff, j=j, uix=uix, hb=hb:
                   e.activation(PBUF[:, uoff:uoff + n], reg[:, 0:n], AF.Exp,
                                bias=BIAS8[:, 8 * hb + j:8 * hb + j + 1],
                                scale=SOFTMAX_SCALE,
                                accum_out=LROW[:, uix:uix + 1]),
                   reads=rtags + (f"bias8{hb}",), writes=(f"pb{j}", f"lrow{uix}"))
            if last:
                u0 = 24 * hb + ufirst[j]
                for ux in range(u0, uix):
                    bld.op("dve", lambda e, uix=uix, ux=ux: e.tensor_add(
                        LROW[:, uix:uix + 1], LROW[:, uix:uix + 1],
                        LROW[:, ux:ux + 1]),
                        reads=(f"lrow{uix}", f"lrow{ux}"), writes=(f"lrow{uix}",))
                jx = 8 * hb + j
                bld.op("dve", lambda e, jx=jx, uix=uix: e.reciprocal(
                    LINV[:, jx:jx + 1], LROW[:, uix:uix + 1]),
                    reads=(f"lrow{uix}",), writes=(f"linv{jx}",))
                bld.op("dve", lambda e, j=j, jx=jx: e.tensor_single_scalar(
                    PBUF[:, OFFJ[j]:OFFJ[j] + LJ[j] * 128],
                    PBUF[:, OFFJ[j]:OFFJ[j] + LJ[j] * 128],
                    LINV[:, jx:jx + 1], ALU.mult),
                    reads=(f"pb{j}", f"linv{jx}"), writes=(f"pb{j}",))

                def _mk_tr(j=j):
                    def go():
                        c0 = min(LJ[j], 8)
                        ptv = PSB[:, 0:512].bitcast(BF16)
                        for kt in range(c0):
                            bld.op("pe", lambda e, j=j, kt=kt, ptv=ptv: e.transpose(
                                ptv[:, kt * 128:(kt + 1) * 128],
                                PBUF[:, OFFJ[j] + kt * 128:OFFJ[j] + (kt + 1) * 128],
                                IDENT),
                                reads=(f"pb{j}", "ident"), writes=(bB(0),))
                        ev_eng = "dve" if (j % 2 == 0) else "act"

                        def _ev0(e, j=j, c0=c0, ptv=ptv):
                            s_ = ptv.rearrange("p (k c) -> p k c", c=128)[:, 0:c0, :]
                            d_ = PT4[:, 0:c0, j, :]
                            return (e.tensor_copy(d_, s_) if hasattr(e, "tensor_copy")
                                    else e.copy(d_, s_))
                        bld.op(ev_eng, _ev0, reads=(bB(0),), writes=(f"ptg{j}",))
                    return go
                pending.append(_mk_tr())
        for fn_ in pending:
            fn_()

    def _att_av(h):
        g, hh = h // 4, h % 4
        qb = kb = hb = h % 2
        for kt in range(8):
            j0 = kt // 2
            rd = tuple(f"ptg{j}" for j in range(j0, 8)) + ("v4",)
            lhs = V43[:, kt, hh * 128:(hh + 1) * 128]
            base = kt * 1024
            if j0 * 128 < 512:
                mm(PSB[:, 1024 + j0 * 128:1536], (bB(2),),
                   lhs, PT[:, base + j0 * 128:base + 512], rd, kt == 0, kt == 7)
            mm(PSB[:, 1536:2048], (bB(3),),
               lhs, PT[:, base + 512:base + 1024], rd, kt == 0, False)
        for j in range(4, 8):
            c1 = LJ[j] - 8
            ptv = PSB[:, 0:512].bitcast(BF16)
            for kt in range(c1):
                bld.op("pe", lambda e, j=j, kt=kt, ptv=ptv: e.transpose(
                    ptv[:, kt * 128:(kt + 1) * 128],
                    PBUF[:, OFFJ[j] + (8 + kt) * 128:OFFJ[j] + (9 + kt) * 128],
                    IDENT),
                    reads=(f"pb{j}", "ident"), writes=(bB(0),))
            ev_eng = "dve" if (j % 2 == 0) else "act"

            def _ev1(e, j=j, c1=c1, ptv=ptv):
                s_ = ptv.rearrange("p (k c) -> p k c", c=128)[:, 0:c1, :]
                d_ = PT4[:, 0:c1, j, :]
                return (e.tensor_copy(d_, s_) if hasattr(e, "tensor_copy")
                        else e.copy(d_, s_))
            bld.op(ev_eng, _ev1, reads=(bB(0),), writes=(f"ptg{j}",))
        for kt in range(8, 16):
            j0 = kt // 2
            rd = tuple(f"ptg{j}" for j in range(j0, 8)) + ("v4",)
            lhs = V43[:, kt, hh * 128:(hh + 1) * 128]
            base = (kt - 8) * 1024
            mm(PSB[:, 1024 + j0 * 128:2048], (bB(3),),
               lhs, PT[:, base + j0 * 128:base + 1024], rd, False, kt == 15)
        bld.op("dve", lambda e: e.tensor_copy(ATTS, PSB[:, 1024:2048]),
               reads=(bB(2), bB(3)), writes=("atts",))
        bld.op("act", lambda e, h=h: e.dma_start(out=attn_d[:, h, :], in_=ATTS),
               reads=("atts",), writes=(f"attnd{h}",), dma=True)
        if debug and h == 0:
            bld.op("act", lambda e: e.dma_start(out=dbg["qh"], in_=QH[0]),
                   reads=("qh0",), dma=True)
            bld.op("act", lambda e: e.dma_start(out=dbg["kf"], in_=KF[0]),
                   reads=("kf0",), dma=True)
            bld.op("act", lambda e: e.dma_start(out=dbg["pb"], in_=PBUF),
                   reads=tuple(f"pb{j}" for j in range(8)), dma=True)

    _kv_fetch(0)
    _wqb_fetch(0)
    bld.barrier()
    bld.op("sp", lambda e: e.dma_start(out=MASKS, in_=masks), writes=("masks",), dma=True)
    bld.op("sp", lambda e: e.dma_start(out=COSQ, in_=cosq), writes=("cosq",), dma=True)
    bld.op("sp", lambda e: e.dma_start(out=SINQ, in_=sinq), writes=("sinq",), dma=True)
    for kb in range(2):
        bld.op("pool", lambda e, kb=kb: e.tensor_copy(KF[kb][0:64, :], KPE),
               reads=("kpe",), writes=(f"kf{kb}",))
    _vgen(0)
    _gen_a(0)
    _gen_b1(0)
    _gen_b2(0)
    for h in range(32):
        nh = h + 1
        boundary = (nh < 32 and nh % 4 == 0)
        if nh < 32 and not boundary:
            _gen_a(nh)
        _att_units(h)
        if nh < 32 and not boundary:
            _gen_b1(nh)
        _att_av(h)
        if nh < 32:
            if boundary:
                _vgen(nh // 4)
                _gen_a(nh)
                _gen_b1(nh)
            _gen_b2(nh)
    if debug:
        bld.op("sp", lambda e: e.dma_start(out=dbg["at"], in_=attn_d),
               reads=tuple(f"attnd{h}" for h in range(32)), dma=True)
    bld.barrier()


def _p4(nc, bld, env):
    carve, mm, PSA, PSB, bA, bB = (env["carve"], env["mm"], env["PSA"], env["PSB"],
                                   env["bA"], env["bB"])
    woP, attn_d, o_d = env["woP"], env["attn_d"], env["o_d"]
    debug, dbg = env["debug"], env["dbg"]
    ATTN = carve(0, 128, 32 * 1024, BF16)
    at3 = ATTN.rearrange("p (h t) -> p h t", t=1024)
    WO_ST = [carve(64 * 1024, 128, 2048, F32), carve(72 * 1024, 128, 2048, F32)]
    WO_BF = [carve(80 * 1024, 128, 4096, BF16), carve(88 * 1024, 128, 4096, BF16)]
    OSTG = carve(96 * 1024, 128, 1024, F32)

    for q in range(4):
        bld.op("sp", lambda e, q=q: e.dma_start(
            out=at3[:, q * 8:(q + 1) * 8, :], in_=attn_d[:, q * 8:(q + 1) * 8, :]),
            reads=tuple(f"attnd{h2}" for h2 in range(q * 8, q * 8 + 8)),
            writes=("attn",), dma=True)
    def _wo_fetch(oc):
        ob = oc % 2
        for hf in range(2):
            dq = "sp" if hf == 0 else "act"
            bld.op(dq, lambda e, oc=oc, hf=hf: e.dma_start(
                out=WO_ST[hf], in_=woP[:, oc, hf * 2048:(hf + 1) * 2048]),
                reads=(f"wost{hf}",), writes=(f"wost{hf}",), dma=True)
            eng = "act" if hf == 0 else "dve"
            def _woc(e, hf=hf, ob=ob):
                d_ = WO_BF[ob][:, hf * 2048:(hf + 1) * 2048]
                return e.copy(d_, WO_ST[hf]) if hf == 0 else e.tensor_copy(d_, WO_ST[hf])
            bld.op(eng, _woc, reads=(f"wost{hf}",), writes=(f"wob{ob}",))
    _wo_fetch(0)
    for oc in range(32):
        ob = oc % 2
        if oc + 1 < 32:
            _wo_fetch(oc + 1)
        ps = PSA[:, 0:1024] if ob == 0 else PSA[:, 1024:2048]
        pbt = (bA(0), bA(1)) if ob == 0 else (bA(2), bA(3))
        for hv in range(32):
            for hf in range(2):
                mm(ps[:, hf * 512:(hf + 1) * 512], (pbt[hf],),
                   WO_BF[ob][:, hv * 128:(hv + 1) * 128],
                   at3[:, hv, hf * 512:(hf + 1) * 512],
                   (f"wob{ob}", "attn"), hv == 0, hv == 31)
        bld.op("act", lambda e, ps=ps: e.copy(OSTG, ps), reads=pbt, writes=("ostg",))
        bld.op("act", lambda e, oc=oc: e.dma_start(out=o_d[:, oc, :], in_=OSTG),
               reads=("ostg",), writes=(f"od{oc}",), dma=True)
    if debug:
        bld.op("sp", lambda e: e.dma_start(out=dbg["od"], in_=o_d),
               reads=tuple(f"od{oc}" for oc in range(32)), dma=True)
    bld.barrier()


def _p5(nc, bld, env):
    carve, mm, PSA, PSB, bA, bB = (env["carve"], env["mm"], env["PSA"], env["PSB"],
                                   env["bA"], env["bB"])
    (xP, gateP, upP, downP, o_d, outP, XN, SCAL, DSEFF) = (
        env["xP"], env["gateP"], env["upP"], env["downP"], env["o_d"], env["outP"],
        env["XN"], env["SCAL"], env["DSEFF"])
    ICMAX = max(ICB_SPLIT)
    INTERB = carve(0, 128, ICMAX * 2 * 512, BF16)
    G_ST = carve(36 * 1024, 128, 4096, F32)
    U_ST = carve(52 * 1024, 128, 4096, F32)
    G_BF = [carve(68 * 1024, 128, 4096, BF16), carve(76 * 1024, 128, 4096, BF16)]
    U_BF = [carve(84 * 1024, 128, 4096, BF16), carve(92 * 1024, 128, 4096, BF16)]
    SILU = carve(100 * 1024, 128, 512, F32)
    D_ST = [carve(36 * 1024, 128, ICMAX * 128, F32),
            carve(45 * 1024, 128, ICMAX * 128, F32)]
    D_BF = [carve(54 * 1024, 128, ICMAX * 128, BF16),
            carve(59 * 1024, 128, ICMAX * 128, BF16)]
    XOC = [carve(64 * 1024, 128, 1024, F32), carve(68 * 1024, 128, 1024, F32)]
    OOC = [carve(72 * 1024, 128, 1024, F32), carve(80 * 1024, 128, 1024, F32)]
    OUTS = [carve(76 * 1024, 128, 512, F32), carve(78 * 1024, 128, 512, F32)]

    def _gu_fetch(ic, ib, after_dma=(), after_cast=()):
        o1 = bld.op("sp", lambda e, ic=ic: e.dma_start(out=G_ST, in_=gateP[:, ic, :]),
                    reads=("gst",), writes=("gst",), dma=True, after=after_dma)
        o2 = bld.op("act", lambda e, ib=ib: e.copy(G_BF[ib], G_ST),
                    reads=("gst",), writes=(f"gbf{ib}",), after=after_cast)
        o3 = bld.op("act", lambda e, ic=ic: e.dma_start(out=U_ST, in_=upP[:, ic, :]),
                    reads=("ust",), writes=("ust",), dma=True, after=after_dma)
        o4 = bld.op("dve", lambda e, ib=ib: e.tensor_copy(U_BF[ib], U_ST),
                    reads=("ust",), writes=(f"ubf{ib}",), after=after_cast)
        return o2, o4

    trans = {"mm": None, "dacts": (), "stt": None, "silu": None}
    for icb in range(len(ICB_SPLIT)):
        ic0 = sum(ICB_SPLIT[:icb])
        nic = ICB_SPLIT[icb]
        ad = tuple(x for x in trans["dacts"] if x is not None)
        am = (trans["mm"],) if trans["mm"] is not None else ()
        lastg, lastu = _gu_fetch(ic0, 0, after_dma=ad + am, after_cast=am + ad)
        lastmm = None
        lastsilu = None
        laststt = None
        for i in range(nic):
            ic = ic0 + i
            ib = i % 2
            if i + 1 < nic:
                xa = (am + ad) if i == 0 else ()
                lastg, lastu = _gu_fetch(ic + 1, (i + 1) % 2,
                                         after_dma=xa, after_cast=xa)
            PS = PSA if ib == 0 else PSB
            bT = bA if ib == 0 else bB
            for th2 in range(2):
                pg = PS[:, (2 * th2) * 512:(2 * th2 + 1) * 512]
                pu = PS[:, (2 * th2 + 1) * 512:(2 * th2 + 2) * 512]
                tg, tu = bT(2 * th2), bT(2 * th2 + 1)
                for dc in range(32):
                    mm(pg, (tg,), G_BF[ib][:, dc * 128:(dc + 1) * 128],
                       XN[:, dc * 1024 + th2 * 512:dc * 1024 + (th2 + 1) * 512],
                       (f"gbf{ib}", "xn"), dc == 0, dc == 31)
                    lastmm = mm(pu, (tu,), U_BF[ib][:, dc * 128:(dc + 1) * 128],
                                XN[:, dc * 1024 + th2 * 512:dc * 1024 + (th2 + 1) * 512],
                                (f"ubf{ib}", "xn"), dc == 0, dc == 31)
            af = am if i == 0 else ()
            for th2 in range(2):
                pg = PS[:, (2 * th2) * 512:(2 * th2 + 1) * 512]
                pu = PS[:, (2 * th2 + 1) * 512:(2 * th2 + 2) * 512]
                tg, tu = bT(2 * th2), bT(2 * th2 + 1)
                lastsilu = bld.op("act", lambda e, pg=pg, tg=tg: e.activation(
                    SILU, pg, AF.Silu, scale=SCAL[:, 0:1]),
                    reads=(tg, "scal"), writes=("silu",))
                laststt = bld.op("dve", lambda e, pu=pu, i=i, th2=th2: e.scalar_tensor_tensor(
                    INTERB[:, (2 * i + th2) * 512:(2 * i + th2 + 1) * 512],
                    SILU, DSEFF, pu, ALU.mult, ALU.mult),
                    reads=("silu", "dseff", tu), writes=("inter",), after=af)
        def _dn_dma(oc, ic0=ic0, nic=nic, after=()):
            ob = oc % 2
            bld.op("sp", lambda e, oc=oc, ob=ob: e.dma_start(
                out=D_ST[ob][:, 0:nic * 128],
                in_=downP[:, oc, ic0 * 128:(ic0 + nic) * 128]),
                reads=(f"dst{ob}",), writes=(f"dst{ob}",), dma=True, after=after)

        dn_acts = []

        def _dn_fetch(oc, icb=icb, ic0=ic0, nic=nic, after=()):
            ob = oc % 2
            nh = ((nic * 128) // 2 + 127) // 128 * 128
            dn_acts.append(bld.op("act", lambda e, ob=ob, nh=nh: e.copy(
                D_BF[ob][:, 0:nh], D_ST[ob][:, 0:nh]),
                reads=(f"dst{ob}",), writes=(f"dbf{ob}",), after=after))
            dn_acts.append(bld.op("dve", lambda e, ob=ob, nh=nh, nic=nic: e.tensor_copy(
                D_BF[ob][:, nh:nic * 128], D_ST[ob][:, nh:nic * 128]),
                reads=(f"dst{ob}",), writes=(f"dbf{ob}",), after=after))
            if icb == 0:
                xoc3 = XOC[ob].rearrange("p (s t) -> p s t", t=128)
                bld.op("act", lambda e, oc=oc, xoc3=xoc3: e.dma_start(
                    out=xoc3,
                    in_=xP[:, oc, :].rearrange(
                        "p (s t) -> p s t", t=256)[:, :, 0:128]),
                    reads=(f"xoc{ob}",), writes=(f"xoc{ob}",), dma=True, after=after)
                bld.op("act", lambda e, oc=oc, ob=ob: e.dma_start(
                    out=OOC[ob], in_=o_d[:, oc, :]),
                    reads=(f"ooc{ob}", f"od{oc}"), writes=(f"ooc{ob}",), dma=True,
                    after=after)
            else:
                bld.op("act", lambda e, oc=oc, ob=ob: e.dma_start(
                    out=OOC[ob], in_=outP[:, oc, :]),
                    reads=(f"ooc{ob}", f"out_{oc}_0", f"out_{oc}_1"),
                    writes=(f"ooc{ob}",), dma=True, after=after)

        _dn_dma(0, after=(lastg, lastu))
        _dn_dma(1, after=(lastg, lastu))
        _dn_fetch(0, after=(lastmm, lastsilu, laststt))
        out_slots = {}
        for oc in range(32):
            ob = oc % 2
            if oc + 2 < 32:
                _dn_dma(oc + 2)
            if oc + 1 < 32:
                _dn_fetch(oc + 1, after=(lastmm,) if oc == 0 else ())
            for th2 in range(2):
                k = 2 * ob + th2
                ps = PSB[:, k * 512:(k + 1) * 512]
                pt = bB(k)
                for i in range(nic):
                    lastmm = mm(ps, (pt,), D_BF[ob][:, i * 128:(i + 1) * 128],
                                INTERB[:, (2 * i + th2) * 512:(2 * i + th2 + 1) * 512],
                                (f"dbf{ob}", "inter"), i == 0, i == nic - 1)
                if icb == 0:
                    bld.op("dve", lambda e, ps=ps, ob=ob, th2=th2: e.tensor_tensor(
                        OUTS[th2], ps, OOC[ob][:, th2 * 512:(th2 + 1) * 512], ALU.add),
                        reads=(pt, f"ooc{ob}", f"outs{th2}"), writes=(f"outs{th2}",))
                    bld.op("dve", lambda e, ob=ob, th2=th2: e.tensor_tensor(
                        OUTS[th2], OUTS[th2], XOC[ob][:, th2 * 512:(th2 + 1) * 512],
                        ALU.add),
                        reads=(f"outs{th2}", f"xoc{ob}"), writes=(f"outs{th2}",))
                else:
                    bld.op("dve", lambda e, ps=ps, ob=ob, th2=th2: e.tensor_tensor(
                        OUTS[th2], ps, OOC[ob][:, th2 * 512:(th2 + 1) * 512], ALU.add),
                        reads=(pt, f"ooc{ob}", f"outs{th2}"), writes=(f"outs{th2}",))
                oo = bld.op("act", lambda e, oc=oc, th2=th2: e.dma_start(
                    out=outP[:, oc, th2 * 512:(th2 + 1) * 512], in_=OUTS[th2]),
                    reads=(f"outs{th2}",), writes=(f"out_{oc}_{th2}",), dma=True)
                out_slots[oo.dsem] = oo
        coll = bld.op("sp", lambda e: e.nop(),
                      after=tuple(dn_acts[-4:]) + tuple(out_slots.values()))
        trans = {"mm": lastmm, "dacts": (coll,),
                 "stt": laststt, "silu": lastsilu}
        dn_acts.clear()


def kernel(**inputs):
    pk = _pack_shared(inputs)
    cos, sin = _yarn_cos_sin()
    nc = _build(DEBUG, PHASES)
    in_maps = []
    for c in range(8):
        b, p = c // 2, c % 2
        m = dict(pk)
        m.update(_core_pack(inputs, cos, sin, b, p))
        in_maps.append(m)
    res = run_bass_kernel_spmd(nc, in_maps, core_ids=list(range(8)))
    out = np.empty((B, S, D), np.float32)
    for c in range(8):
        b, p = c // 2, c % 2
        O = np.asarray(res.results[c]["outP"])
        t = O.reshape(128, 32, 8, 128).transpose(2, 3, 1, 0).reshape(8, 128, 4096)
        ov = out[b].reshape(16, 128, 4096)
        for j in range(8):
            ov[2 * j + p] = t[j]
    return out
